# version 08
# baseline (speedup 1.0000x reference)
import contextlib
import numpy as np
import ml_dtypes
import concourse.bass as bass
import concourse.mybir as mybir
from concourse.bass_utils import run_bass_kernel_spmd

F32 = mybir.dt.float32
BF16 = mybir.dt.bfloat16
AF = mybir.ActivationFunctionType
ALU = mybir.AluOpType
AX = mybir.AxisListType

ENGS = ("sync", "gpsimd", "scalar", "vector", "tensor")

D = 1024
TL = 2048
NSEG = 16
NA = 4
NE = 16
NE_RUN = 16
STOP_AFTER = None
A_RUN = 4
SEG_RUN = 16
SEG_OPS = 99
DBG_SEG0 = False
EI = 512
NEGC = -1.0e30


def _masks_np():
    out = np.zeros((4, 128, 512), np.float32)
    for beta in range(4):
        hl, ccg = beta // 2, beta % 2
        for ccl in range(4):
            for j in range(32):
                r = ccl * 32 + j
                m = 16 * j + 2 * (4 * ccg + ccl) + hl
                for hlp in range(2):
                    for jp in range(32):
                        for ccp in range(8):
                            c = hlp * 256 + (jp // 16) * 128 + (jp % 16) * 8 + ccp
                            mp = 16 * jp + 2 * ccp + hlp
                            if mp > m:
                                out[beta, r, c] = NEGC
    return out.astype(ml_dtypes.bfloat16)


class Em:
    def __init__(self, sems):
        self.sems = sems
        self.cnt = {k: 0 for k in sems}
        self.ops = {e: [] for e in ENGS}
        self.scope = None

    def emit(self, eng, f, waits=None, inc=None, k=1):
        sems = self.sems
        wl = tuple((s, v) for (s, v) in (waits or ()) if v > 0)

        def g(engobj, wl=wl, inc=inc, k=k, f=f):
            for s, v in wl:
                engobj.wait_ge(sems[s], v)
            ins = f(engobj)
            if inc is not None:
                ins.then_inc(sems[inc], k)
        self.ops[eng].append((self.scope, g))
        if inc is not None:
            self.cnt[inc] += k
        return self.cnt[inc] if inc is not None else None

    def at(self, s):
        return self.cnt[s]


def build_program():
    nc = bass.Bass("TRN2", target_bir_lowering=False,
                   detect_race_conditions=False)

    xT = nc.dram_tensor("xT", [D, TL], F32, kind="ExternalInput").ap()
    wq_d = nc.dram_tensor("wq", [D, D], F32, kind="ExternalInput").ap()
    wk_d = nc.dram_tensor("wk", [D, D], F32, kind="ExternalInput").ap()
    wv_d = nc.dram_tensor("wv", [D, D], F32, kind="ExternalInput").ap()
    wo_d = nc.dram_tensor("wo", [D, D], F32, kind="ExternalInput").ap()
    gwT_d = nc.dram_tensor("gwT", [D, NE], F32, kind="ExternalInput").ap()
    wg_d = nc.dram_tensor("wg", [NE, D, EI], F32, kind="ExternalInput").ap()
    wu_d = nc.dram_tensor("wu", [NE, D, EI], F32, kind="ExternalInput").ap()
    wd_d = nc.dram_tensor("wd", [NE, EI, D], F32, kind="ExternalInput").ap()
    wsg_d = nc.dram_tensor("wsg", [D, D], F32, kind="ExternalInput").ap()
    wsu_d = nc.dram_tensor("wsu", [D, D], F32, kind="ExternalInput").ap()
    wsd_d = nc.dram_tensor("wsd", [D, D], F32, kind="ExternalInput").ap()
    betas_d = nc.dram_tensor("betas", [1, 256], F32, kind="ExternalInput").ap()
    maug_d = nc.dram_tensor("maug", [NA, 64, 65], F32, kind="ExternalInput").ap()
    outT = nc.dram_tensor("outT", [D, TL], F32, kind="ExternalOutput").ap()

    ident_d = nc.inline_tensor(np.eye(128, dtype=ml_dtypes.bfloat16), "c_ident").ap()
    identf_d = nc.inline_tensor(np.eye(16, dtype=np.float32), "c_identf").ap()
    ones_d = nc.inline_tensor(np.ones((128, 128), dtype=ml_dtypes.bfloat16), "c_ones").ap()
    masks_d = nc.inline_tensor(np.ascontiguousarray(_masks_np().reshape(4 * 128, 512)),
                               "c_masks").ap()
    sel_np = np.zeros((16, 16, 128), np.float32)
    for e_ in range(16):
        sel_np[e_, e_, :] = 1.0
    sel_d = nc.inline_tensor(np.ascontiguousarray(
        sel_np.transpose(1, 0, 2).reshape(16, 16 * 128)).astype(ml_dtypes.bfloat16),
        "c_sel").ap()

    ctx = contextlib.ExitStack()
    with ctx:
        def sbt(st, name, shape, dt):
            return st.enter_context(nc.sbuf_tensor(name, shape, dt))

        sems = {}
        for s_ in ("dS", "dG", "PE", "V", "SC", "GP", "dO"):
            sems[s_] = ctx.enter_context(nc.semaphore(s_))
        em = Em(sems)

        ident = sbt(ctx, "ident", [128, 128], BF16)
        identf = sbt(ctx, "identf", [16, 16], F32)
        ones_b = sbt(ctx, "ones_b", [128, 128], BF16)

        betas_sb = sbt(ctx, "betas_sb", [1, 256], F32)
        gw_b = sbt(ctx, "gw_b", [128, 8 * NE], BF16)
        xn = sbt(ctx, "xn", [128, 8 * TL], BF16)
        att_out = sbt(ctx, "att_out", [128, 8 * TL], BF16)
        gate_row = sbt(ctx, "gate_row", [1, 256], F32)
        onemg_row = sbt(ctx, "onemg_row", [1, 256], F32)
        onemg_b = sbt(ctx, "onemg_b", [1, 256], BF16)
        gam_row = sbt(ctx, "gam_row", [1, 256], F32)
        gam65_b = sbt(ctx, "gam65_b", [1, NA * 65], BF16)

        p2w = ctx.enter_context(contextlib.ExitStack())
        masks_b = [sbt(p2w, f"mask{i}", [128, 512], BF16) for i in range(4)]
        wq_b = sbt(p2w, "wq_b", [128, 8 * 512], BF16)
        wk_b = sbt(p2w, "wk_b", [128, 8 * 512], BF16)
        wvo_b = sbt(p2w, "wvo_b", [128, 8 * 512], BF16)

        def wload(e, buf, src_d, h):
            return e.dma_start(
                out=buf[:].rearrange("p (r c) -> p r c", r=8),
                in_=src_d.rearrange("(r p) c -> p r c", p=128)[
                    :, :, h * 512:(h + 1) * 512])

        em.scope = "p0"
        for dst, src in ((ident, ident_d), (identf, identf_d), (ones_b, ones_d),
                         (betas_sb, betas_d)):
            em.emit("sync", lambda e, dst=dst, src=src: e.dma_start(out=dst[:], in_=src),
                    inc="dS", k=16)
        for i in range(4):
            em.emit("sync", lambda e, i=i: e.dma_start(
                out=masks_b[i][:], in_=masks_d[i * 128:(i + 1) * 128, :]), inc="dS", k=16)
        d_consts = em.at("dS")

        em.emit("gpsimd", lambda e: e.dma_start(
            out=gw_b[:].rearrange("p (r c) -> p r c", r=8),
            in_=gwT_d.rearrange("(r p) c -> p r c", p=128)), inc="dG", k=16)
        em.emit("gpsimd", lambda e: wload(e, wq_b, wq_d, 0), inc="dG", k=16)
        d_wq_h = [em.at("dG"), 0]
        em.emit("gpsimd", lambda e: wload(e, wk_b, wk_d, 0), inc="dG", k=16)
        d_wk_h = [em.at("dG"), 0]

        em.emit("scalar", lambda e: nc.scalar.activation(gate_row[:], betas_sb[:], AF.Sigmoid),
                waits=[("dS", d_consts)], inc="SC")
        sc_gate = em.at("SC")
        em.emit("vector", lambda e: nc.vector.tensor_scalar(
            onemg_row[:], gate_row[:], 1.0, -1.0, ALU.subtract, ALU.mult),
            waits=[("SC", sc_gate)], inc="V")
        em.emit("vector", lambda e: nc.vector.reciprocal(gam_row[:], onemg_row[:]), inc="V")
        em.emit("vector", lambda e: nc.vector.tensor_mul(gam_row[:], gate_row[:], gam_row[:]), inc="V")
        em.emit("vector", lambda e: nc.vector.tensor_copy(onemg_b[:], onemg_row[:]), inc="V")
        for a in range(NA):
            em.emit("vector", lambda e, a=a: nc.vector.tensor_copy(
                gam65_b[:, a * 65:a * 65 + 64], gam_row[:, a * 64:(a + 1) * 64]), inc="V")
        em.emit("vector", lambda e: nc.vector.memset(
            gam65_b[:].rearrange("p (a r) -> p a r", a=NA)[:, :, 64:65], 1.0), inc="V")
        v_gate = em.at("V")

        em.scope = "ln"
        with contextlib.ExitStack() as ph:
            xf = [sbt(ph, f"xf{i}", [128, TL], F32) for i in range(2)]
            xbf = sbt(ph, "xbf", [128, 8 * TL], BF16)
            x2 = sbt(ph, "x2", [128, TL], BF16)
            msb = sbt(ph, "msb", [128, TL], F32)
            vsb = sbt(ph, "vsb", [128, TL], F32)
            rstd = sbt(ph, "rstd", [128, TL], F32)
            ps_sx = ph.enter_context(nc.psum_tensor("ps_sx", [128, TL], F32))
            ps_sx2 = ph.enter_context(nc.psum_tensor("ps_sx2", [128, TL], F32))

            v_free = [0, 0]
            pe_x2done = [0] * 8
            for kc in range(8):
                q, qs = (("sync", "dS") if kc % 2 == 0 else ("gpsimd", "dG"))
                em.emit(q, lambda e, kc=kc: e.dma_start(
                    out=xf[kc % 2][:], in_=xT[kc * 128:(kc + 1) * 128, :]),
                    waits=[("V", v_free[kc % 2])], inc=qs, k=16)
                dld = em.at(qs)
                em.emit("vector", lambda e, kc=kc: nc.vector.tensor_copy(
                    xbf[:, kc * TL:(kc + 1) * TL], xf[kc % 2][:]),
                    waits=[(qs, dld)], inc="V")
                v_cast = em.at("V")
                v_free[kc % 2] = v_cast
                em.emit("scalar", lambda e, kc=kc: nc.scalar.activation(
                    x2[:], xbf[:, kc * TL:(kc + 1) * TL], AF.Square),
                    waits=[("V", v_cast)] + ([("PE", pe_x2done[kc - 1])] if kc >= 1 else []),
                    inc="SC")
                sc_sq = em.at("SC")

                def lnmm(e, kc=kc):
                    for nt in range(4):
                        nc.tensor.matmul(
                            ps_sx[:, nt * 512:(nt + 1) * 512], ones_b[:],
                            xbf[:, kc * TL + nt * 512: kc * TL + (nt + 1) * 512],
                            start=(kc == 0), stop=(kc == 7), skip_group_check=True)
                    for nt in range(4):
                        ins = nc.tensor.matmul(
                            ps_sx2[:, nt * 512:(nt + 1) * 512], ones_b[:],
                            x2[:, nt * 512:(nt + 1) * 512],
                            start=(kc == 0), stop=(kc == 7), skip_group_check=True)
                    return ins
                em.emit("tensor", lnmm, waits=[("V", v_cast), ("SC", sc_sq)], inc="PE")
                pe_x2done[kc] = em.at("PE")
            pe_ln = em.at("PE")

            em.emit("vector", lambda e: nc.vector.tensor_scalar(
                msb[:], ps_sx[:], 1.0 / 1024.0, None, ALU.mult),
                waits=[("PE", pe_ln)], inc="V")
            em.emit("vector", lambda e: nc.vector.tensor_mul(vsb[:], msb[:], msb[:]), inc="V")
            em.emit("vector", lambda e: nc.vector.scalar_tensor_tensor(
                vsb[:], ps_sx2[:], 1.0 / 1024.0, vsb[:], ALU.mult, ALU.subtract), inc="V")
            em.emit("vector", lambda e: nc.vector.tensor_scalar(
                vsb[:], vsb[:], 1e-5, None, ALU.add), inc="V")
            em.emit("vector", lambda e: nc.vector.reciprocal(vsb[:], vsb[:]), inc="V")
            v_var = em.at("V")
            em.emit("scalar", lambda e: nc.scalar.activation(rstd[:], vsb[:], AF.Sqrt),
                    waits=[("V", v_var)], inc="SC")
            sc_rstd = em.at("SC")
            for kc in range(8):
                em.emit("vector", lambda e, kc=kc: nc.vector.tensor_sub(
                    vsb[:], xbf[:, kc * TL:(kc + 1) * TL], msb[:]),
                    waits=([("SC", sc_rstd)] if kc == 0 else None), inc="V")
                em.emit("vector", lambda e, kc=kc: nc.vector.tensor_mul(
                    xn[:, kc * TL:(kc + 1) * TL], vsb[:], rstd[:]), inc="V")
            v_xn = em.at("V")

        if STOP_AFTER == 'ln':
            _finale(nc, em, xn, outT, v_xn)
            _replay(nc, em)
            return nc

        em.scope = "attn_pre"
        with contextlib.ExitStack() as ph:
            Q2A = sbt(ph, "Q2A", [128, NSEG * 256], BF16)
            Q2B = sbt(ph, "Q2B", [128, NSEG * 256], BF16)
            K2 = sbt(ph, "K2", [128, NSEG * 256], BF16)
            K2X = sbt(ph, "K2X", [128, NSEG * 256], BF16)
            Vscr = sbt(ph, "Vscr", [128, NSEG * 256], BF16)
            VscrX = sbt(ph, "VscrX", [64, NSEG * 256], BF16)
            vgcol = sbt(ph, "vgcol", [128, 1], F32)
            att_bf = sbt(ph, "att_bf", [128, 8 * 512], BF16)
            A_scr = sbt(ph, "A_scr", [128, 8 * 512], BF16)
            P_sb = sbt(ph, "P_sb", [128, 2048], BF16)
            PT_sb = sbt(ph, "PT_sb", [128, 2048], BF16)
            sq_scr = sbt(ph, "sq_scr", [128, 512], BF16)
            skE = sbt(ph, "skE", [64, 256], BF16)
            skO = sbt(ph, "skO", [64, 256], BF16)
            elu_t = sbt(ph, "elu_t", [128, 256], F32)
            skT_sb = sbt(ph, "skT_sb", [128, 512], BF16)
            M_master = sbt(ph, "M_master", [128, NA * 65], F32)
            M_dupA = sbt(ph, "M_dupA", [128, 130], BF16)
            M_dupB = sbt(ph, "M_dupB", [128, 130], BF16)
            gamrep = sbt(ph, "gamrep", [128, NA * 65], BF16)
            maug2 = sbt(ph, "maug2", [128, NA * 65], F32)
            rdl_a = sbt(ph, "rdl_a", [1, 512], BF16)
            rdl_b = sbt(ph, "rdl_b", [1, 512], BF16)
            rr_sb = sbt(ph, "rr_sb", [128, 512], BF16)
            rl_sb = sbt(ph, "rl_sb", [128, 512], BF16)
            t1_sb = sbt(ph, "t1_sb", [128, 512], BF16)
            t2_sb = sbt(ph, "t2_sb", [128, 512], BF16)

            for a in range(NA):
                for hb in range(2):
                    em.emit("sync", lambda e, a=a, hb=hb: e.dma_start(
                        out=maug2[hb * 64:(hb + 1) * 64, a * 65:(a + 1) * 65],
                        in_=maug_d[a]),
                        waits=([("V", v_xn)] if (a == 0 and hb == 0) else None),
                        inc="dS", k=16)
            d_maug = em.at("dS")

            em.emit("vector", lambda e: nc.vector.memset(Q2A[64:128, :], 0.0),
                    waits=[("V", v_xn)], inc="V")
            em.emit("vector", lambda e: nc.vector.memset(Q2B[0:64, :], 0.0), inc="V")
            em.emit("vector", lambda e: nc.vector.memset(M_dupA[64:128, :], 0.0), inc="V")
            em.emit("vector", lambda e: nc.vector.memset(M_dupB[0:64, :], 0.0), inc="V")
            v_zinit = em.at("V")

            with nc.psum_tensor("ps_small", [128, 512], F32) as ps_small:
                def g65(e):
                    for a in range(NA):
                        ins = nc.tensor.matmul(
                            ps_small[:, a * 65:(a + 1) * 65], ones_b[0:1, :],
                            gam65_b[:, a * 65:(a + 1) * 65],
                            start=(a == 0), stop=(a == NA - 1), skip_group_check=True)
                    return ins
                em.emit("tensor", g65, waits=[("V", em.at("V"))], inc="PE")
                pe_g65 = em.at("PE")
                em.emit("vector", lambda e: nc.vector.tensor_copy(
                    gamrep[:], ps_small[:, 0:NA * 65]),
                    waits=[("PE", pe_g65)], inc="V")
                v_g65 = em.at("V")
            with nc.psum_tensor("ps_small2", [128, 512], F32) as ps_small2:
                def m0rep(e):
                    for a in range(NA):
                        ins = nc.tensor.matmul(
                            ps_small2[:, a * 64:(a + 1) * 64], ones_b[0:1, :],
                            onemg_b[:, a * 64:(a + 1) * 64],
                            start=(a == 0), stop=(a == NA - 1), skip_group_check=True)
                    return ins
                em.emit("tensor", m0rep, waits=[("V", max(v_gate, v_g65))], inc="PE")
                pe_m0 = em.at("PE")
                for a in range(NA):
                    em.emit("vector", lambda e, a=a: nc.vector.tensor_tensor(
                        M_master[:, a * 65:a * 65 + 64], maug2[:, a * 65:a * 65 + 64],
                        ps_small2[:, a * 64:(a + 1) * 64], ALU.mult),
                        waits=[("PE", pe_m0), ("dS", d_maug)], inc="V")
                    em.emit("vector", lambda e, a=a: nc.vector.tensor_copy(
                        M_master[:, a * 65 + 64:a * 65 + 65],
                        maug2[:, a * 65 + 64:a * 65 + 65]), inc="V")
                v_minit = em.at("V")

            pe_prev_a = 0
            v_mup = v_minit

            for a in range(A_RUN):
                em.scope = "proj"
                em.emit("gpsimd", lambda e: wload(e, wvo_b, wv_d, 0),
                        waits=[("PE", pe_prev_a)], inc="dG", k=16)
                d_wv_h = [em.at("dG"), 0]

                pha = contextlib.ExitStack()
                xn_a = pha.enter_context(nc.sbuf_tensor(f"xn_a{a}", [128, 8 * 512], BF16))
                for kc in range(8):
                    src = xn[:].rearrange("p (k s ar j) -> p k s ar j", k=8, s=16, ar=4)[
                        :, kc:kc + 1, :, a:a + 1, :].squeeze(1).squeeze(2)
                    em.emit("vector", lambda e, kc=kc, src=src: nc.vector.tensor_copy(
                        xn_a[:, kc * 512:(kc + 1) * 512].rearrange("p (s j) -> p s j", s=16),
                        src),
                        waits=([("V", v_xn), ("PE", pe_prev_a)] if kc == 0 else None),
                        inc="V")
                v_xna = em.at("V")

                with nc.psum_tensor(f"ps_vgc{a}", [64, 8], F32) as ps_vgc:
                    em.emit("tensor", lambda e, a=a: nc.tensor.transpose(
                        ps_vgc[0:64, 0:1], onemg_row[0:1, a * 64:(a + 1) * 64],
                        identf[0:1, 0:1]),
                        waits=[("V", em.at("V"))], inc="PE")
                    pe_vg = em.at("PE")
                    em.emit("vector", lambda e: nc.vector.tensor_copy(
                        vgcol[0:64, :], ps_vgc[:, 0:1]), waits=[("PE", pe_vg)], inc="V")
                    em.emit("vector", lambda e: e.drain() or e.engine_nop(), inc="V")
                    v_vg0 = em.at("V")
                    em.emit("gpsimd", lambda e: e.dma_start(
                        out=vgcol[64:128, :], in_=vgcol[0:64, :]),
                        waits=[("V", v_vg0)], inc="dG", k=16)
                    d_vg = em.at("dG")
                    v_vg = v_vg0

                with nc.psum_tensor(f"ps_proj{a}", [128, 1024], F32) as ps_proj:
                    v_evac_half = [v_vg, v_vg]
                    dmap = {0: d_wq_h, 1: d_wk_h, 2: d_wv_h}
                    d_wo_h = [0, 0]
                    for wh in range(2):
                        for ti, wb in enumerate((wq_b, wk_b, wvo_b)):
                            for ccl in range(4):
                                cc = wh * 4 + ccl
                                half = ccl % 2

                                def projmm(e, wb=wb, ccl=ccl, half=half):
                                    for kc in range(8):
                                        ins = nc.tensor.matmul(
                                            ps_proj[:, half * 512:(half + 1) * 512],
                                            wb[:, kc * 512 + ccl * 128: kc * 512 + (ccl + 1) * 128],
                                            xn_a[:, kc * 512:(kc + 1) * 512],
                                            start=(kc == 0), stop=(kc == 7))
                                    return ins
                                wl = [("V", max(v_xna, v_evac_half[half]))]
                                if ccl == 0:
                                    wl.append(("dG", dmap[ti][wh]))
                                em.emit("tensor", projmm, waits=wl, inc="PE")
                                pe_p = em.at("PE")
                                if ccl == 3:
                                    if wh == 0:
                                        em.emit("gpsimd", lambda e, wb=wb, ti=ti: wload(
                                            e, wb, (wq_d, wk_d, wv_d)[ti], 1),
                                            waits=[("PE", pe_p)], inc="dG", k=16)
                                        dmap[ti][1] = em.at("dG")
                                    elif ti < 2 and a + 1 < A_RUN:
                                        em.emit("gpsimd", lambda e, wb=wb, ti=ti: wload(
                                            e, wb, (wq_d, wk_d)[ti], 0),
                                            waits=[("PE", pe_p)], inc="dG", k=16)
                                        dmap[ti][0] = em.at("dG")
                                    elif ti == 2:
                                        em.emit("gpsimd", lambda e: wload(e, wvo_b, wo_d, 0),
                                                waits=[("PE", pe_p)], inc="dG", k=16)
                                        d_wo_h[0] = em.at("dG")
                                for hl in range(2):
                                    if ti == 0:
                                        def scram(e, cc=cc, hl=hl, half=half):
                                            qdst = Q2A if hl == 0 else Q2B
                                            dst = qdst[hl * 64:(hl + 1) * 64, :].rearrange(
                                                "p (s c8 j) -> p s c8 j", s=16, c8=8, j=32)[
                                                :, :, cc:cc + 1, :].squeeze(2)
                                            src = ps_proj[hl * 64:(hl + 1) * 64,
                                                          half * 512:(half + 1) * 512].rearrange(
                                                "p (s j) -> p s j", s=16)
                                            return nc.vector.tensor_copy(dst, src)
                                    elif ti == 1:
                                        def scram(e, cc=cc, hl=hl, half=half):
                                            dst = K2[hl * 64:(hl + 1) * 64, :].rearrange(
                                                "p (s j c8) -> p s j c8", s=16, j=32, c8=8)[
                                                :, :, :, cc:cc + 1].squeeze(3)
                                            src = ps_proj[hl * 64:(hl + 1) * 64,
                                                          half * 512:(half + 1) * 512].rearrange(
                                                "p (s j) -> p s j", s=16)
                                            return nc.vector.tensor_copy(dst, src)
                                    else:
                                        def scram(e, cc=cc, hl=hl, half=half):
                                            dst = Vscr[hl * 64:(hl + 1) * 64, :].rearrange(
                                                "p (s j c8) -> p s j c8", s=16, j=32, c8=8)[
                                                :, :, :, cc:cc + 1].squeeze(3)
                                            src = ps_proj[hl * 64:(hl + 1) * 64,
                                                          half * 512:(half + 1) * 512].rearrange(
                                                "p (s j) -> p s j", s=16)
                                            return nc.vector.tensor_scalar(
                                                dst, src, vgcol[hl * 64:(hl + 1) * 64, 0:1],
                                                None, ALU.mult)
                                    wl2 = [("PE", pe_p)]
                                    if ti == 2:
                                        wl2.append(("dG", d_vg))
                                    em.emit("vector", scram, waits=wl2, inc="V")
                                v_evac_half[half] = em.at("V")
                    v_qk = em.at("V")
                pha.close()

                pe_proj_done = em.at("PE")
                em.emit("gpsimd", lambda e: e.dma_start(out=K2X[0:64, :], in_=K2[64:128, :]),
                        waits=[("V", v_qk), ("PE", pe_proj_done)], inc="dG", k=16)
                em.emit("gpsimd", lambda e: e.dma_start(out=K2X[64:128, :], in_=K2[0:64, :]),
                        inc="dG", k=16)
                em.emit("gpsimd", lambda e: e.dma_start(out=VscrX[0:64, :], in_=Vscr[64:128, :]),
                        inc="dG", k=16)
                d_k2x = em.at("dG")

                em.scope = "rec"
                with contextlib.ExitStack() as rec:
                    ps_S = rec.enter_context(nc.psum_tensor(f"ps_S{a}", [128, 1024], F32))
                    ps_pt = rec.enter_context(nc.psum_tensor(f"ps_pt{a}", [128, 2048], BF16))
                    ps_num = rec.enter_context(nc.psum_tensor(f"ps_num{a}", [128, 512], F32))
                    ps_dot = rec.enter_context(nc.psum_tensor(f"ps_dot{a}", [128, 512], F32))
                    ps_msc = rec.enter_context(nc.psum_tensor(f"ps_msc{a}", [128, 512], F32))
                    ps_rr = rec.enter_context(nc.psum_tensor(f"ps_rr{a}", [128, 512], F32))

                    v_comb = em.at("V")
                    v_e1_prev = 0
                    v_rr_prev = 0
                    v_skt_prev = 0
                    sc_p_prev = 0
                    sc_pw_seg = {}

                    def emit_swave(s):
                        nonlocal sc_p_prev
                        sc_pw_seg[s] = [0, 0]
                        for wave in range(2):
                            def swave(e, s=s, wave=wave):
                                for beta in (wave * 2, wave * 2 + 1):
                                    hl, ccg = beta // 2, beta % 2
                                    bk = (beta - wave * 2) * 512
                                    qsrc = Q2A if hl == 0 else Q2B
                                    for hlp in range(2):
                                        ksrc = K2 if hlp == hl else K2X
                                        nc.tensor.matmul(
                                            ps_S[:, bk + hlp * 256: bk + (hlp + 1) * 256],
                                            qsrc[:, s * 256 + ccg * 128: s * 256 + (ccg + 1) * 128],
                                            ksrc[:, s * 256:(s + 1) * 256],
                                            start=(hlp == 0), stop=False,
                                            skip_group_check=True)
                                    ins = nc.tensor.matmul(
                                        ps_S[:, bk:bk + 512], ident[:], masks_b[beta][:],
                                        start=False, stop=True, skip_group_check=True)
                                return ins
                            em.emit("tensor", swave,
                                    waits=[("SC", sc_p_prev), ("dG", d_k2x)], inc="PE")
                            pe_w = em.at("PE")

                            def pexp(e, wave=wave):
                                for bi in range(2):
                                    beta = wave * 2 + bi
                                    ins = nc.scalar.activation(
                                        P_sb[:, beta * 512:(beta + 1) * 512],
                                        ps_S[:, bi * 512:(bi + 1) * 512], AF.Exp,
                                        scale=0.125)
                                return ins
                            em.emit("scalar", pexp,
                                    waits=[("PE", pe_w)], inc="SC")
                            sc_p_prev = em.at("SC")
                            sc_pw_seg[s][wave] = sc_p_prev
                    sc_el_prev = 0
                    pe_pt_prev = 0
                    pe_numden_prev = 0
                    pe_dm_prev = 0
                    pe_rep_prev = 0
                    for s in range(SEG_RUN):
                        so = (s % 2) * 256
                        em.emit("vector", lambda e, s=s: nc.vector.tensor_scalar_min(
                            elu_t[0:64, :], Q2A[0:64, s * 256:(s + 1) * 256], 0.0),
                            waits=[("V", max(v_qk, v_zinit)), ("SC", sc_el_prev)],
                            inc="V")
                        em.emit("vector", lambda e, s=s: nc.vector.tensor_scalar_min(
                            elu_t[64:128, :], Q2B[64:128, s * 256:(s + 1) * 256], 0.0),
                            inc="V")
                        v_el1 = em.at("V")
                        em.emit("scalar", lambda e: nc.scalar.activation(
                            elu_t[:], elu_t[:], AF.Exp),
                            waits=[("V", v_el1)], inc="SC")
                        sc_el = em.at("SC")
                        em.emit("vector", lambda e, s=s, so=so: nc.vector.scalar_tensor_tensor(
                            sq_scr[0:64, so:so + 256], Q2A[0:64, s * 256:(s + 1) * 256], 0.0,
                            elu_t[0:64, :], ALU.max, ALU.add),
                            waits=[("SC", sc_el)], inc="V")
                        em.emit("vector", lambda e, s=s, so=so: nc.vector.scalar_tensor_tensor(
                            sq_scr[64:128, so:so + 256], Q2B[64:128, s * 256:(s + 1) * 256], 0.0,
                            elu_t[64:128, :], ALU.max, ALU.add), inc="V")
                        em.emit("vector", lambda e, s=s: nc.vector.tensor_scalar_min(
                            elu_t[0:64, :], K2[0:64, s * 256:(s + 1) * 256], 0.0),
                            waits=[("PE", pe_pt_prev)], inc="V")
                        v_el2 = em.at("V")
                        em.emit("scalar", lambda e: nc.scalar.activation(
                            elu_t[0:64, :], elu_t[0:64, :], AF.Exp),
                            waits=[("V", v_el2)], inc="SC")
                        sc_el = em.at("SC")
                        em.emit("vector", lambda e, s=s: nc.vector.scalar_tensor_tensor(
                            skE[:], K2[0:64, s * 256:(s + 1) * 256], 0.0,
                            elu_t[0:64, :], ALU.max, ALU.add),
                            waits=[("SC", sc_el)], inc="V")
                        em.emit("vector", lambda e, s=s: nc.vector.tensor_scalar_min(
                            elu_t[0:64, :], K2X[0:64, s * 256:(s + 1) * 256], 0.0),
                            waits=[("dG", d_k2x)], inc="V")
                        v_el3 = em.at("V")
                        em.emit("scalar", lambda e: nc.scalar.activation(
                            elu_t[0:64, :], elu_t[0:64, :], AF.Exp),
                            waits=[("V", v_el3)], inc="SC")
                        sc_el = em.at("SC")
                        sc_el_prev = sc_el
                        em.emit("vector", lambda e, s=s: nc.vector.scalar_tensor_tensor(
                            skO[:], K2X[0:64, s * 256:(s + 1) * 256], 0.0,
                            elu_t[0:64, :], ALU.max, ALU.add),
                            waits=[("SC", sc_el)], inc="V")
                        v_sqsk = em.at("V")
                        if SEG_OPS <= 2:
                            continue

                        def mdup(e, a=a):
                            for hl, MD in ((0, M_dupA), (1, M_dupB)):
                                r0, r1 = hl * 64, hl * 64 + 64
                                for co in (0, 64):
                                    nc.vector.tensor_tensor(
                                        MD[r0:r1, co:co + 64],
                                        M_master[r0:r1, a * 65:a * 65 + 64],
                                        gamrep[r0:r1, a * 65:a * 65 + 64], ALU.mult)
                                ins = nc.vector.tensor_copy(
                                    MD[r0:r1, 128:129],
                                    M_master[r0:r1, a * 65 + 64:a * 65 + 65])
                            return ins
                        em.emit("vector", mdup,
                                waits=[("V", max(v_g65, v_mup)), ("PE", pe_numden_prev)],
                                inc="V")
                        v_mbf = em.at("V")
                        if SEG_OPS <= 3:
                            continue

                        if SEG_OPS <= 4:
                            continue

                        if s == 0 and SEG_OPS > 5:
                            emit_swave(0)
                        if SEG_OPS <= 5:
                            continue

                        for wv in range(2):
                            def ptrans(e, wv=wv):
                                for beta in (wv * 2, wv * 2 + 1):
                                    for kap in range(4):
                                        ins = nc.tensor.transpose(
                                            ps_pt[:, kap * 512 + beta * 128: kap * 512 + (beta + 1) * 128],
                                            P_sb[:, beta * 512 + kap * 128: beta * 512 + (kap + 1) * 128],
                                            ident[:])
                                return ins
                            em.emit("tensor", ptrans,
                                    waits=[("SC", sc_pw_seg[s][wv]), ("V", v_skt_prev)],
                                    inc="PE")
                        pe_pt = em.at("PE")
                        for kap in range(4):
                            em.emit("scalar", lambda e, kap=kap: nc.scalar.activation(
                                PT_sb[:, kap * 512:(kap + 1) * 512],
                                ps_pt[:, kap * 512:(kap + 1) * 512], AF.Copy),
                                waits=([("PE", pe_pt)] if kap == 0 else None), inc="SC")
                            if kap == 0:
                                sc_ptev_k0 = em.at("SC")
                        sc_ptev = em.at("SC")
                        if SEG_OPS <= 7:
                            continue

                        def sktr(e, s=s):
                            for kap in range(4):
                                hlp, jh = kap // 2, kap % 2
                                sksrc = skE if hlp == 0 else skO
                                nc.tensor.transpose(
                                    ps_pt[:, kap * 64:(kap + 1) * 64],
                                    sksrc[0:64, jh * 128:(jh + 1) * 128],
                                    ident[0:64, 0:64])
                            for kap in range(4):
                                hlp, jh = kap // 2, kap % 2
                                ins = nc.tensor.transpose(
                                    ps_pt[:, 256 + kap * 64:256 + (kap + 1) * 64],
                                    (Vscr if hlp == 0 else VscrX)[
                                        0:64, s * 256 + jh * 128: s * 256 + (jh + 1) * 128],
                                    ident[0:64, 0:64])
                            return ins
                        em.emit("tensor", sktr,
                                waits=[("SC", sc_ptev_k0), ("V", max(v_sqsk, v_qk)),
                                       ("dG", d_k2x)], inc="PE")
                        pe_skt = em.at("PE")
                        pe_pt_prev = pe_skt
                        em.emit("vector", lambda e: nc.vector.tensor_copy(
                            skT_sb[:], ps_pt[:, 0:512]),
                            waits=[("PE", pe_skt)], inc="V")
                        v_skt = em.at("V")
                        v_skt_prev = v_skt
                        if SEG_OPS <= 9:
                            continue

                        def dmm(e, s=s):
                            for ob in range(2):
                                for kap in range(4):
                                    nc.tensor.matmul(
                                        ps_rr[ob * 64:(ob + 1) * 64, 0:64],
                                        skT_sb[:, kap * 64:(kap + 1) * 64],
                                        skT_sb[:, 256 + kap * 64:256 + (kap + 1) * 64],
                                        start=(kap == 0), stop=False,
                                        skip_group_check=True)
                                for kap in range(4):
                                    ins = nc.tensor.matmul(
                                        ps_rr[ob * 64:(ob + 1) * 64, 64:65],
                                        skT_sb[:, kap * 64:(kap + 1) * 64],
                                        ones_b[:, 0:1],
                                        start=False, stop=(kap == 3),
                                        skip_group_check=True)
                            return ins
                        em.emit("tensor", dmm,
                                waits=[("V", max(v_skt, v_rr_prev))], inc="PE")
                        pe_dm = em.at("PE")
                        pe_dm_prev = pe_dm
                        em.emit("vector", lambda e, a=a: nc.vector.tensor_tensor(
                            M_master[:, a * 65:(a + 1) * 65], M_master[:, a * 65:(a + 1) * 65],
                            ps_rr[:, 0:65], ALU.add),
                            waits=[("PE", pe_dm), ("V", v_mbf)], inc="V")
                        v_mup = em.at("V")

                        def numden(e, s=s):
                            so = (s % 2) * 256
                            for hl, MD in ((0, M_dupA), (1, M_dupB)):
                                nc.tensor.matmul(
                                    ps_num[:, hl * 256:(hl + 1) * 256],
                                    MD[:, 0:128], sq_scr[:, so:so + 256],
                                    start=True, stop=True, skip_group_check=True)
                            for hl, MD in ((0, M_dupA), (1, M_dupB)):
                                ins = nc.tensor.matmul(
                                    ps_msc[0:1, hl * 256:(hl + 1) * 256],
                                    MD[:, 128:129], sq_scr[:, so:so + 256],
                                    start=True, stop=True, skip_group_check=True)
                            return ins
                        em.emit("tensor", numden,
                                waits=[("V", max(v_mbf, v_sqsk, v_e1_prev))], inc="PE")
                        pe_num = em.at("PE")
                        pe_numden_prev = pe_num

                        def lammm(e, s=s):
                            for kap in range(4):
                                ins = nc.tensor.matmul(
                                    ps_rr[0:1, :], ones_b[:, 0:1],
                                    PT_sb[:, kap * 512:(kap + 1) * 512],
                                    start=(kap == 0), stop=(kap == 3),
                                    skip_group_check=True)
                            return ins
                        em.emit("tensor", lammm,
                                waits=[("V", max(v_comb, v_e1_prev, v_skt, v_mup)),
                                       ("SC", sc_ptev)],
                                inc="PE")
                        pe_lam = em.at("PE")

                        def dotmm(e, s=s):
                            for ob in range(2):
                                for kap in range(4):
                                    ins = nc.tensor.matmul(
                                        ps_dot[ob * 64:(ob + 1) * 64, :],
                                        skT_sb[:, 256 + kap * 64:256 + (kap + 1) * 64],
                                        PT_sb[:, kap * 512:(kap + 1) * 512],
                                        start=(kap == 0), stop=(kap == 3),
                                        skip_group_check=True)
                            return ins
                        em.emit("tensor", dotmm, inc="PE")
                        pe_dot = em.at("PE")
                        if s + 1 < SEG_RUN:
                            emit_swave(s + 1)
                        if SEG_OPS <= 10:
                            continue

                        def recips_a(e):
                            with nc.allow_low_precision(reason="bf16 matmul operand"):
                                return nc.vector.reciprocal(rdl_a[:], ps_msc[0:1, :])
                        def recips_b(e):
                            with nc.allow_low_precision(reason="bf16 matmul operand"):
                                return nc.vector.reciprocal(rdl_b[:], ps_rr[0:1, :])
                        em.emit("vector", recips_a, waits=[("PE", pe_num)], inc="V")
                        em.emit("vector", recips_b, waits=[("PE", pe_lam)], inc="V")
                        v_rdl = em.at("V")
                        if SEG_OPS <= 11:
                            continue

                        def rrmm(e):
                            nc.tensor.matmul(ps_rr[:], ones_b[0:1, :], rdl_a[:],
                                             start=True, stop=True, skip_group_check=True)
                            return nc.tensor.matmul(ps_msc[:], ones_b[0:1, :], rdl_b[:],
                                                    start=True, stop=True,
                                                    skip_group_check=True)
                        em.emit("tensor", rrmm,
                                waits=[("V", v_rdl)], inc="PE")
                        pe_rep = em.at("PE")
                        pe_rep_prev = pe_rep
                        if SEG_OPS <= 12:
                            continue
                        em.emit("vector", lambda e: nc.vector.tensor_copy(
                            rr_sb[:], ps_rr[:]), waits=[("PE", pe_rep)], inc="V")
                        em.emit("vector", lambda e: nc.vector.tensor_copy(
                            rl_sb[:], ps_msc[:]), inc="V")
                        v_rr = em.at("V")
                        v_rr_prev = v_rr
                        em.emit("vector", lambda e: nc.vector.tensor_tensor(
                            t1_sb[:], ps_num[:], rr_sb[:], ALU.mult), inc="V")
                        v_e1 = em.at("V")
                        v_e1_prev = v_e1
                        em.emit("vector", lambda e: nc.vector.tensor_tensor(
                            t2_sb[:], ps_dot[:], rl_sb[:], ALU.mult), inc="V")
                        em.emit("vector", lambda e, s=s: nc.vector.tensor_tensor(
                            att_bf[:, (s % 8) * 512:(s % 8 + 1) * 512], t1_sb[:], t2_sb[:],
                            ALU.add), inc="V")
                        v_comb = em.at("V")
                        if DBG_SEG0 == 1 and a == 0 and s == 0:
                            dumps = [
                                (outT[0:128, 0:512], att_bf[:, 0:512]),
                                (outT[0:128, 512:1024], t1_sb[:]),
                                (outT[0:128, 1024:1536], t2_sb[:]),
                                (outT[0:128, 1536:2048], skT_sb[:]),
                                (outT[128:256, 0:512], rr_sb[:]),
                                (outT[128:256, 512:1024], rl_sb[:]),
                                (outT[128:256, 1024:1280], sq_scr[:, 0:256]),
                                (outT[128:256, 1280:1792], P_sb[:, 0:512]),
                                (outT[256:384, :], PT_sb[:]),
                                (outT[384:512, 0:129], M_dupA[:, 0:129]),
                                (outT[384:512, 512:641], M_dupB[:, 0:129]),
                                (outT[384:512, 1024:1089], gamrep[:, 0:65]),
                            ]
                            for i, (dst, src) in enumerate(dumps):
                                em.emit("gpsimd", lambda e, dst=dst, src=src: e.dma_start(
                                    out=dst, in_=src),
                                    waits=([("V", v_comb)] if i == 0 else None),
                                    inc="dO", k=16)
                            em.emit("gpsimd", lambda e: e.dma_start(
                                out=outT[512:640, 0:65], in_=M_master[:, 0:65]),
                                inc="dO", k=16)
                            d_dump = em.at("dO")
                            em.emit("vector", lambda e: e.engine_nop(),
                                    waits=[("dO", d_dump)], inc="V")
                            break
                        if SEG_OPS <= 13:
                            continue

                        if s in (7, 15):
                            base = 0 if s == 7 else 256
                            for h in range(16):
                                hl = h % 2
                                cc = h // 2
                                ccg, ccl = cc // 4, cc % 4
                                moff = hl * 256 + ccg * 128 + ccl * 32

                                def unscr(e, h=h, hl=hl, base=base, moff=moff):
                                    ch = h // 2
                                    dst = A_scr[hl * 64:(hl + 1) * 64,
                                                ch * 512 + base: ch * 512 + base + 256
                                                ].rearrange("p (s j) -> p s j", s=8)
                                    src = att_bf[hl * 64:(hl + 1) * 64, :].rearrange(
                                        "p (s m) -> p s m", s=8)[:, :, moff:moff + 32]
                                    return nc.vector.tensor_copy(dst, src)
                                em.emit("vector", unscr, inc="V")
                            v_comb = em.at("V")
                    v_unscr = em.at("V")

                if DBG_SEG0 == 2 and a == 0:
                    dumps2 = [
                        (outT[0:128, :], att_bf[:, 0:2048]),
                        (outT[128:256, :], att_bf[:, 2048:4096]),
                        (outT[256:384, :], A_scr[:, 0:2048]),
                        (outT[384:512, :], A_scr[:, 2048:4096]),
                    ]
                    for i, (dst, src) in enumerate(dumps2):
                        em.emit("gpsimd", lambda e, dst=dst, src=src: e.dma_start(
                            out=dst, in_=src),
                            waits=([("V", v_unscr)] if i == 0 else None),
                            inc="dO", k=16)
                    d_dump2 = em.at("dO")
                    em.emit("vector", lambda e: e.engine_nop(),
                            waits=[("dO", d_dump2)], inc="V")

                em.scope = "wo"
                if DBG_SEG0:
                    continue
                with nc.psum_tensor(f"ps_wo{a}", [128, 1024], F32) as ps_wo:
                    v_woev = [v_unscr, v_unscr]
                    for oc in range(8):
                        half = oc % 2
                        wh = oc // 4
                        ocl = oc % 4

                        def womm(e, ocl=ocl, half=half):
                            for ch in range(8):
                                ins = nc.tensor.matmul(
                                    ps_wo[:, half * 512:(half + 1) * 512],
                                    wvo_b[:, ch * 512 + ocl * 128: ch * 512 + (ocl + 1) * 128],
                                    A_scr[:, ch * 512:(ch + 1) * 512],
                                    start=(ch == 0), stop=(ch == 7))
                            return ins
                        wl = [("V", max(v_unscr, v_woev[half]))]
                        if ocl == 0:
                            wl.append(("dG", d_wo_h[wh]))
                        em.emit("tensor", womm, waits=wl, inc="PE")
                        pe_wo = em.at("PE")
                        if oc == 3:
                            em.emit("gpsimd", lambda e: wload(e, wvo_b, wo_d, 1),
                                    waits=[("PE", pe_wo)], inc="dG", k=16)
                            d_wo_h[1] = em.at("dG")

                        def woev(e, oc=oc, half=half, a=a):
                            dst = att_out[:].rearrange(
                                "p (o s ar j) -> p o s ar j", o=8, s=16, ar=4)[
                                :, oc:oc + 1, :, a:a + 1, :].squeeze(1).squeeze(2)
                            src = ps_wo[:, half * 512:(half + 1) * 512].rearrange(
                                "p (s j) -> p s j", s=16)
                            return nc.vector.tensor_copy(dst, src)
                        em.emit("vector", woev, waits=[("PE", pe_wo)], inc="V")
                        v_woev[half] = em.at("V")
                pe_prev_a = em.at("PE")
            v_attn = em.at("V")
        p2w.close()

        if STOP_AFTER == 'attn':
            if not DBG_SEG0:
                _finale(nc, em, att_out, outT, v_attn)
            _replay(nc, em)
            return nc

        em.scope = "ffn"
        with contextlib.ExitStack() as ph:
            acc = sbt(ph, "acc", [128, 8 * TL], F32)
            combT = sbt(ph, "combT", [16, TL], BF16)

            with contextlib.ExitStack() as lgs:
                lg_sb = sbt(lgs, "lg_sb", [16, TL], F32)
                lgt = sbt(lgs, "lgt", [128, 256], F32)
                est = sbt(lgs, "est", [128, 256], F32)
                m1s = sbt(lgs, "m1s", [128, 16], F32)
                nm1 = sbt(lgs, "nm1", [128, 16], F32)
                ssum = sbt(lgs, "ssum", [128, 16], F32)
                rs = sbt(lgs, "rs", [128, 16], F32)
                eqm = sbt(lgs, "eqm", [128, 256], F32)
                m2s = sbt(lgs, "m2s", [128, 16], F32)
                keep = sbt(lgs, "keep", [128, 256], F32)
                comb = sbt(lgs, "comb", [128, 256], F32)
                comb_b = sbt(lgs, "comb_b", [128, 256], BF16)

                with nc.psum_tensor("ps_lg", [16, TL], F32) as ps_lg:
                    def lgmm(e):
                        for kc in range(8):
                            for nt in range(4):
                                ins = nc.tensor.matmul(
                                    ps_lg[:, nt * 512:(nt + 1) * 512],
                                    gw_b[:, kc * NE:(kc + 1) * NE],
                                    xn[:, kc * TL + nt * 512: kc * TL + (nt + 1) * 512],
                                    start=(kc == 0), stop=(kc == 7), skip_group_check=True)
                        return ins
                    em.emit("tensor", lgmm, waits=[("V", v_attn)], inc="PE")
                    pe_lg = em.at("PE")
                    em.emit("vector", lambda e: nc.vector.tensor_copy(lg_sb[:], ps_lg[:]),
                            waits=[("PE", pe_lg)], inc="V")
                    v_lg = em.at("V")
                with nc.psum_tensor("ps_lgt", [128, 256], F32) as ps_lgt:
                    def lgtr(e):
                        for tc in range(16):
                            ins = nc.tensor.transpose(
                                ps_lgt[:, tc * 16:(tc + 1) * 16],
                                lg_sb[:, tc * 128:(tc + 1) * 128], identf[:])
                        return ins
                    em.emit("tensor", lgtr, waits=[("V", v_lg)], inc="PE")
                    pe_lgt = em.at("PE")
                    em.emit("vector", lambda e: nc.vector.tensor_copy(lgt[:], ps_lgt[:]),
                            waits=[("PE", pe_lgt)], inc="V")

                    def m1r(e):
                        for tc in range(16):
                            ins = nc.vector.reduce_max(
                                m1s[:, tc:tc + 1], lgt[:, tc * 16:(tc + 1) * 16], AX.X)
                        return ins
                    em.emit("vector", m1r, inc="V")
                    em.emit("vector", lambda e: nc.vector.tensor_scalar(
                        nm1[:], m1s[:], -1.0, None, ALU.mult), inc="V")
                    em.emit("vector", lambda e: e.drain() or e.engine_nop(), inc="V")
                    v_m1 = em.at("V")

                    def sexp(e):
                        for tc in range(16):
                            ins = nc.scalar.activation(
                                est[:, tc * 16:(tc + 1) * 16], lgt[:, tc * 16:(tc + 1) * 16],
                                AF.Exp, bias=nm1[:, tc:tc + 1], accum_out=ssum[:, tc:tc + 1])
                        return ins
                    em.emit("scalar", sexp, waits=[("V", v_m1)], inc="SC")
                    sc_e = em.at("SC")
                    em.emit("vector", lambda e: nc.vector.reciprocal(rs[:], ssum[:]),
                            waits=[("SC", sc_e)], inc="V")

                    def eqms(e):
                        for tc in range(16):
                            ins = nc.vector.tensor_scalar(
                                eqm[:, tc * 16:(tc + 1) * 16], lgt[:, tc * 16:(tc + 1) * 16],
                                m1s[:, tc:tc + 1], None, ALU.is_equal)
                        return ins
                    em.emit("vector", eqms, inc="V")
                    em.emit("vector", lambda e: nc.vector.scalar_tensor_tensor(
                        eqm[:], eqm[:], NEGC, lgt[:], ALU.mult, ALU.add), inc="V")

                    def m2r(e):
                        for tc in range(16):
                            ins = nc.vector.reduce_max(
                                m2s[:, tc:tc + 1], eqm[:, tc * 16:(tc + 1) * 16], AX.X)
                        return ins
                    em.emit("vector", m2r, inc="V")
                    em.emit("vector", lambda e: e.drain() or e.engine_nop(), inc="V")

                    def keeps(e):
                        for tc in range(16):
                            ins = nc.vector.tensor_scalar(
                                keep[:, tc * 16:(tc + 1) * 16], lgt[:, tc * 16:(tc + 1) * 16],
                                m2s[:, tc:tc + 1], None, ALU.is_ge)
                        return ins
                    em.emit("vector", keeps, inc="V")

                    def combs(e):
                        for tc in range(16):
                            ins = nc.vector.scalar_tensor_tensor(
                                comb[:, tc * 16:(tc + 1) * 16], est[:, tc * 16:(tc + 1) * 16],
                                rs[:, tc:tc + 1], keep[:, tc * 16:(tc + 1) * 16],
                                ALU.mult, ALU.mult)
                        return ins
                    em.emit("vector", combs, inc="V")
                    em.emit("vector", lambda e: nc.vector.tensor_copy(comb_b[:], comb[:]), inc="V")
                    v_comb2 = em.at("V")
                with nc.psum_tensor("ps_cbt", [16, TL], BF16) as ps_cbt:
                    def cbtr(e):
                        for tc in range(16):
                            ins = nc.tensor.transpose(
                                ps_cbt[:, tc * 128:(tc + 1) * 128],
                                comb_b[:, tc * 16:(tc + 1) * 16], ident[:])
                        return ins
                    em.emit("tensor", cbtr, waits=[("V", v_comb2)], inc="PE")
                    pe_cbt = em.at("PE")
                    em.emit("vector", lambda e: nc.vector.tensor_copy(combT[:], ps_cbt[:]),
                            waits=[("PE", pe_cbt)], inc="V")
                    v_cbt = em.at("V")

            em.scope = "ffn_sh"
            with contextlib.ExitStack() as shs:
                wsg_b = sbt(shs, "wsg_b", [128, 8 * D], BF16)
                wsu_b = sbt(shs, "wsu_b", [128, 8 * D], BF16)
                wsd_b = sbt(shs, "wsd_b", [128, 8 * D], BF16)
                act_sh = sbt(shs, "act_sh", [128, 4 * 1024], BF16)
                sg_t = sbt(shs, "sg_t", [128, 1024], BF16)
                first = True
                for dst, src in ((wsg_b, wsg_d), (wsu_b, wsu_d), (wsd_b, wsd_d)):
                    em.emit("gpsimd", lambda e, dst=dst, src=src: e.dma_start(
                        out=dst[:].rearrange("p (r c) -> p r c", r=8),
                        in_=src.rearrange("(r p) c -> p r c", p=128)),
                        waits=([("PE", em.at("PE")), ("V", em.at("V"))] if first else None),
                        inc="dG", k=16)
                    first = False
                d_ws = em.at("dG")

                with contextlib.ExitStack() as pss:
                    ps_g = pss.enter_context(nc.psum_tensor("ps_sg", [128, 1024], F32))
                    ps_u = pss.enter_context(nc.psum_tensor("ps_su", [128, 1024], F32))
                    ps_d2 = pss.enter_context(nc.psum_tensor("ps_sd2", [128, 1024], F32))
                    for th in range(2):
                        for ih in range(2):
                            for icl in range(4):
                                ic = ih * 4 + icl

                                def gumm(e, th=th, ic=ic):
                                    for kc in range(8):
                                        for nh in range(2):
                                            nc.tensor.matmul(
                                                ps_g[:, nh * 512:(nh + 1) * 512],
                                                wsg_b[:, kc * D + ic * 128: kc * D + (ic + 1) * 128],
                                                xn[:, kc * TL + th * 1024 + nh * 512:
                                                   kc * TL + th * 1024 + (nh + 1) * 512],
                                                start=(kc == 0), stop=(kc == 7),
                                                skip_group_check=True)
                                    for kc in range(8):
                                        for nh in range(2):
                                            ins = nc.tensor.matmul(
                                                ps_u[:, nh * 512:(nh + 1) * 512],
                                                wsu_b[:, kc * D + ic * 128: kc * D + (ic + 1) * 128],
                                                xn[:, kc * TL + th * 1024 + nh * 512:
                                                   kc * TL + th * 1024 + (nh + 1) * 512],
                                                start=(kc == 0), stop=(kc == 7),
                                                skip_group_check=True)
                                    return ins
                                wl = [("V", em.at("V")), ("SC", em.at("SC"))]
                                if th == 0 and ic == 0:
                                    wl.append(("dG", d_ws))
                                em.emit("tensor", gumm, waits=wl, inc="PE")
                                pe_gu = em.at("PE")
                                em.emit("scalar", lambda e: nc.scalar.activation(
                                    sg_t[:], ps_g[:], AF.Silu),
                                    waits=[("PE", pe_gu)], inc="SC")
                                sc_sg = em.at("SC")
                                em.emit("vector", lambda e, icl=icl: nc.vector.tensor_tensor(
                                    act_sh[:, icl * 1024:(icl + 1) * 1024], sg_t[:], ps_u[:],
                                    ALU.mult),
                                    waits=[("SC", sc_sg)], inc="V")
                            v_actd = em.at("V")
                            v_dev = v_actd
                            for oc in range(8):
                                def sdmm(e, oc=oc, ih=ih):
                                    for icl in range(4):
                                        ic = ih * 4 + icl
                                        for nh in range(2):
                                            ins = nc.tensor.matmul(
                                                ps_d2[:, nh * 512:(nh + 1) * 512],
                                                wsd_b[:, ic * D + oc * 128: ic * D + (oc + 1) * 128],
                                                act_sh[:, icl * 1024 + nh * 512:
                                                       icl * 1024 + (nh + 1) * 512],
                                                start=(icl == 0), stop=(icl == 3),
                                                skip_group_check=True)
                                    return ins
                                em.emit("tensor", sdmm, waits=[("V", v_dev)], inc="PE")
                                pe_sd = em.at("PE")
                                if ih == 0:
                                    em.emit("vector", lambda e, oc=oc, th=th: nc.vector.tensor_copy(
                                        acc[:, oc * TL + th * 1024: oc * TL + (th + 1) * 1024],
                                        ps_d2[:]),
                                        waits=[("PE", pe_sd)], inc="V")
                                else:
                                    em.emit("vector", lambda e, oc=oc, th=th: nc.vector.tensor_tensor(
                                        acc[:, oc * TL + th * 1024: oc * TL + (th + 1) * 1024],
                                        acc[:, oc * TL + th * 1024: oc * TL + (th + 1) * 1024],
                                        ps_d2[:], ALU.add),
                                        waits=[("PE", pe_sd)], inc="V")
                                v_dev = em.at("V")

            em.scope = "ffn_expw"
            with contextlib.ExitStack() as exs:
                wgb = [sbt(exs, f"wgb{i}", [128, 8 * EI], BF16) for i in range(2)]
                wub = [sbt(exs, f"wub{i}", [128, 8 * EI], BF16) for i in range(2)]
                wdb = [sbt(exs, f"wdb{i}", [128, 4 * D], BF16) for i in range(2)]
                act_e = sbt(exs, "act_e", [128, 4 * 1024], BF16)
                sg_e = sbt(exs, "sg_e", [128, 1024], BF16)
                uc_e = sbt(exs, "uc_e", [128, 1024], BF16)
                crep = sbt(exs, "crep", [128, 2 * TL], BF16)

                d_exp = [0] * NE
                d_crep = [0] * NE
                pe_wuse = [0] * NE
                v_uexp = [0] * NE

                def emit_exp_load(ei):
                    sl = ei % 2
                    wl = ([("PE", pe_wuse[ei - 2])] if ei >= 2
                          else [("PE", em.at("PE")), ("V", em.at("V"))])
                    em.emit("gpsimd", lambda e, ei=ei, sl=sl: e.dma_start(
                        out=wgb[sl][:].rearrange("p (r c) -> p r c", r=8),
                        in_=wg_d[ei].rearrange("(r p) c -> p r c", p=128)),
                        waits=wl, inc="dG", k=16)
                    em.emit("gpsimd", lambda e, ei=ei, sl=sl: e.dma_start(
                        out=wub[sl][:].rearrange("p (r c) -> p r c", r=8),
                        in_=wu_d[ei].rearrange("(r p) c -> p r c", p=128)), inc="dG", k=16)
                    em.emit("gpsimd", lambda e, ei=ei, sl=sl: e.dma_start(
                        out=wdb[sl][:].rearrange("p (r c) -> p r c", r=4),
                        in_=wd_d[ei].rearrange("(r p) c -> p r c", p=128)), inc="dG", k=16)
                    d_exp[ei] = em.at("dG")

                def emit_crep_build(ei):
                    sl = ei % 2
                    base = sl * TL
                    wl = [("V", v_cbt)]
                    if ei >= 2:
                        wl.append(("V", v_uexp[ei - 2]))
                    em.emit("sync", lambda e, ei=ei, base=base: e.dma_start(
                        out=crep[0:1, base:base + TL], in_=combT[ei:ei + 1, :]),
                        waits=wl, inc="dS", k=16)
                    for step in (1, 2, 4, 8, 16, 32, 64):
                        dprev = em.at("dS")
                        em.emit("sync", lambda e, step=step, base=base: e.dma_start(
                            out=crep[step:2 * step, base:base + TL],
                            in_=crep[0:step, base:base + TL]),
                            waits=[("dS", dprev)], inc="dS", k=16)
                    d_crep[ei] = em.at("dS")

                emit_exp_load(0)
                if NE_RUN > 1:
                    emit_exp_load(1)

                with contextlib.ExitStack() as pse:
                    ps_g = pse.enter_context(nc.psum_tensor("ps_eg", [128, 1024], F32))
                    ps_u = pse.enter_context(nc.psum_tensor("ps_eu", [128, 1024], F32))
                    ps_d2 = pse.enter_context(nc.psum_tensor("ps_ed", [128, 1024], F32))

                    em.scope = "ffn_exp"
                    half_user = {0: (0, 0), 1: (0, 0)}
                    sg_user = {0: 0, 1: 0}
                    act_user = {0: 0, 1: 0}
                    gp_act = {}
                    v_dev = [em.at("V"), em.at("V")]
                    pending_ed = None

                    def emit_gu(ei, tp):
                        sl = ei % 2
                        for ic in range(4):
                            h = ic % 2

                            def gumm(e, sl=sl, tp=tp, ic=ic, h=h):
                                for kc in range(8):
                                    nc.tensor.matmul(
                                        ps_g[:, h * 512:(h + 1) * 512],
                                        wgb[sl][:, kc * EI + ic * 128: kc * EI + (ic + 1) * 128],
                                        xn[:, kc * TL + tp * 512: kc * TL + (tp + 1) * 512],
                                        start=(kc == 0), stop=(kc == 7),
                                        skip_group_check=True)
                                for kc in range(8):
                                    ins = nc.tensor.matmul(
                                        ps_u[:, h * 512:(h + 1) * 512],
                                        wub[sl][:, kc * EI + ic * 128: kc * EI + (ic + 1) * 128],
                                        xn[:, kc * TL + tp * 512: kc * TL + (tp + 1) * 512],
                                        start=(kc == 0), stop=(kc == 7),
                                        skip_group_check=True)
                                return ins
                            sc_u, v_u = half_user[h]
                            wl = [("SC", sc_u), ("V", v_u)]
                            if tp == 0 and ic == 0:
                                wl.append(("dG", d_exp[ei]))
                            em.emit("tensor", gumm, waits=wl, inc="PE")
                            pe_gu = em.at("PE")
                            em.emit("scalar", lambda e, h=h: nc.scalar.activation(
                                sg_e[:, h * 512:(h + 1) * 512],
                                ps_g[:, h * 512:(h + 1) * 512], AF.Silu),
                                waits=[("PE", pe_gu), ("GP", sg_user[h])], inc="SC")
                            sc_sg = em.at("SC")
                            wl2 = [("PE", pe_gu), ("GP", sg_user[h])]
                            if tp == 0 and ic == 0:
                                wl2.append(("dS", d_crep[ei]))

                            def ucop(e, h=h, tp=tp, sl=sl):
                                return nc.vector.scalar_tensor_tensor(
                                    uc_e[:, h * 512:(h + 1) * 512],
                                    ps_u[:, h * 512:(h + 1) * 512], 0.0,
                                    crep[:, sl * TL + tp * 512: sl * TL + (tp + 1) * 512],
                                    ALU.add, ALU.mult)
                            em.emit("vector", ucop, waits=wl2, inc="V")
                            v_uc = em.at("V")
                            half_user[h] = (sc_sg, v_uc)
                            asl = tp % 2
                            em.emit("gpsimd", lambda e, h=h, ic=ic, asl=asl: e.tensor_tensor(
                                act_e[:, asl * 2048 + ic * 512: asl * 2048 + (ic + 1) * 512],
                                sg_e[:, h * 512:(h + 1) * 512],
                                uc_e[:, h * 512:(h + 1) * 512], ALU.mult),
                                waits=[("SC", sc_sg), ("V", v_uc),
                                       ("PE", act_user[asl])], inc="GP")
                            sg_user[h] = em.at("GP")
                        gp_act[(ei, tp)] = em.at("GP")
                        if tp == 3:
                            v_uexp[ei] = em.at("V")

                    def emit_ed(ei, tp):
                        sl = ei % 2
                        asl = tp % 2
                        for oc in range(8):
                            h = oc % 2

                            def edmm(e, sl=sl, asl=asl, oc=oc, h=h):
                                for ic in range(4):
                                    ins = nc.tensor.matmul(
                                        ps_d2[:, h * 512:(h + 1) * 512],
                                        wdb[sl][:, ic * D + oc * 128: ic * D + (oc + 1) * 128],
                                        act_e[:, asl * 2048 + ic * 512: asl * 2048 + (ic + 1) * 512],
                                        start=(ic == 0), stop=(ic == 3),
                                        skip_group_check=True)
                                return ins
                            wl = [("V", v_dev[h])]
                            if oc == 0:
                                wl.append(("GP", gp_act[(ei, tp)]))
                            em.emit("tensor", edmm, waits=wl, inc="PE")
                            pe_ed = em.at("PE")

                            def dacc(e, oc=oc, tp=tp, h=h):
                                return nc.vector.tensor_tensor(
                                    acc[:, oc * TL + tp * 512: oc * TL + (tp + 1) * 512],
                                    acc[:, oc * TL + tp * 512: oc * TL + (tp + 1) * 512],
                                    ps_d2[:, h * 512:(h + 1) * 512], ALU.add)
                            em.emit("vector", dacc, waits=[("PE", pe_ed)], inc="V")
                            v_dev[h] = em.at("V")
                        act_user[asl] = em.at("PE")

                    emit_crep_build(0)
                    if NE_RUN > 1:
                        emit_crep_build(1)
                    for ei in range(NE_RUN):
                        emit_gu(ei, 0)
                        if pending_ed is not None:
                            emit_ed(*pending_ed)
                            pe_wuse[pending_ed[0]] = em.at("PE")
                            if pending_ed[0] + 2 < NE_RUN:
                                emit_exp_load(pending_ed[0] + 2)
                                emit_crep_build(pending_ed[0] + 2)
                        emit_gu(ei, 1)
                        emit_ed(ei, 0)
                        emit_gu(ei, 2)
                        emit_ed(ei, 1)
                        emit_gu(ei, 3)
                        emit_ed(ei, 2)
                        pending_ed = (ei, 3)
                    emit_ed(*pending_ed)
                v_ffn = em.at("V")

            em.scope = "p4"
            with contextlib.ExitStack() as ph4:
                xf2 = [sbt(ph4, f"xf2_{i}", [128, TL], F32) for i in range(2)]
                osb = [sbt(ph4, f"osb{i}", [128, TL], F32) for i in range(2)]
                v_of = [v_ffn, v_ffn]
                d_of = [0, 0]
                for oc in range(8):
                    half = oc % 2
                    em.emit("sync", lambda e, oc=oc, half=half: e.dma_start(
                        out=xf2[half][:], in_=xT[oc * 128:(oc + 1) * 128, :]),
                        waits=[("V", v_of[half])], inc="dS", k=16)
                    d_x2 = em.at("dS")
                    em.emit("vector", lambda e, oc=oc, half=half: nc.vector.tensor_tensor(
                        osb[half][:], xf2[half][:], att_out[:, oc * TL:(oc + 1) * TL],
                        ALU.add),
                        waits=[("dS", d_x2), ("dO", d_of[half]), ("V", v_ffn)], inc="V")
                    em.emit("vector", lambda e, oc=oc, half=half: nc.vector.tensor_tensor(
                        osb[half][:], osb[half][:], acc[:, oc * TL:(oc + 1) * TL],
                        ALU.add), inc="V")
                    v_of[half] = em.at("V")
                    em.emit("sync", lambda e, oc=oc, half=half: e.dma_start(
                        out=outT[oc * 128:(oc + 1) * 128, :], in_=osb[half][:]),
                        waits=[("V", v_of[half])], inc="dO", k=16)
                    d_of[half] = em.at("dO")
                d_all = em.at("dO")
                em.emit("sync", lambda e: e.nop(), waits=[("dO", d_all)], inc=None)

        _replay(nc, em)

    return nc


def _finale(nc, em, src_bf, outT, v_wait):
    for oc in range(8):
        em.emit("gpsimd", lambda e, oc=oc: e.dma_start(
            out=outT[oc * 128:(oc + 1) * 128, :],
            in_=src_bf[:, oc * TL:(oc + 1) * TL]),
            waits=([("V", v_wait)] if oc == 0 else None), inc="dO", k=16)
    d_all = em.at("dO")
    em.emit("gpsimd", lambda e: e.nop(), waits=[("dO", d_all)], inc=None)


def _run_ops(nc, e, ops):
    for sc, f in ops:
        if sc:
            with nc.named_scope(sc):
                f(e)
        else:
            f(e)


def _replay(nc, em):
    with nc.Block() as block:
        @block.sync
        def _(e):
            _run_ops(nc, e, em.ops["sync"])

        @block.gpsimd
        def _(e):
            _run_ops(nc, e, em.ops["gpsimd"])

        @block.scalar
        def _(e):
            _run_ops(nc, e, em.ops["scalar"])

        @block.vector
        def _(e):
            _run_ops(nc, e, em.ops["vector"])

        @block.tensor
        def _(e):
            _run_ops(nc, e, em.ops["tensor"])


_NC_CACHE = None


def kernel(L0_attn_x, L0_attn_Wq, L0_attn_Wk, L0_attn_Wv, L0_attn_Wo,
           L0_attn_betas, L0_attn_memory0, L0_attn_z0,
           L0_ffn_gate_w, L0_ffn_w_gate, L0_ffn_w_up, L0_ffn_w_down,
           L0_ffn_ws_gate, L0_ffn_ws_up, L0_ffn_ws_down):
    global _NC_CACHE
    x = np.asarray(L0_attn_x, dtype=np.float32)
    assert x.shape == (2, 8192, 1024)

    cont = np.ascontiguousarray
    shared = {
        "wq": cont(np.asarray(L0_attn_Wq, np.float32)),
        "wk": cont(np.asarray(L0_attn_Wk, np.float32)),
        "wv": cont(np.asarray(L0_attn_Wv, np.float32)),
        "wo": cont(np.asarray(L0_attn_Wo, np.float32)),
        "gwT": cont(np.asarray(L0_ffn_gate_w, np.float32).T),
        "wg": cont(np.asarray(L0_ffn_w_gate, np.float32)),
        "wu": cont(np.asarray(L0_ffn_w_up, np.float32)),
        "wd": cont(np.asarray(L0_ffn_w_down, np.float32)),
        "wsg": cont(np.asarray(L0_ffn_ws_gate, np.float32)),
        "wsu": cont(np.asarray(L0_ffn_ws_up, np.float32)),
        "wsd": cont(np.asarray(L0_ffn_ws_down, np.float32)),
    }
    betas = np.asarray(L0_attn_betas, np.float32)
    mem0 = np.asarray(L0_attn_memory0, np.float32)
    z0 = np.asarray(L0_attn_z0, np.float32)
    maug_full = np.concatenate([mem0[0], np.swapaxes(z0[0], 1, 2)], axis=2)

    in_maps = []
    for c in range(8):
        b, p = c // 4, c % 4
        xs = x[b].reshape(16, 16, 32, 1024)[:, 4 * p:4 * p + 4]
        m = dict(shared)
        m["xT"] = cont(xs.reshape(2048, 1024).T)
        m["betas"] = cont(betas[0, 4 * p:4 * p + 4, 0, :].reshape(1, 256))
        m["maug"] = cont(maug_full[4 * p:4 * p + 4])
        in_maps.append(m)

    if _NC_CACHE is None:
        _NC_CACHE = build_program()
    res = run_bass_kernel_spmd(_NC_CACHE, in_maps, core_ids=list(range(8)))

    out = np.empty((2, 8192, 1024), np.float32)
    ov = out.reshape(2, 16, 16, 32, 1024)
    for c in range(8):
        b, p = c // 4, c % 4
        o = np.asarray(res.results[c]["outT"], np.float32)
        ov[b, :, 4 * p:4 * p + 4] = o.T.reshape(16, 4, 32, 1024)
    return out


# version 09
# speedup vs baseline: 1.0017x; 1.0017x over previous
import contextlib
import numpy as np
import ml_dtypes
import concourse.bass as bass
import concourse.mybir as mybir
from concourse.bass_utils import run_bass_kernel_spmd

F32 = mybir.dt.float32
BF16 = mybir.dt.bfloat16
AF = mybir.ActivationFunctionType
ALU = mybir.AluOpType
AX = mybir.AxisListType

ENGS = ("sync", "gpsimd", "scalar", "vector", "tensor")

D = 1024
TL = 2048
NSEG = 16
NA = 4
NE = 16
NE_RUN = 16
STOP_AFTER = None
A_RUN = 4
SEG_RUN = 16
SEG_OPS = 99
DBG_SEG0 = False
EI = 512
NEGC = -1.0e30


def _masks_np():
    out = np.zeros((4, 128, 512), np.float32)
    for beta in range(4):
        hl, ccg = beta // 2, beta % 2
        for ccl in range(4):
            for j in range(32):
                r = ccl * 32 + j
                m = 16 * j + 2 * (4 * ccg + ccl) + hl
                for hlp in range(2):
                    for jp in range(32):
                        for ccp in range(8):
                            c = hlp * 256 + (jp // 16) * 128 + (jp % 16) * 8 + ccp
                            mp = 16 * jp + 2 * ccp + hlp
                            if mp > m:
                                out[beta, r, c] = NEGC
    return out.astype(ml_dtypes.bfloat16)


class Em:
    def __init__(self, sems):
        self.sems = sems
        self.cnt = {k: 0 for k in sems}
        self.ops = {e: [] for e in ENGS}
        self.scope = None

    def emit(self, eng, f, waits=None, inc=None, k=1):
        sems = self.sems
        wl = tuple((s, v) for (s, v) in (waits or ()) if v > 0)

        def g(engobj, wl=wl, inc=inc, k=k, f=f):
            for s, v in wl:
                engobj.wait_ge(sems[s], v)
            ins = f(engobj)
            if inc is not None:
                ins.then_inc(sems[inc], k)
        self.ops[eng].append((self.scope, g))
        if inc is not None:
            self.cnt[inc] += k
        return self.cnt[inc] if inc is not None else None

    def at(self, s):
        return self.cnt[s]


def build_program():
    nc = bass.Bass("TRN2", target_bir_lowering=False,
                   detect_race_conditions=False)

    xT = nc.dram_tensor("xT", [D, TL], F32, kind="ExternalInput").ap()
    wq_d = nc.dram_tensor("wq", [D, D], F32, kind="ExternalInput").ap()
    wk_d = nc.dram_tensor("wk", [D, D], F32, kind="ExternalInput").ap()
    wv_d = nc.dram_tensor("wv", [D, D], F32, kind="ExternalInput").ap()
    wo_d = nc.dram_tensor("wo", [D, D], F32, kind="ExternalInput").ap()
    gwT_d = nc.dram_tensor("gwT", [D, NE], F32, kind="ExternalInput").ap()
    wg_d = nc.dram_tensor("wg", [NE, D, EI], F32, kind="ExternalInput").ap()
    wu_d = nc.dram_tensor("wu", [NE, D, EI], F32, kind="ExternalInput").ap()
    wd_d = nc.dram_tensor("wd", [NE, EI, D], F32, kind="ExternalInput").ap()
    wsg_d = nc.dram_tensor("wsg", [D, D], F32, kind="ExternalInput").ap()
    wsu_d = nc.dram_tensor("wsu", [D, D], F32, kind="ExternalInput").ap()
    wsd_d = nc.dram_tensor("wsd", [D, D], F32, kind="ExternalInput").ap()
    betas_d = nc.dram_tensor("betas", [1, 256], F32, kind="ExternalInput").ap()
    maug_d = nc.dram_tensor("maug", [NA, 64, 65], F32, kind="ExternalInput").ap()
    outT = nc.dram_tensor("outT", [D, TL], F32, kind="ExternalOutput").ap()

    ident_d = nc.inline_tensor(np.eye(128, dtype=ml_dtypes.bfloat16), "c_ident").ap()
    identf_d = nc.inline_tensor(np.eye(16, dtype=np.float32), "c_identf").ap()
    ones_d = nc.inline_tensor(np.ones((128, 128), dtype=ml_dtypes.bfloat16), "c_ones").ap()
    masks_d = nc.inline_tensor(np.ascontiguousarray(_masks_np().reshape(4 * 128, 512)),
                               "c_masks").ap()
    sel_np = np.zeros((16, 16, 128), np.float32)
    for e_ in range(16):
        sel_np[e_, e_, :] = 1.0
    sel_d = nc.inline_tensor(np.ascontiguousarray(
        sel_np.transpose(1, 0, 2).reshape(16, 16 * 128)).astype(ml_dtypes.bfloat16),
        "c_sel").ap()

    ctx = contextlib.ExitStack()
    with ctx:
        def sbt(st, name, shape, dt):
            return st.enter_context(nc.sbuf_tensor(name, shape, dt))

        sems = {}
        for s_ in ("dS", "dG", "PE", "V", "SC", "GP", "dO"):
            sems[s_] = ctx.enter_context(nc.semaphore(s_))
        em = Em(sems)

        ident = sbt(ctx, "ident", [128, 128], BF16)
        identf = sbt(ctx, "identf", [16, 16], F32)
        ones_b = sbt(ctx, "ones_b", [128, 128], BF16)

        betas_sb = sbt(ctx, "betas_sb", [1, 256], F32)
        gw_b = sbt(ctx, "gw_b", [128, 8 * NE], BF16)
        xn = sbt(ctx, "xn", [128, 8 * TL], BF16)
        att_out = sbt(ctx, "att_out", [128, 8 * TL], BF16)
        gate_row = sbt(ctx, "gate_row", [1, 256], F32)
        onemg_row = sbt(ctx, "onemg_row", [1, 256], F32)
        onemg_b = sbt(ctx, "onemg_b", [1, 256], BF16)
        gam_row = sbt(ctx, "gam_row", [1, 256], F32)
        gam65_b = sbt(ctx, "gam65_b", [1, NA * 65], BF16)

        p2w = ctx.enter_context(contextlib.ExitStack())
        masks_b = [sbt(p2w, f"mask{i}", [128, 512], BF16) for i in range(4)]
        wq_b = sbt(p2w, "wq_b", [128, 8 * 512], BF16)
        wk_b = sbt(p2w, "wk_b", [128, 8 * 512], BF16)
        wvo_b = sbt(p2w, "wvo_b", [128, 8 * 512], BF16)

        def wload(e, buf, src_d, h):
            return e.dma_start(
                out=buf[:].rearrange("p (r c) -> p r c", r=8),
                in_=src_d.rearrange("(r p) c -> p r c", p=128)[
                    :, :, h * 512:(h + 1) * 512])

        em.scope = "p0"
        for dst, src in ((ident, ident_d), (identf, identf_d), (ones_b, ones_d),
                         (betas_sb, betas_d)):
            em.emit("sync", lambda e, dst=dst, src=src: e.dma_start(out=dst[:], in_=src),
                    inc="dS", k=16)
        for i in range(4):
            em.emit("sync", lambda e, i=i: e.dma_start(
                out=masks_b[i][:], in_=masks_d[i * 128:(i + 1) * 128, :]), inc="dS", k=16)
        d_consts = em.at("dS")

        em.emit("gpsimd", lambda e: e.dma_start(
            out=gw_b[:].rearrange("p (r c) -> p r c", r=8),
            in_=gwT_d.rearrange("(r p) c -> p r c", p=128)), inc="dG", k=16)
        em.emit("gpsimd", lambda e: wload(e, wq_b, wq_d, 0), inc="dG", k=16)
        d_wq_h = [em.at("dG"), 0]
        em.emit("gpsimd", lambda e: wload(e, wk_b, wk_d, 0), inc="dG", k=16)
        d_wk_h = [em.at("dG"), 0]

        em.emit("scalar", lambda e: nc.scalar.activation(gate_row[:], betas_sb[:], AF.Sigmoid),
                waits=[("dS", d_consts)], inc="SC")
        sc_gate = em.at("SC")
        em.emit("vector", lambda e: nc.vector.tensor_scalar(
            onemg_row[:], gate_row[:], 1.0, -1.0, ALU.subtract, ALU.mult),
            waits=[("SC", sc_gate)], inc="V")
        em.emit("vector", lambda e: nc.vector.reciprocal(gam_row[:], onemg_row[:]), inc="V")
        em.emit("vector", lambda e: nc.vector.tensor_mul(gam_row[:], gate_row[:], gam_row[:]), inc="V")
        em.emit("vector", lambda e: nc.vector.tensor_copy(onemg_b[:], onemg_row[:]), inc="V")
        for a in range(NA):
            em.emit("vector", lambda e, a=a: nc.vector.tensor_copy(
                gam65_b[:, a * 65:a * 65 + 64], gam_row[:, a * 64:(a + 1) * 64]), inc="V")
        em.emit("vector", lambda e: nc.vector.memset(
            gam65_b[:].rearrange("p (a r) -> p a r", a=NA)[:, :, 64:65], 1.0), inc="V")
        v_gate = em.at("V")

        em.scope = "ln"
        with contextlib.ExitStack() as ph:
            xf = [sbt(ph, f"xf{i}", [128, TL], F32) for i in range(2)]
            xbf = sbt(ph, "xbf", [128, 8 * TL], BF16)
            x2 = sbt(ph, "x2", [128, TL], BF16)
            msb = sbt(ph, "msb", [128, TL], F32)
            vsb = sbt(ph, "vsb", [128, TL], F32)
            rstd = sbt(ph, "rstd", [128, TL], F32)
            ps_sx = ph.enter_context(nc.psum_tensor("ps_sx", [128, TL], F32))
            ps_sx2 = ph.enter_context(nc.psum_tensor("ps_sx2", [128, TL], F32))

            v_free = [0, 0]
            pe_x2done = [0] * 8
            for kc in range(8):
                q, qs = (("sync", "dS") if kc % 2 == 0 else ("gpsimd", "dG"))
                em.emit(q, lambda e, kc=kc: e.dma_start(
                    out=xf[kc % 2][:], in_=xT[kc * 128:(kc + 1) * 128, :]),
                    waits=[("V", v_free[kc % 2])], inc=qs, k=16)
                dld = em.at(qs)
                em.emit("vector", lambda e, kc=kc: nc.vector.tensor_copy(
                    xbf[:, kc * TL:(kc + 1) * TL], xf[kc % 2][:]),
                    waits=[(qs, dld)], inc="V")
                v_cast = em.at("V")
                v_free[kc % 2] = v_cast
                em.emit("scalar", lambda e, kc=kc: nc.scalar.activation(
                    x2[:], xbf[:, kc * TL:(kc + 1) * TL], AF.Square),
                    waits=[("V", v_cast)] + ([("PE", pe_x2done[kc - 1])] if kc >= 1 else []),
                    inc="SC")
                sc_sq = em.at("SC")

                def lnmm(e, kc=kc):
                    for nt in range(4):
                        nc.tensor.matmul(
                            ps_sx[:, nt * 512:(nt + 1) * 512], ones_b[:],
                            xbf[:, kc * TL + nt * 512: kc * TL + (nt + 1) * 512],
                            start=(kc == 0), stop=(kc == 7), skip_group_check=True)
                    for nt in range(4):
                        ins = nc.tensor.matmul(
                            ps_sx2[:, nt * 512:(nt + 1) * 512], ones_b[:],
                            x2[:, nt * 512:(nt + 1) * 512],
                            start=(kc == 0), stop=(kc == 7), skip_group_check=True)
                    return ins
                em.emit("tensor", lnmm, waits=[("V", v_cast), ("SC", sc_sq)], inc="PE")
                pe_x2done[kc] = em.at("PE")
            pe_ln = em.at("PE")

            em.emit("vector", lambda e: nc.vector.tensor_scalar(
                msb[:], ps_sx[:], 1.0 / 1024.0, None, ALU.mult),
                waits=[("PE", pe_ln)], inc="V")
            em.emit("vector", lambda e: nc.vector.tensor_mul(vsb[:], msb[:], msb[:]), inc="V")
            em.emit("vector", lambda e: nc.vector.scalar_tensor_tensor(
                vsb[:], ps_sx2[:], 1.0 / 1024.0, vsb[:], ALU.mult, ALU.subtract), inc="V")
            em.emit("vector", lambda e: nc.vector.tensor_scalar(
                vsb[:], vsb[:], 1e-5, None, ALU.add), inc="V")
            em.emit("vector", lambda e: nc.vector.reciprocal(vsb[:], vsb[:]), inc="V")
            v_var = em.at("V")
            em.emit("scalar", lambda e: nc.scalar.activation(rstd[:], vsb[:], AF.Sqrt),
                    waits=[("V", v_var)], inc="SC")
            sc_rstd = em.at("SC")
            for kc in range(8):
                em.emit("vector", lambda e, kc=kc: nc.vector.tensor_sub(
                    vsb[:], xbf[:, kc * TL:(kc + 1) * TL], msb[:]),
                    waits=([("SC", sc_rstd)] if kc == 0 else None), inc="V")
                em.emit("vector", lambda e, kc=kc: nc.vector.tensor_mul(
                    xn[:, kc * TL:(kc + 1) * TL], vsb[:], rstd[:]), inc="V")
            v_xn = em.at("V")

        if STOP_AFTER == 'ln':
            _finale(nc, em, xn, outT, v_xn)
            _replay(nc, em)
            return nc

        em.scope = "attn_pre"
        with contextlib.ExitStack() as ph:
            Q2A = sbt(ph, "Q2A", [128, NSEG * 256], BF16)
            Q2B = sbt(ph, "Q2B", [128, NSEG * 256], BF16)
            K2 = sbt(ph, "K2", [128, NSEG * 256], BF16)
            K2X = sbt(ph, "K2X", [128, NSEG * 256], BF16)
            Vscr = sbt(ph, "Vscr", [128, NSEG * 256], BF16)
            VscrX = sbt(ph, "VscrX", [64, NSEG * 256], BF16)
            vgcol = sbt(ph, "vgcol", [128, 1], F32)
            att_bf = sbt(ph, "att_bf", [128, 8 * 512], BF16)
            A_scr = sbt(ph, "A_scr", [128, 8 * 512], BF16)
            P_sb = sbt(ph, "P_sb", [128, 2048], BF16)
            PT_sb = sbt(ph, "PT_sb", [128, 2048], BF16)
            sq_scr = sbt(ph, "sq_scr", [128, 256], BF16)
            skE = sbt(ph, "skE", [64, 256], BF16)
            skO = sbt(ph, "skO", [64, 256], BF16)
            elu_t = sbt(ph, "elu_t", [128, 256], F32)
            skT_sb = sbt(ph, "skT_sb", [128, 512], BF16)
            M_master = sbt(ph, "M_master", [128, NA * 65], F32)
            M_dupA = sbt(ph, "M_dupA", [128, 130], BF16)
            M_dupB = sbt(ph, "M_dupB", [128, 130], BF16)
            gamrep = sbt(ph, "gamrep", [128, NA * 65], BF16)
            maug2 = sbt(ph, "maug2", [128, NA * 65], F32)
            rdl_a = sbt(ph, "rdl_a", [1, 512], BF16)
            rdl_b = sbt(ph, "rdl_b", [1, 512], BF16)
            rr_sb = sbt(ph, "rr_sb", [128, 512], BF16)
            rl_sb = sbt(ph, "rl_sb", [128, 512], BF16)
            t1_sb = sbt(ph, "t1_sb", [128, 512], BF16)
            t2_sb = sbt(ph, "t2_sb", [128, 512], BF16)

            for a in range(NA):
                for hb in range(2):
                    em.emit("sync", lambda e, a=a, hb=hb: e.dma_start(
                        out=maug2[hb * 64:(hb + 1) * 64, a * 65:(a + 1) * 65],
                        in_=maug_d[a]),
                        waits=([("V", v_xn)] if (a == 0 and hb == 0) else None),
                        inc="dS", k=16)
            d_maug = em.at("dS")

            em.emit("vector", lambda e: nc.vector.memset(Q2A[64:128, :], 0.0),
                    waits=[("V", v_xn)], inc="V")
            em.emit("vector", lambda e: nc.vector.memset(Q2B[0:64, :], 0.0), inc="V")
            em.emit("vector", lambda e: nc.vector.memset(M_dupA[64:128, :], 0.0), inc="V")
            em.emit("vector", lambda e: nc.vector.memset(M_dupB[0:64, :], 0.0), inc="V")
            v_zinit = em.at("V")

            with nc.psum_tensor("ps_small", [128, 512], F32) as ps_small:
                def g65(e):
                    for a in range(NA):
                        ins = nc.tensor.matmul(
                            ps_small[:, a * 65:(a + 1) * 65], ones_b[0:1, :],
                            gam65_b[:, a * 65:(a + 1) * 65],
                            start=(a == 0), stop=(a == NA - 1), skip_group_check=True)
                    return ins
                em.emit("tensor", g65, waits=[("V", em.at("V"))], inc="PE")
                pe_g65 = em.at("PE")
                em.emit("vector", lambda e: nc.vector.tensor_copy(
                    gamrep[:], ps_small[:, 0:NA * 65]),
                    waits=[("PE", pe_g65)], inc="V")
                v_g65 = em.at("V")
            with nc.psum_tensor("ps_small2", [128, 512], F32) as ps_small2:
                def m0rep(e):
                    for a in range(NA):
                        ins = nc.tensor.matmul(
                            ps_small2[:, a * 64:(a + 1) * 64], ones_b[0:1, :],
                            onemg_b[:, a * 64:(a + 1) * 64],
                            start=(a == 0), stop=(a == NA - 1), skip_group_check=True)
                    return ins
                em.emit("tensor", m0rep, waits=[("V", max(v_gate, v_g65))], inc="PE")
                pe_m0 = em.at("PE")
                for a in range(NA):
                    em.emit("vector", lambda e, a=a: nc.vector.tensor_tensor(
                        M_master[:, a * 65:a * 65 + 64], maug2[:, a * 65:a * 65 + 64],
                        ps_small2[:, a * 64:(a + 1) * 64], ALU.mult),
                        waits=[("PE", pe_m0), ("dS", d_maug)], inc="V")
                    em.emit("vector", lambda e, a=a: nc.vector.tensor_copy(
                        M_master[:, a * 65 + 64:a * 65 + 65],
                        maug2[:, a * 65 + 64:a * 65 + 65]), inc="V")
                v_minit = em.at("V")

            pe_prev_a = 0
            v_mup = v_minit

            for a in range(A_RUN):
                em.scope = "proj"
                em.emit("gpsimd", lambda e: wload(e, wvo_b, wv_d, 0),
                        waits=[("PE", pe_prev_a)], inc="dG", k=16)
                d_wv_h = [em.at("dG"), 0]

                pha = contextlib.ExitStack()
                xn_a = pha.enter_context(nc.sbuf_tensor(f"xn_a{a}", [128, 8 * 512], BF16))
                for kc in range(8):
                    src = xn[:].rearrange("p (k s ar j) -> p k s ar j", k=8, s=16, ar=4)[
                        :, kc:kc + 1, :, a:a + 1, :].squeeze(1).squeeze(2)
                    em.emit("vector", lambda e, kc=kc, src=src: nc.vector.tensor_copy(
                        xn_a[:, kc * 512:(kc + 1) * 512].rearrange("p (s j) -> p s j", s=16),
                        src),
                        waits=([("V", v_xn), ("PE", pe_prev_a)] if kc == 0 else None),
                        inc="V")
                v_xna = em.at("V")

                with nc.psum_tensor(f"ps_vgc{a}", [64, 8], F32) as ps_vgc:
                    em.emit("tensor", lambda e, a=a: nc.tensor.transpose(
                        ps_vgc[0:64, 0:1], onemg_row[0:1, a * 64:(a + 1) * 64],
                        identf[0:1, 0:1]),
                        waits=[("V", em.at("V"))], inc="PE")
                    pe_vg = em.at("PE")
                    em.emit("vector", lambda e: nc.vector.tensor_copy(
                        vgcol[0:64, :], ps_vgc[:, 0:1]), waits=[("PE", pe_vg)], inc="V")
                    em.emit("vector", lambda e: e.drain() or e.engine_nop(), inc="V")
                    v_vg0 = em.at("V")
                    em.emit("gpsimd", lambda e: e.dma_start(
                        out=vgcol[64:128, :], in_=vgcol[0:64, :]),
                        waits=[("V", v_vg0)], inc="dG", k=16)
                    d_vg = em.at("dG")
                    v_vg = v_vg0

                with nc.psum_tensor(f"ps_proj{a}", [128, 1024], F32) as ps_proj:
                    v_evac_half = [v_vg, v_vg]
                    sc_evac_half = [0, 0]
                    dmap = {0: d_wq_h, 1: d_wk_h, 2: d_wv_h}
                    d_wo_h = [0, 0]
                    for wh in range(2):
                        for ti, wb in enumerate((wq_b, wk_b, wvo_b)):
                            for ccl in range(4):
                                cc = wh * 4 + ccl
                                half = ccl % 2

                                def projmm(e, wb=wb, ccl=ccl, half=half):
                                    for kc in range(8):
                                        ins = nc.tensor.matmul(
                                            ps_proj[:, half * 512:(half + 1) * 512],
                                            wb[:, kc * 512 + ccl * 128: kc * 512 + (ccl + 1) * 128],
                                            xn_a[:, kc * 512:(kc + 1) * 512],
                                            start=(kc == 0), stop=(kc == 7))
                                    return ins
                                wl = [("V", max(v_xna, v_evac_half[half]))]
                                if sc_evac_half[half]:
                                    wl.append(("SC", sc_evac_half[half]))
                                if ccl == 0:
                                    wl.append(("dG", dmap[ti][wh]))
                                em.emit("tensor", projmm, waits=wl, inc="PE")
                                pe_p = em.at("PE")
                                if ccl == 3:
                                    if wh == 0:
                                        em.emit("gpsimd", lambda e, wb=wb, ti=ti: wload(
                                            e, wb, (wq_d, wk_d, wv_d)[ti], 1),
                                            waits=[("PE", pe_p)], inc="dG", k=16)
                                        dmap[ti][1] = em.at("dG")
                                    elif ti < 2 and a + 1 < A_RUN:
                                        em.emit("gpsimd", lambda e, wb=wb, ti=ti: wload(
                                            e, wb, (wq_d, wk_d)[ti], 0),
                                            waits=[("PE", pe_p)], inc="dG", k=16)
                                        dmap[ti][0] = em.at("dG")
                                    elif ti == 2:
                                        em.emit("gpsimd", lambda e: wload(e, wvo_b, wo_d, 0),
                                                waits=[("PE", pe_p)], inc="dG", k=16)
                                        d_wo_h[0] = em.at("dG")
                                for hl in range(2):
                                    if ti == 0:
                                        def scram(e, cc=cc, hl=hl, half=half):
                                            qdst = Q2A if hl == 0 else Q2B
                                            dst = qdst[hl * 64:(hl + 1) * 64, :].rearrange(
                                                "p (s c8 j) -> p s c8 j", s=16, c8=8, j=32)[
                                                :, :, cc:cc + 1, :].squeeze(2)
                                            src = ps_proj[hl * 64:(hl + 1) * 64,
                                                          half * 512:(half + 1) * 512].rearrange(
                                                "p (s j) -> p s j", s=16)
                                            return nc.scalar.activation(dst, src, AF.Copy)
                                    elif ti == 1:
                                        def scram(e, cc=cc, hl=hl, half=half):
                                            dst = K2[hl * 64:(hl + 1) * 64, :].rearrange(
                                                "p (s j c8) -> p s j c8", s=16, j=32, c8=8)[
                                                :, :, :, cc:cc + 1].squeeze(3)
                                            src = ps_proj[hl * 64:(hl + 1) * 64,
                                                          half * 512:(half + 1) * 512].rearrange(
                                                "p (s j) -> p s j", s=16)
                                            return nc.vector.tensor_copy(dst, src)
                                    else:
                                        def scram(e, cc=cc, hl=hl, half=half):
                                            dst = Vscr[hl * 64:(hl + 1) * 64, :].rearrange(
                                                "p (s j c8) -> p s j c8", s=16, j=32, c8=8)[
                                                :, :, :, cc:cc + 1].squeeze(3)
                                            src = ps_proj[hl * 64:(hl + 1) * 64,
                                                          half * 512:(half + 1) * 512].rearrange(
                                                "p (s j) -> p s j", s=16)
                                            return nc.vector.tensor_scalar(
                                                dst, src, vgcol[hl * 64:(hl + 1) * 64, 0:1],
                                                None, ALU.mult)
                                    wl2 = [("PE", pe_p)]
                                    if ti == 2:
                                        wl2.append(("dG", d_vg))
                                    if ti == 0:
                                        em.emit("scalar", scram, waits=wl2, inc="SC")
                                    else:
                                        em.emit("vector", scram, waits=wl2, inc="V")
                                if ti == 0:
                                    sc_evac_half[half] = em.at("SC")
                                else:
                                    v_evac_half[half] = em.at("V")
                    v_qk = em.at("V")
                    sc_qk = em.at("SC")
                pha.close()

                pe_proj_done = em.at("PE")
                em.emit("gpsimd", lambda e: e.dma_start(out=K2X[0:64, :], in_=K2[64:128, :]),
                        waits=[("V", v_qk), ("PE", pe_proj_done)], inc="dG", k=16)
                em.emit("gpsimd", lambda e: e.dma_start(out=K2X[64:128, :], in_=K2[0:64, :]),
                        inc="dG", k=16)
                em.emit("gpsimd", lambda e: e.dma_start(out=VscrX[0:64, :], in_=Vscr[64:128, :]),
                        inc="dG", k=16)
                d_k2x = em.at("dG")

                em.scope = "rec"
                with contextlib.ExitStack() as rec:
                    ps_S = rec.enter_context(nc.psum_tensor(f"ps_S{a}", [128, 1024], F32))
                    ps_pt = rec.enter_context(nc.psum_tensor(f"ps_pt{a}", [128, 2048], BF16))
                    ps_num = rec.enter_context(nc.psum_tensor(f"ps_num{a}", [128, 512], F32))
                    ps_dot = rec.enter_context(nc.psum_tensor(f"ps_dot{a}", [128, 512], F32))
                    ps_msc = rec.enter_context(nc.psum_tensor(f"ps_msc{a}", [128, 512], F32))
                    ps_rr = rec.enter_context(nc.psum_tensor(f"ps_rr{a}", [128, 512], F32))

                    v_comb = em.at("V")
                    v_e1_prev = 0
                    v_rr_prev = 0
                    v_skt_prev = 0
                    sc_p_prev = sc_qk
                    sc_pw_seg = {}

                    def emit_swave(s):
                        nonlocal sc_p_prev
                        sc_pw_seg[s] = [0, 0]
                        for wave in range(2):
                            def swave(e, s=s, wave=wave):
                                for beta in (wave * 2, wave * 2 + 1):
                                    hl, ccg = beta // 2, beta % 2
                                    bk = (beta - wave * 2) * 512
                                    qsrc = Q2A if hl == 0 else Q2B
                                    for hlp in range(2):
                                        ksrc = K2 if hlp == hl else K2X
                                        nc.tensor.matmul(
                                            ps_S[:, bk + hlp * 256: bk + (hlp + 1) * 256],
                                            qsrc[:, s * 256 + ccg * 128: s * 256 + (ccg + 1) * 128],
                                            ksrc[:, s * 256:(s + 1) * 256],
                                            start=(hlp == 0), stop=False,
                                            skip_group_check=True)
                                    ins = nc.tensor.matmul(
                                        ps_S[:, bk:bk + 512], ident[:], masks_b[beta][:],
                                        start=False, stop=True, skip_group_check=True)
                                return ins
                            em.emit("tensor", swave,
                                    waits=[("SC", sc_p_prev), ("dG", d_k2x)], inc="PE")
                            pe_w = em.at("PE")

                            def pexp(e, wave=wave):
                                for bi in range(2):
                                    beta = wave * 2 + bi
                                    ins = nc.scalar.activation(
                                        P_sb[:, beta * 512:(beta + 1) * 512],
                                        ps_S[:, bi * 512:(bi + 1) * 512], AF.Exp,
                                        scale=0.125)
                                return ins
                            em.emit("scalar", pexp,
                                    waits=[("PE", pe_w)], inc="SC")
                            sc_p_prev = em.at("SC")
                            sc_pw_seg[s][wave] = sc_p_prev
                    sc_el_prev = sc_qk
                    pe_pt_prev = 0
                    pe_numden_prev = 0
                    pe_dm_prev = 0
                    pe_rep_prev = 0
                    for s in range(SEG_RUN):
                        em.emit("vector", lambda e, s=s: nc.vector.tensor_scalar_min(
                            elu_t[0:64, :], Q2A[0:64, s * 256:(s + 1) * 256], 0.0),
                            waits=[("V", max(v_qk, v_zinit)), ("SC", sc_el_prev),
                                   ("PE", pe_numden_prev)], inc="V")
                        em.emit("vector", lambda e, s=s: nc.vector.tensor_scalar_min(
                            elu_t[64:128, :], Q2B[64:128, s * 256:(s + 1) * 256], 0.0),
                            inc="V")
                        v_el1 = em.at("V")
                        em.emit("scalar", lambda e: nc.scalar.activation(
                            elu_t[:], elu_t[:], AF.Exp),
                            waits=[("V", v_el1)], inc="SC")
                        sc_el = em.at("SC")
                        em.emit("vector", lambda e, s=s: nc.vector.scalar_tensor_tensor(
                            sq_scr[0:64, :], Q2A[0:64, s * 256:(s + 1) * 256], 0.0,
                            elu_t[0:64, :], ALU.max, ALU.add),
                            waits=[("SC", sc_el)], inc="V")
                        em.emit("vector", lambda e, s=s: nc.vector.scalar_tensor_tensor(
                            sq_scr[64:128, :], Q2B[64:128, s * 256:(s + 1) * 256], 0.0,
                            elu_t[64:128, :], ALU.max, ALU.add), inc="V")
                        em.emit("vector", lambda e, s=s: nc.vector.tensor_scalar_min(
                            elu_t[0:64, :], K2[0:64, s * 256:(s + 1) * 256], 0.0),
                            waits=[("PE", pe_pt_prev)], inc="V")
                        v_el2 = em.at("V")
                        em.emit("scalar", lambda e: nc.scalar.activation(
                            elu_t[0:64, :], elu_t[0:64, :], AF.Exp),
                            waits=[("V", v_el2)], inc="SC")
                        sc_el = em.at("SC")
                        em.emit("vector", lambda e, s=s: nc.vector.scalar_tensor_tensor(
                            skE[:], K2[0:64, s * 256:(s + 1) * 256], 0.0,
                            elu_t[0:64, :], ALU.max, ALU.add),
                            waits=[("SC", sc_el)], inc="V")
                        em.emit("vector", lambda e, s=s: nc.vector.tensor_scalar_min(
                            elu_t[0:64, :], K2X[0:64, s * 256:(s + 1) * 256], 0.0),
                            waits=[("dG", d_k2x)], inc="V")
                        v_el3 = em.at("V")
                        em.emit("scalar", lambda e: nc.scalar.activation(
                            elu_t[0:64, :], elu_t[0:64, :], AF.Exp),
                            waits=[("V", v_el3)], inc="SC")
                        sc_el = em.at("SC")
                        sc_el_prev = sc_el
                        em.emit("vector", lambda e, s=s: nc.vector.scalar_tensor_tensor(
                            skO[:], K2X[0:64, s * 256:(s + 1) * 256], 0.0,
                            elu_t[0:64, :], ALU.max, ALU.add),
                            waits=[("SC", sc_el)], inc="V")
                        v_sqsk = em.at("V")
                        if SEG_OPS <= 2:
                            continue

                        def mdup(e, a=a):
                            for hl, MD in ((0, M_dupA), (1, M_dupB)):
                                r0, r1 = hl * 64, hl * 64 + 64
                                for co in (0, 64):
                                    nc.vector.tensor_tensor(
                                        MD[r0:r1, co:co + 64],
                                        M_master[r0:r1, a * 65:a * 65 + 64],
                                        gamrep[r0:r1, a * 65:a * 65 + 64], ALU.mult)
                                ins = nc.vector.tensor_copy(
                                    MD[r0:r1, 128:129],
                                    M_master[r0:r1, a * 65 + 64:a * 65 + 65])
                            return ins
                        em.emit("vector", mdup,
                                waits=[("V", max(v_g65, v_mup)), ("PE", pe_numden_prev)],
                                inc="V")
                        v_mbf = em.at("V")
                        if SEG_OPS <= 3:
                            continue

                        if SEG_OPS <= 4:
                            continue

                        if s == 0 and SEG_OPS > 5:
                            emit_swave(0)
                        if SEG_OPS <= 5:
                            continue

                        for wv in range(2):
                            def ptrans(e, wv=wv):
                                for beta in (wv * 2, wv * 2 + 1):
                                    for kap in range(4):
                                        ins = nc.tensor.transpose(
                                            ps_pt[:, kap * 512 + beta * 128: kap * 512 + (beta + 1) * 128],
                                            P_sb[:, beta * 512 + kap * 128: beta * 512 + (kap + 1) * 128],
                                            ident[:])
                                return ins
                            em.emit("tensor", ptrans,
                                    waits=[("SC", sc_pw_seg[s][wv]), ("V", v_skt_prev)],
                                    inc="PE")
                        pe_pt = em.at("PE")
                        for kap in range(4):
                            em.emit("scalar", lambda e, kap=kap: nc.scalar.activation(
                                PT_sb[:, kap * 512:(kap + 1) * 512],
                                ps_pt[:, kap * 512:(kap + 1) * 512], AF.Copy),
                                waits=([("PE", pe_pt)] if kap == 0 else None), inc="SC")
                            if kap == 0:
                                sc_ptev_k0 = em.at("SC")
                        sc_ptev = em.at("SC")
                        if SEG_OPS <= 7:
                            continue

                        def sktr(e, s=s):
                            for kap in range(4):
                                hlp, jh = kap // 2, kap % 2
                                sksrc = skE if hlp == 0 else skO
                                nc.tensor.transpose(
                                    ps_pt[:, kap * 64:(kap + 1) * 64],
                                    sksrc[0:64, jh * 128:(jh + 1) * 128],
                                    ident[0:64, 0:64])
                            for kap in range(4):
                                hlp, jh = kap // 2, kap % 2
                                ins = nc.tensor.transpose(
                                    ps_pt[:, 256 + kap * 64:256 + (kap + 1) * 64],
                                    (Vscr if hlp == 0 else VscrX)[
                                        0:64, s * 256 + jh * 128: s * 256 + (jh + 1) * 128],
                                    ident[0:64, 0:64])
                            return ins
                        em.emit("tensor", sktr,
                                waits=[("SC", sc_ptev_k0), ("V", max(v_sqsk, v_qk)),
                                       ("dG", d_k2x)], inc="PE")
                        pe_skt = em.at("PE")
                        pe_pt_prev = pe_skt
                        em.emit("vector", lambda e: nc.vector.tensor_copy(
                            skT_sb[:], ps_pt[:, 0:512]),
                            waits=[("PE", pe_skt)], inc="V")
                        v_skt = em.at("V")
                        v_skt_prev = v_skt
                        if SEG_OPS <= 9:
                            continue

                        def dmm(e, s=s):
                            for ob in range(2):
                                for kap in range(4):
                                    nc.tensor.matmul(
                                        ps_rr[ob * 64:(ob + 1) * 64, 0:64],
                                        skT_sb[:, kap * 64:(kap + 1) * 64],
                                        skT_sb[:, 256 + kap * 64:256 + (kap + 1) * 64],
                                        start=(kap == 0), stop=False,
                                        skip_group_check=True)
                                for kap in range(4):
                                    ins = nc.tensor.matmul(
                                        ps_rr[ob * 64:(ob + 1) * 64, 64:65],
                                        skT_sb[:, kap * 64:(kap + 1) * 64],
                                        ones_b[:, 0:1],
                                        start=False, stop=(kap == 3),
                                        skip_group_check=True)
                            return ins
                        em.emit("tensor", dmm,
                                waits=[("V", max(v_skt, v_rr_prev))], inc="PE")
                        pe_dm = em.at("PE")
                        pe_dm_prev = pe_dm
                        em.emit("vector", lambda e, a=a: nc.vector.tensor_tensor(
                            M_master[:, a * 65:(a + 1) * 65], M_master[:, a * 65:(a + 1) * 65],
                            ps_rr[:, 0:65], ALU.add),
                            waits=[("PE", pe_dm), ("V", v_mbf)], inc="V")
                        v_mup = em.at("V")

                        def numden(e, s=s):
                            for hl, MD in ((0, M_dupA), (1, M_dupB)):
                                nc.tensor.matmul(
                                    ps_num[:, hl * 256:(hl + 1) * 256],
                                    MD[:, 0:128], sq_scr[:, :],
                                    start=True, stop=True, skip_group_check=True)
                            for hl, MD in ((0, M_dupA), (1, M_dupB)):
                                ins = nc.tensor.matmul(
                                    ps_msc[0:1, hl * 256:(hl + 1) * 256],
                                    MD[:, 128:129], sq_scr[:, :],
                                    start=True, stop=True, skip_group_check=True)
                            return ins
                        em.emit("tensor", numden,
                                waits=[("V", max(v_mbf, v_sqsk, v_e1_prev))], inc="PE")
                        pe_num = em.at("PE")
                        pe_numden_prev = pe_num

                        def lammm(e, s=s):
                            for kap in range(4):
                                ins = nc.tensor.matmul(
                                    ps_rr[0:1, :], ones_b[:, 0:1],
                                    PT_sb[:, kap * 512:(kap + 1) * 512],
                                    start=(kap == 0), stop=(kap == 3),
                                    skip_group_check=True)
                            return ins
                        em.emit("tensor", lammm,
                                waits=[("V", max(v_comb, v_e1_prev, v_skt, v_mup)),
                                       ("SC", sc_ptev)],
                                inc="PE")
                        pe_lam = em.at("PE")

                        def dotmm(e, s=s):
                            for ob in range(2):
                                for kap in range(4):
                                    ins = nc.tensor.matmul(
                                        ps_dot[ob * 64:(ob + 1) * 64, :],
                                        skT_sb[:, 256 + kap * 64:256 + (kap + 1) * 64],
                                        PT_sb[:, kap * 512:(kap + 1) * 512],
                                        start=(kap == 0), stop=(kap == 3),
                                        skip_group_check=True)
                            return ins
                        em.emit("tensor", dotmm, inc="PE")
                        pe_dot = em.at("PE")
                        if s + 1 < SEG_RUN:
                            emit_swave(s + 1)
                        if SEG_OPS <= 10:
                            continue

                        def recips_a(e):
                            with nc.allow_low_precision(reason="bf16 matmul operand"):
                                return nc.vector.reciprocal(rdl_a[:], ps_msc[0:1, :])
                        def recips_b(e):
                            with nc.allow_low_precision(reason="bf16 matmul operand"):
                                return nc.vector.reciprocal(rdl_b[:], ps_rr[0:1, :])
                        em.emit("vector", recips_a, waits=[("PE", pe_num)], inc="V")
                        em.emit("vector", recips_b, waits=[("PE", pe_lam)], inc="V")
                        v_rdl = em.at("V")
                        if SEG_OPS <= 11:
                            continue

                        def rrmm(e):
                            nc.tensor.matmul(ps_rr[:], ones_b[0:1, :], rdl_a[:],
                                             start=True, stop=True, skip_group_check=True)
                            return nc.tensor.matmul(ps_msc[:], ones_b[0:1, :], rdl_b[:],
                                                    start=True, stop=True,
                                                    skip_group_check=True)
                        em.emit("tensor", rrmm,
                                waits=[("V", v_rdl)], inc="PE")
                        pe_rep = em.at("PE")
                        pe_rep_prev = pe_rep
                        if SEG_OPS <= 12:
                            continue
                        em.emit("vector", lambda e: nc.vector.tensor_copy(
                            rr_sb[:], ps_rr[:]), waits=[("PE", pe_rep)], inc="V")
                        em.emit("vector", lambda e: nc.vector.tensor_copy(
                            rl_sb[:], ps_msc[:]), inc="V")
                        v_rr = em.at("V")
                        v_rr_prev = v_rr
                        em.emit("vector", lambda e: nc.vector.tensor_tensor(
                            t1_sb[:], ps_num[:], rr_sb[:], ALU.mult), inc="V")
                        v_e1 = em.at("V")
                        v_e1_prev = v_e1
                        em.emit("vector", lambda e: nc.vector.tensor_tensor(
                            t2_sb[:], ps_dot[:], rl_sb[:], ALU.mult), inc="V")
                        em.emit("vector", lambda e, s=s: nc.vector.tensor_tensor(
                            att_bf[:, (s % 8) * 512:(s % 8 + 1) * 512], t1_sb[:], t2_sb[:],
                            ALU.add), inc="V")
                        v_comb = em.at("V")
                        if DBG_SEG0 == 1 and a == 0 and s == 0:
                            dumps = [
                                (outT[0:128, 0:512], att_bf[:, 0:512]),
                                (outT[0:128, 512:1024], t1_sb[:]),
                                (outT[0:128, 1024:1536], t2_sb[:]),
                                (outT[0:128, 1536:2048], skT_sb[:]),
                                (outT[128:256, 0:512], rr_sb[:]),
                                (outT[128:256, 512:1024], rl_sb[:]),
                                (outT[128:256, 1024:1280], sq_scr[:]),
                                (outT[128:256, 1280:1792], P_sb[:, 0:512]),
                                (outT[256:384, :], PT_sb[:]),
                                (outT[384:512, 0:129], M_dupA[:, 0:129]),
                                (outT[384:512, 512:641], M_dupB[:, 0:129]),
                                (outT[384:512, 1024:1089], gamrep[:, 0:65]),
                            ]
                            for i, (dst, src) in enumerate(dumps):
                                em.emit("gpsimd", lambda e, dst=dst, src=src: e.dma_start(
                                    out=dst, in_=src),
                                    waits=([("V", v_comb)] if i == 0 else None),
                                    inc="dO", k=16)
                            em.emit("gpsimd", lambda e: e.dma_start(
                                out=outT[512:640, 0:65], in_=M_master[:, 0:65]),
                                inc="dO", k=16)
                            d_dump = em.at("dO")
                            em.emit("vector", lambda e: e.engine_nop(),
                                    waits=[("dO", d_dump)], inc="V")
                            break
                        if SEG_OPS <= 13:
                            continue

                        if s in (7, 15):
                            base = 0 if s == 7 else 256
                            for h in range(16):
                                hl = h % 2
                                cc = h // 2
                                ccg, ccl = cc // 4, cc % 4
                                moff = hl * 256 + ccg * 128 + ccl * 32

                                def unscr(e, h=h, hl=hl, base=base, moff=moff):
                                    ch = h // 2
                                    dst = A_scr[hl * 64:(hl + 1) * 64,
                                                ch * 512 + base: ch * 512 + base + 256
                                                ].rearrange("p (s j) -> p s j", s=8)
                                    src = att_bf[hl * 64:(hl + 1) * 64, :].rearrange(
                                        "p (s m) -> p s m", s=8)[:, :, moff:moff + 32]
                                    return nc.vector.tensor_copy(dst, src)
                                em.emit("vector", unscr, inc="V")
                            v_comb = em.at("V")
                    v_unscr = em.at("V")

                if DBG_SEG0 == 2 and a == 0:
                    dumps2 = [
                        (outT[0:128, :], att_bf[:, 0:2048]),
                        (outT[128:256, :], att_bf[:, 2048:4096]),
                        (outT[256:384, :], A_scr[:, 0:2048]),
                        (outT[384:512, :], A_scr[:, 2048:4096]),
                    ]
                    for i, (dst, src) in enumerate(dumps2):
                        em.emit("gpsimd", lambda e, dst=dst, src=src: e.dma_start(
                            out=dst, in_=src),
                            waits=([("V", v_unscr)] if i == 0 else None),
                            inc="dO", k=16)
                    d_dump2 = em.at("dO")
                    em.emit("vector", lambda e: e.engine_nop(),
                            waits=[("dO", d_dump2)], inc="V")

                em.scope = "wo"
                if DBG_SEG0:
                    continue
                with nc.psum_tensor(f"ps_wo{a}", [128, 1024], F32) as ps_wo:
                    v_woev = [v_unscr, v_unscr]
                    for oc in range(8):
                        half = oc % 2
                        wh = oc // 4
                        ocl = oc % 4

                        def womm(e, ocl=ocl, half=half):
                            for ch in range(8):
                                ins = nc.tensor.matmul(
                                    ps_wo[:, half * 512:(half + 1) * 512],
                                    wvo_b[:, ch * 512 + ocl * 128: ch * 512 + (ocl + 1) * 128],
                                    A_scr[:, ch * 512:(ch + 1) * 512],
                                    start=(ch == 0), stop=(ch == 7))
                            return ins
                        wl = [("V", max(v_unscr, v_woev[half]))]
                        if ocl == 0:
                            wl.append(("dG", d_wo_h[wh]))
                        em.emit("tensor", womm, waits=wl, inc="PE")
                        pe_wo = em.at("PE")
                        if oc == 3:
                            em.emit("gpsimd", lambda e: wload(e, wvo_b, wo_d, 1),
                                    waits=[("PE", pe_wo)], inc="dG", k=16)
                            d_wo_h[1] = em.at("dG")

                        def woev(e, oc=oc, half=half, a=a):
                            dst = att_out[:].rearrange(
                                "p (o s ar j) -> p o s ar j", o=8, s=16, ar=4)[
                                :, oc:oc + 1, :, a:a + 1, :].squeeze(1).squeeze(2)
                            src = ps_wo[:, half * 512:(half + 1) * 512].rearrange(
                                "p (s j) -> p s j", s=16)
                            return nc.vector.tensor_copy(dst, src)
                        em.emit("vector", woev, waits=[("PE", pe_wo)], inc="V")
                        v_woev[half] = em.at("V")
                pe_prev_a = em.at("PE")
            v_attn = em.at("V")
        p2w.close()

        if STOP_AFTER == 'attn':
            if not DBG_SEG0:
                _finale(nc, em, att_out, outT, v_attn)
            _replay(nc, em)
            return nc

        em.scope = "ffn"
        with contextlib.ExitStack() as ph:
            acc = sbt(ph, "acc", [128, 8 * TL], F32)
            combT = sbt(ph, "combT", [16, TL], BF16)

            with contextlib.ExitStack() as lgs:
                lg_sb = sbt(lgs, "lg_sb", [16, TL], F32)
                lgt = sbt(lgs, "lgt", [128, 256], F32)
                est = sbt(lgs, "est", [128, 256], F32)
                m1s = sbt(lgs, "m1s", [128, 16], F32)
                nm1 = sbt(lgs, "nm1", [128, 16], F32)
                ssum = sbt(lgs, "ssum", [128, 16], F32)
                rs = sbt(lgs, "rs", [128, 16], F32)
                eqm = sbt(lgs, "eqm", [128, 256], F32)
                m2s = sbt(lgs, "m2s", [128, 16], F32)
                keep = sbt(lgs, "keep", [128, 256], F32)
                comb = sbt(lgs, "comb", [128, 256], F32)
                comb_b = sbt(lgs, "comb_b", [128, 256], BF16)

                with nc.psum_tensor("ps_lg", [16, TL], F32) as ps_lg:
                    def lgmm(e):
                        for kc in range(8):
                            for nt in range(4):
                                ins = nc.tensor.matmul(
                                    ps_lg[:, nt * 512:(nt + 1) * 512],
                                    gw_b[:, kc * NE:(kc + 1) * NE],
                                    xn[:, kc * TL + nt * 512: kc * TL + (nt + 1) * 512],
                                    start=(kc == 0), stop=(kc == 7), skip_group_check=True)
                        return ins
                    em.emit("tensor", lgmm, waits=[("V", v_attn)], inc="PE")
                    pe_lg = em.at("PE")
                    em.emit("vector", lambda e: nc.vector.tensor_copy(lg_sb[:], ps_lg[:]),
                            waits=[("PE", pe_lg)], inc="V")
                    v_lg = em.at("V")
                with nc.psum_tensor("ps_lgt", [128, 256], F32) as ps_lgt:
                    def lgtr(e):
                        for tc in range(16):
                            ins = nc.tensor.transpose(
                                ps_lgt[:, tc * 16:(tc + 1) * 16],
                                lg_sb[:, tc * 128:(tc + 1) * 128], identf[:])
                        return ins
                    em.emit("tensor", lgtr, waits=[("V", v_lg)], inc="PE")
                    pe_lgt = em.at("PE")
                    em.emit("vector", lambda e: nc.vector.tensor_copy(lgt[:], ps_lgt[:]),
                            waits=[("PE", pe_lgt)], inc="V")

                    def m1r(e):
                        for tc in range(16):
                            ins = nc.vector.reduce_max(
                                m1s[:, tc:tc + 1], lgt[:, tc * 16:(tc + 1) * 16], AX.X)
                        return ins
                    em.emit("vector", m1r, inc="V")
                    em.emit("vector", lambda e: nc.vector.tensor_scalar(
                        nm1[:], m1s[:], -1.0, None, ALU.mult), inc="V")
                    em.emit("vector", lambda e: e.drain() or e.engine_nop(), inc="V")
                    v_m1 = em.at("V")

                    def sexp(e):
                        for tc in range(16):
                            ins = nc.scalar.activation(
                                est[:, tc * 16:(tc + 1) * 16], lgt[:, tc * 16:(tc + 1) * 16],
                                AF.Exp, bias=nm1[:, tc:tc + 1], accum_out=ssum[:, tc:tc + 1])
                        return ins
                    em.emit("scalar", sexp, waits=[("V", v_m1)], inc="SC")
                    sc_e = em.at("SC")
                    em.emit("vector", lambda e: nc.vector.reciprocal(rs[:], ssum[:]),
                            waits=[("SC", sc_e)], inc="V")

                    def eqms(e):
                        for tc in range(16):
                            ins = nc.vector.tensor_scalar(
                                eqm[:, tc * 16:(tc + 1) * 16], lgt[:, tc * 16:(tc + 1) * 16],
                                m1s[:, tc:tc + 1], None, ALU.is_equal)
                        return ins
                    em.emit("vector", eqms, inc="V")
                    em.emit("vector", lambda e: nc.vector.scalar_tensor_tensor(
                        eqm[:], eqm[:], NEGC, lgt[:], ALU.mult, ALU.add), inc="V")

                    def m2r(e):
                        for tc in range(16):
                            ins = nc.vector.reduce_max(
                                m2s[:, tc:tc + 1], eqm[:, tc * 16:(tc + 1) * 16], AX.X)
                        return ins
                    em.emit("vector", m2r, inc="V")
                    em.emit("vector", lambda e: e.drain() or e.engine_nop(), inc="V")

                    def keeps(e):
                        for tc in range(16):
                            ins = nc.vector.tensor_scalar(
                                keep[:, tc * 16:(tc + 1) * 16], lgt[:, tc * 16:(tc + 1) * 16],
                                m2s[:, tc:tc + 1], None, ALU.is_ge)
                        return ins
                    em.emit("vector", keeps, inc="V")

                    def combs(e):
                        for tc in range(16):
                            ins = nc.vector.scalar_tensor_tensor(
                                comb[:, tc * 16:(tc + 1) * 16], est[:, tc * 16:(tc + 1) * 16],
                                rs[:, tc:tc + 1], keep[:, tc * 16:(tc + 1) * 16],
                                ALU.mult, ALU.mult)
                        return ins
                    em.emit("vector", combs, inc="V")
                    em.emit("vector", lambda e: nc.vector.tensor_copy(comb_b[:], comb[:]), inc="V")
                    v_comb2 = em.at("V")
                with nc.psum_tensor("ps_cbt", [16, TL], BF16) as ps_cbt:
                    def cbtr(e):
                        for tc in range(16):
                            ins = nc.tensor.transpose(
                                ps_cbt[:, tc * 128:(tc + 1) * 128],
                                comb_b[:, tc * 16:(tc + 1) * 16], ident[:])
                        return ins
                    em.emit("tensor", cbtr, waits=[("V", v_comb2)], inc="PE")
                    pe_cbt = em.at("PE")
                    em.emit("vector", lambda e: nc.vector.tensor_copy(combT[:], ps_cbt[:]),
                            waits=[("PE", pe_cbt)], inc="V")
                    v_cbt = em.at("V")

            em.scope = "ffn_sh"
            with contextlib.ExitStack() as shs:
                wsg_b = sbt(shs, "wsg_b", [128, 8 * D], BF16)
                wsu_b = sbt(shs, "wsu_b", [128, 8 * D], BF16)
                wsd_b = sbt(shs, "wsd_b", [128, 8 * D], BF16)
                act_sh = sbt(shs, "act_sh", [128, 4 * 1024], BF16)
                sg_t = sbt(shs, "sg_t", [128, 1024], BF16)
                first = True
                for dst, src in ((wsg_b, wsg_d), (wsu_b, wsu_d), (wsd_b, wsd_d)):
                    em.emit("gpsimd", lambda e, dst=dst, src=src: e.dma_start(
                        out=dst[:].rearrange("p (r c) -> p r c", r=8),
                        in_=src.rearrange("(r p) c -> p r c", p=128)),
                        waits=([("PE", em.at("PE")), ("V", em.at("V"))] if first else None),
                        inc="dG", k=16)
                    first = False
                d_ws = em.at("dG")

                with contextlib.ExitStack() as pss:
                    ps_g = pss.enter_context(nc.psum_tensor("ps_sg", [128, 1024], F32))
                    ps_u = pss.enter_context(nc.psum_tensor("ps_su", [128, 1024], F32))
                    ps_d2 = pss.enter_context(nc.psum_tensor("ps_sd2", [128, 1024], F32))
                    for th in range(2):
                        for ih in range(2):
                            for icl in range(4):
                                ic = ih * 4 + icl

                                def gumm(e, th=th, ic=ic):
                                    for kc in range(8):
                                        for nh in range(2):
                                            nc.tensor.matmul(
                                                ps_g[:, nh * 512:(nh + 1) * 512],
                                                wsg_b[:, kc * D + ic * 128: kc * D + (ic + 1) * 128],
                                                xn[:, kc * TL + th * 1024 + nh * 512:
                                                   kc * TL + th * 1024 + (nh + 1) * 512],
                                                start=(kc == 0), stop=(kc == 7),
                                                skip_group_check=True)
                                    for kc in range(8):
                                        for nh in range(2):
                                            ins = nc.tensor.matmul(
                                                ps_u[:, nh * 512:(nh + 1) * 512],
                                                wsu_b[:, kc * D + ic * 128: kc * D + (ic + 1) * 128],
                                                xn[:, kc * TL + th * 1024 + nh * 512:
                                                   kc * TL + th * 1024 + (nh + 1) * 512],
                                                start=(kc == 0), stop=(kc == 7),
                                                skip_group_check=True)
                                    return ins
                                wl = [("V", em.at("V")), ("SC", em.at("SC"))]
                                if th == 0 and ic == 0:
                                    wl.append(("dG", d_ws))
                                em.emit("tensor", gumm, waits=wl, inc="PE")
                                pe_gu = em.at("PE")
                                em.emit("scalar", lambda e: nc.scalar.activation(
                                    sg_t[:], ps_g[:], AF.Silu),
                                    waits=[("PE", pe_gu)], inc="SC")
                                sc_sg = em.at("SC")
                                em.emit("vector", lambda e, icl=icl: nc.vector.tensor_tensor(
                                    act_sh[:, icl * 1024:(icl + 1) * 1024], sg_t[:], ps_u[:],
                                    ALU.mult),
                                    waits=[("SC", sc_sg)], inc="V")
                            v_actd = em.at("V")
                            v_dev = v_actd
                            for oc in range(8):
                                def sdmm(e, oc=oc, ih=ih):
                                    for icl in range(4):
                                        ic = ih * 4 + icl
                                        for nh in range(2):
                                            ins = nc.tensor.matmul(
                                                ps_d2[:, nh * 512:(nh + 1) * 512],
                                                wsd_b[:, ic * D + oc * 128: ic * D + (oc + 1) * 128],
                                                act_sh[:, icl * 1024 + nh * 512:
                                                       icl * 1024 + (nh + 1) * 512],
                                                start=(icl == 0), stop=(icl == 3),
                                                skip_group_check=True)
                                    return ins
                                em.emit("tensor", sdmm, waits=[("V", v_dev)], inc="PE")
                                pe_sd = em.at("PE")
                                if ih == 0:
                                    em.emit("vector", lambda e, oc=oc, th=th: nc.vector.tensor_copy(
                                        acc[:, oc * TL + th * 1024: oc * TL + (th + 1) * 1024],
                                        ps_d2[:]),
                                        waits=[("PE", pe_sd)], inc="V")
                                else:
                                    em.emit("vector", lambda e, oc=oc, th=th: nc.vector.tensor_tensor(
                                        acc[:, oc * TL + th * 1024: oc * TL + (th + 1) * 1024],
                                        acc[:, oc * TL + th * 1024: oc * TL + (th + 1) * 1024],
                                        ps_d2[:], ALU.add),
                                        waits=[("PE", pe_sd)], inc="V")
                                v_dev = em.at("V")

            em.scope = "ffn_expw"
            with contextlib.ExitStack() as exs:
                wgb = [sbt(exs, f"wgb{i}", [128, 8 * EI], BF16) for i in range(2)]
                wub = [sbt(exs, f"wub{i}", [128, 8 * EI], BF16) for i in range(2)]
                wdb = [sbt(exs, f"wdb{i}", [128, 4 * D], BF16) for i in range(2)]
                act_e = sbt(exs, "act_e", [128, 4 * 1024], BF16)
                sg_e = sbt(exs, "sg_e", [128, 1024], BF16)
                uc_e = sbt(exs, "uc_e", [128, 1024], BF16)
                crep = sbt(exs, "crep", [128, 2 * TL], BF16)

                d_exp = [0] * NE
                d_crep = [0] * NE
                pe_wuse = [0] * NE
                v_uexp = [0] * NE

                def emit_exp_load(ei):
                    sl = ei % 2
                    wl = ([("PE", pe_wuse[ei - 2])] if ei >= 2
                          else [("PE", em.at("PE")), ("V", em.at("V"))])
                    em.emit("gpsimd", lambda e, ei=ei, sl=sl: e.dma_start(
                        out=wgb[sl][:].rearrange("p (r c) -> p r c", r=8),
                        in_=wg_d[ei].rearrange("(r p) c -> p r c", p=128)),
                        waits=wl, inc="dG", k=16)
                    em.emit("gpsimd", lambda e, ei=ei, sl=sl: e.dma_start(
                        out=wub[sl][:].rearrange("p (r c) -> p r c", r=8),
                        in_=wu_d[ei].rearrange("(r p) c -> p r c", p=128)), inc="dG", k=16)
                    em.emit("gpsimd", lambda e, ei=ei, sl=sl: e.dma_start(
                        out=wdb[sl][:].rearrange("p (r c) -> p r c", r=4),
                        in_=wd_d[ei].rearrange("(r p) c -> p r c", p=128)), inc="dG", k=16)
                    d_exp[ei] = em.at("dG")

                def emit_crep_build(ei):
                    sl = ei % 2
                    base = sl * TL
                    wl = [("V", v_cbt)]
                    if ei >= 2:
                        wl.append(("V", v_uexp[ei - 2]))
                    em.emit("sync", lambda e, ei=ei, base=base: e.dma_start(
                        out=crep[0:1, base:base + TL], in_=combT[ei:ei + 1, :]),
                        waits=wl, inc="dS", k=16)
                    for step in (1, 2, 4, 8, 16, 32, 64):
                        dprev = em.at("dS")
                        em.emit("sync", lambda e, step=step, base=base: e.dma_start(
                            out=crep[step:2 * step, base:base + TL],
                            in_=crep[0:step, base:base + TL]),
                            waits=[("dS", dprev)], inc="dS", k=16)
                    d_crep[ei] = em.at("dS")

                emit_exp_load(0)
                if NE_RUN > 1:
                    emit_exp_load(1)

                with contextlib.ExitStack() as pse:
                    ps_g = pse.enter_context(nc.psum_tensor("ps_eg", [128, 1024], F32))
                    ps_u = pse.enter_context(nc.psum_tensor("ps_eu", [128, 1024], F32))
                    ps_d2 = pse.enter_context(nc.psum_tensor("ps_ed", [128, 1024], F32))

                    em.scope = "ffn_exp"
                    half_user = {0: (0, 0), 1: (0, 0)}
                    sg_user = {0: 0, 1: 0}
                    act_user = {0: 0, 1: 0}
                    gp_act = {}
                    v_dev = [em.at("V"), em.at("V")]
                    pending_ed = None

                    def emit_gu(ei, tp):
                        sl = ei % 2
                        for ic in range(4):
                            h = ic % 2

                            def gumm(e, sl=sl, tp=tp, ic=ic, h=h):
                                for kc in range(8):
                                    nc.tensor.matmul(
                                        ps_g[:, h * 512:(h + 1) * 512],
                                        wgb[sl][:, kc * EI + ic * 128: kc * EI + (ic + 1) * 128],
                                        xn[:, kc * TL + tp * 512: kc * TL + (tp + 1) * 512],
                                        start=(kc == 0), stop=(kc == 7),
                                        skip_group_check=True)
                                for kc in range(8):
                                    ins = nc.tensor.matmul(
                                        ps_u[:, h * 512:(h + 1) * 512],
                                        wub[sl][:, kc * EI + ic * 128: kc * EI + (ic + 1) * 128],
                                        xn[:, kc * TL + tp * 512: kc * TL + (tp + 1) * 512],
                                        start=(kc == 0), stop=(kc == 7),
                                        skip_group_check=True)
                                return ins
                            sc_u, v_u = half_user[h]
                            wl = [("SC", sc_u), ("V", v_u)]
                            if tp == 0 and ic == 0:
                                wl.append(("dG", d_exp[ei]))
                            em.emit("tensor", gumm, waits=wl, inc="PE")
                            pe_gu = em.at("PE")
                            em.emit("scalar", lambda e, h=h: nc.scalar.activation(
                                sg_e[:, h * 512:(h + 1) * 512],
                                ps_g[:, h * 512:(h + 1) * 512], AF.Silu),
                                waits=[("PE", pe_gu), ("GP", sg_user[h])], inc="SC")
                            sc_sg = em.at("SC")
                            wl2 = [("PE", pe_gu), ("GP", sg_user[h])]
                            if tp == 0 and ic == 0:
                                wl2.append(("dS", d_crep[ei]))

                            def ucop(e, h=h, tp=tp, sl=sl):
                                return nc.vector.scalar_tensor_tensor(
                                    uc_e[:, h * 512:(h + 1) * 512],
                                    ps_u[:, h * 512:(h + 1) * 512], 0.0,
                                    crep[:, sl * TL + tp * 512: sl * TL + (tp + 1) * 512],
                                    ALU.add, ALU.mult)
                            em.emit("vector", ucop, waits=wl2, inc="V")
                            v_uc = em.at("V")
                            half_user[h] = (sc_sg, v_uc)
                            asl = tp % 2
                            em.emit("gpsimd", lambda e, h=h, ic=ic, asl=asl: e.tensor_tensor(
                                act_e[:, asl * 2048 + ic * 512: asl * 2048 + (ic + 1) * 512],
                                sg_e[:, h * 512:(h + 1) * 512],
                                uc_e[:, h * 512:(h + 1) * 512], ALU.mult),
                                waits=[("SC", sc_sg), ("V", v_uc),
                                       ("PE", act_user[asl])], inc="GP")
                            sg_user[h] = em.at("GP")
                        gp_act[(ei, tp)] = em.at("GP")
                        if tp == 3:
                            v_uexp[ei] = em.at("V")

                    def emit_ed(ei, tp):
                        sl = ei % 2
                        asl = tp % 2
                        for oc in range(8):
                            h = oc % 2

                            def edmm(e, sl=sl, asl=asl, oc=oc, h=h):
                                for ic in range(4):
                                    ins = nc.tensor.matmul(
                                        ps_d2[:, h * 512:(h + 1) * 512],
                                        wdb[sl][:, ic * D + oc * 128: ic * D + (oc + 1) * 128],
                                        act_e[:, asl * 2048 + ic * 512: asl * 2048 + (ic + 1) * 512],
                                        start=(ic == 0), stop=(ic == 3),
                                        skip_group_check=True)
                                return ins
                            wl = [("V", v_dev[h])]
                            if oc == 0:
                                wl.append(("GP", gp_act[(ei, tp)]))
                            em.emit("tensor", edmm, waits=wl, inc="PE")
                            pe_ed = em.at("PE")

                            def dacc(e, oc=oc, tp=tp, h=h):
                                return nc.vector.tensor_tensor(
                                    acc[:, oc * TL + tp * 512: oc * TL + (tp + 1) * 512],
                                    acc[:, oc * TL + tp * 512: oc * TL + (tp + 1) * 512],
                                    ps_d2[:, h * 512:(h + 1) * 512], ALU.add)
                            em.emit("vector", dacc, waits=[("PE", pe_ed)], inc="V")
                            v_dev[h] = em.at("V")
                        act_user[asl] = em.at("PE")

                    emit_crep_build(0)
                    if NE_RUN > 1:
                        emit_crep_build(1)
                    for ei in range(NE_RUN):
                        emit_gu(ei, 0)
                        if pending_ed is not None:
                            emit_ed(*pending_ed)
                            pe_wuse[pending_ed[0]] = em.at("PE")
                            if pending_ed[0] + 2 < NE_RUN:
                                emit_exp_load(pending_ed[0] + 2)
                                emit_crep_build(pending_ed[0] + 2)
                        emit_gu(ei, 1)
                        emit_ed(ei, 0)
                        emit_gu(ei, 2)
                        emit_ed(ei, 1)
                        emit_gu(ei, 3)
                        emit_ed(ei, 2)
                        pending_ed = (ei, 3)
                    emit_ed(*pending_ed)
                v_ffn = em.at("V")

            em.scope = "p4"
            with contextlib.ExitStack() as ph4:
                xf2 = [sbt(ph4, f"xf2_{i}", [128, TL], F32) for i in range(2)]
                osb = [sbt(ph4, f"osb{i}", [128, TL], F32) for i in range(2)]
                v_of = [v_ffn, v_ffn]
                d_of = [0, 0]
                for oc in range(8):
                    half = oc % 2
                    em.emit("sync", lambda e, oc=oc, half=half: e.dma_start(
                        out=xf2[half][:], in_=xT[oc * 128:(oc + 1) * 128, :]),
                        waits=[("V", v_of[half])], inc="dS", k=16)
                    d_x2 = em.at("dS")
                    em.emit("vector", lambda e, oc=oc, half=half: nc.vector.tensor_tensor(
                        osb[half][:], xf2[half][:], att_out[:, oc * TL:(oc + 1) * TL],
                        ALU.add),
                        waits=[("dS", d_x2), ("dO", d_of[half]), ("V", v_ffn)], inc="V")
                    em.emit("vector", lambda e, oc=oc, half=half: nc.vector.tensor_tensor(
                        osb[half][:], osb[half][:], acc[:, oc * TL:(oc + 1) * TL],
                        ALU.add), inc="V")
                    v_of[half] = em.at("V")
                    em.emit("sync", lambda e, oc=oc, half=half: e.dma_start(
                        out=outT[oc * 128:(oc + 1) * 128, :], in_=osb[half][:]),
                        waits=[("V", v_of[half])], inc="dO", k=16)
                    d_of[half] = em.at("dO")
                d_all = em.at("dO")
                em.emit("sync", lambda e: e.nop(), waits=[("dO", d_all)], inc=None)

        _replay(nc, em)

    return nc


def _finale(nc, em, src_bf, outT, v_wait):
    for oc in range(8):
        em.emit("gpsimd", lambda e, oc=oc: e.dma_start(
            out=outT[oc * 128:(oc + 1) * 128, :],
            in_=src_bf[:, oc * TL:(oc + 1) * TL]),
            waits=([("V", v_wait)] if oc == 0 else None), inc="dO", k=16)
    d_all = em.at("dO")
    em.emit("gpsimd", lambda e: e.nop(), waits=[("dO", d_all)], inc=None)


def _run_ops(nc, e, ops):
    for sc, f in ops:
        if sc:
            with nc.named_scope(sc):
                f(e)
        else:
            f(e)


def _replay(nc, em):
    with nc.Block() as block:
        @block.sync
        def _(e):
            _run_ops(nc, e, em.ops["sync"])

        @block.gpsimd
        def _(e):
            _run_ops(nc, e, em.ops["gpsimd"])

        @block.scalar
        def _(e):
            _run_ops(nc, e, em.ops["scalar"])

        @block.vector
        def _(e):
            _run_ops(nc, e, em.ops["vector"])

        @block.tensor
        def _(e):
            _run_ops(nc, e, em.ops["tensor"])


_NC_CACHE = None


def kernel(L0_attn_x, L0_attn_Wq, L0_attn_Wk, L0_attn_Wv, L0_attn_Wo,
           L0_attn_betas, L0_attn_memory0, L0_attn_z0,
           L0_ffn_gate_w, L0_ffn_w_gate, L0_ffn_w_up, L0_ffn_w_down,
           L0_ffn_ws_gate, L0_ffn_ws_up, L0_ffn_ws_down):
    global _NC_CACHE
    x = np.asarray(L0_attn_x, dtype=np.float32)
    assert x.shape == (2, 8192, 1024)

    cont = np.ascontiguousarray
    shared = {
        "wq": cont(np.asarray(L0_attn_Wq, np.float32)),
        "wk": cont(np.asarray(L0_attn_Wk, np.float32)),
        "wv": cont(np.asarray(L0_attn_Wv, np.float32)),
        "wo": cont(np.asarray(L0_attn_Wo, np.float32)),
        "gwT": cont(np.asarray(L0_ffn_gate_w, np.float32).T),
        "wg": cont(np.asarray(L0_ffn_w_gate, np.float32)),
        "wu": cont(np.asarray(L0_ffn_w_up, np.float32)),
        "wd": cont(np.asarray(L0_ffn_w_down, np.float32)),
        "wsg": cont(np.asarray(L0_ffn_ws_gate, np.float32)),
        "wsu": cont(np.asarray(L0_ffn_ws_up, np.float32)),
        "wsd": cont(np.asarray(L0_ffn_ws_down, np.float32)),
    }
    betas = np.asarray(L0_attn_betas, np.float32)
    mem0 = np.asarray(L0_attn_memory0, np.float32)
    z0 = np.asarray(L0_attn_z0, np.float32)
    maug_full = np.concatenate([mem0[0], np.swapaxes(z0[0], 1, 2)], axis=2)

    in_maps = []
    for c in range(8):
        b, p = c // 4, c % 4
        xs = x[b].reshape(16, 16, 32, 1024)[:, 4 * p:4 * p + 4]
        m = dict(shared)
        m["xT"] = cont(xs.reshape(2048, 1024).T)
        m["betas"] = cont(betas[0, 4 * p:4 * p + 4, 0, :].reshape(1, 256))
        m["maug"] = cont(maug_full[4 * p:4 * p + 4])
        in_maps.append(m)

    if _NC_CACHE is None:
        _NC_CACHE = build_program()
    res = run_bass_kernel_spmd(_NC_CACHE, in_maps, core_ids=list(range(8)))

    out = np.empty((2, 8192, 1024), np.float32)
    ov = out.reshape(2, 16, 16, 32, 1024)
    for c in range(8):
        b, p = c // 4, c % 4
        o = np.asarray(res.results[c]["outT"], np.float32)
        ov[b, :, 4 * p:4 * p + 4] = o.T.reshape(16, 4, 32, 1024)
    return out


# version 10
# speedup vs baseline: 1.0101x; 1.0084x over previous
import contextlib
import numpy as np
import ml_dtypes
import concourse.bass as bass
import concourse.mybir as mybir
from concourse.bass_utils import run_bass_kernel_spmd

F32 = mybir.dt.float32
BF16 = mybir.dt.bfloat16
AF = mybir.ActivationFunctionType
ALU = mybir.AluOpType
AX = mybir.AxisListType

ENGS = ("sync", "gpsimd", "scalar", "vector", "tensor")

D = 1024
TL = 2048
NSEG = 16
NA = 4
NE = 16
NE_RUN = 16
STOP_AFTER = None
A_RUN = 4
SEG_RUN = 16
SEG_OPS = 99
DBG_SEG0 = False
EI = 512
NEGC = -1.0e30


def _masks_np():
    out = np.zeros((4, 128, 512), np.float32)
    for beta in range(4):
        hl, ccg = beta // 2, beta % 2
        for ccl in range(4):
            for j in range(32):
                r = ccl * 32 + j
                m = 16 * j + 2 * (4 * ccg + ccl) + hl
                for hlp in range(2):
                    for jp in range(32):
                        for ccp in range(8):
                            c = hlp * 256 + (jp // 16) * 128 + (jp % 16) * 8 + ccp
                            mp = 16 * jp + 2 * ccp + hlp
                            if mp > m:
                                out[beta, r, c] = NEGC
    return out.astype(ml_dtypes.bfloat16)


class Em:
    def __init__(self, sems):
        self.sems = sems
        self.cnt = {k: 0 for k in sems}
        self.ops = {e: [] for e in ENGS}
        self.scope = None

    def emit(self, eng, f, waits=None, inc=None, k=1):
        sems = self.sems
        wl = tuple((s, v) for (s, v) in (waits or ()) if v > 0)

        def g(engobj, wl=wl, inc=inc, k=k, f=f):
            for s, v in wl:
                engobj.wait_ge(sems[s], v)
            ins = f(engobj)
            if inc is not None:
                ins.then_inc(sems[inc], k)
        self.ops[eng].append((self.scope, g))
        if inc is not None:
            self.cnt[inc] += k
        return self.cnt[inc] if inc is not None else None

    def at(self, s):
        return self.cnt[s]


def build_program():
    nc = bass.Bass("TRN2", target_bir_lowering=False,
                   detect_race_conditions=False)

    xT = nc.dram_tensor("xT", [D, TL], F32, kind="ExternalInput").ap()
    wq_d = nc.dram_tensor("wq", [D, D], F32, kind="ExternalInput").ap()
    wk_d = nc.dram_tensor("wk", [D, D], F32, kind="ExternalInput").ap()
    wv_d = nc.dram_tensor("wv", [D, D], F32, kind="ExternalInput").ap()
    wo_d = nc.dram_tensor("wo", [D, D], F32, kind="ExternalInput").ap()
    gwT_d = nc.dram_tensor("gwT", [D, NE], F32, kind="ExternalInput").ap()
    wg_d = nc.dram_tensor("wg", [NE, D, EI], F32, kind="ExternalInput").ap()
    wu_d = nc.dram_tensor("wu", [NE, D, EI], F32, kind="ExternalInput").ap()
    wd_d = nc.dram_tensor("wd", [NE, EI, D], F32, kind="ExternalInput").ap()
    wsg_d = nc.dram_tensor("wsg", [D, D], F32, kind="ExternalInput").ap()
    wsu_d = nc.dram_tensor("wsu", [D, D], F32, kind="ExternalInput").ap()
    wsd_d = nc.dram_tensor("wsd", [D, D], F32, kind="ExternalInput").ap()
    betas_d = nc.dram_tensor("betas", [1, 256], F32, kind="ExternalInput").ap()
    maug_d = nc.dram_tensor("maug", [NA, 64, 65], F32, kind="ExternalInput").ap()
    outT = nc.dram_tensor("outT", [D, TL], F32, kind="ExternalOutput").ap()

    ident_d = nc.inline_tensor(np.eye(128, dtype=ml_dtypes.bfloat16), "c_ident").ap()
    identf_d = nc.inline_tensor(np.eye(16, dtype=np.float32), "c_identf").ap()
    ones_d = nc.inline_tensor(np.ones((128, 128), dtype=ml_dtypes.bfloat16), "c_ones").ap()
    masks_d = nc.inline_tensor(np.ascontiguousarray(_masks_np().reshape(4 * 128, 512)),
                               "c_masks").ap()
    sel_np = np.zeros((16, 16, 128), np.float32)
    for e_ in range(16):
        sel_np[e_, e_, :] = 1.0
    sel_d = nc.inline_tensor(np.ascontiguousarray(
        sel_np.transpose(1, 0, 2).reshape(16, 16 * 128)).astype(ml_dtypes.bfloat16),
        "c_sel").ap()

    ctx = contextlib.ExitStack()
    with ctx:
        def sbt(st, name, shape, dt):
            return st.enter_context(nc.sbuf_tensor(name, shape, dt))

        sems = {}
        for s_ in ("dS", "dG", "PE", "V", "SC", "GP", "dO", "dO2"):
            sems[s_] = ctx.enter_context(nc.semaphore(s_))
        em = Em(sems)

        ident = sbt(ctx, "ident", [128, 128], BF16)
        identf = sbt(ctx, "identf", [16, 16], F32)
        ones_b = sbt(ctx, "ones_b", [128, 128], BF16)

        betas_sb = sbt(ctx, "betas_sb", [1, 256], F32)
        gw_b = sbt(ctx, "gw_b", [128, 8 * NE], BF16)
        xn = sbt(ctx, "xn", [128, 8 * TL], BF16)
        att_out = sbt(ctx, "att_out", [128, 8 * TL], BF16)
        gate_row = sbt(ctx, "gate_row", [1, 256], F32)
        onemg_row = sbt(ctx, "onemg_row", [1, 256], F32)
        onemg_b = sbt(ctx, "onemg_b", [1, 256], BF16)
        gam_row = sbt(ctx, "gam_row", [1, 256], F32)
        gam65_b = sbt(ctx, "gam65_b", [1, NA * 65], BF16)

        p2w = ctx.enter_context(contextlib.ExitStack())
        masks_b = [sbt(p2w, f"mask{i}", [128, 512], BF16) for i in range(4)]
        wq_b = sbt(p2w, "wq_b", [128, 8 * 512], BF16)
        wk_b = sbt(p2w, "wk_b", [128, 8 * 512], BF16)
        wvo_b = sbt(p2w, "wvo_b", [128, 8 * 512], BF16)

        def wload(e, buf, src_d, h):
            return e.dma_start(
                out=buf[:].rearrange("p (r c) -> p r c", r=8),
                in_=src_d.rearrange("(r p) c -> p r c", p=128)[
                    :, :, h * 512:(h + 1) * 512])

        em.scope = "p0"
        for dst, src in ((ident, ident_d), (identf, identf_d), (ones_b, ones_d),
                         (betas_sb, betas_d)):
            em.emit("sync", lambda e, dst=dst, src=src: e.dma_start(out=dst[:], in_=src),
                    inc="dS", k=16)
        for i in range(4):
            em.emit("sync", lambda e, i=i: e.dma_start(
                out=masks_b[i][:], in_=masks_d[i * 128:(i + 1) * 128, :]), inc="dS", k=16)
        d_consts = em.at("dS")

        em.emit("gpsimd", lambda e: e.dma_start(
            out=gw_b[:].rearrange("p (r c) -> p r c", r=8),
            in_=gwT_d.rearrange("(r p) c -> p r c", p=128)), inc="dG", k=16)
        em.emit("gpsimd", lambda e: wload(e, wq_b, wq_d, 0), inc="dG", k=16)
        d_wq_h = [em.at("dG"), 0]
        em.emit("gpsimd", lambda e: wload(e, wk_b, wk_d, 0), inc="dG", k=16)
        d_wk_h = [em.at("dG"), 0]

        em.emit("scalar", lambda e: nc.scalar.activation(gate_row[:], betas_sb[:], AF.Sigmoid),
                waits=[("dS", d_consts)], inc="SC")
        sc_gate = em.at("SC")
        em.emit("vector", lambda e: nc.vector.tensor_scalar(
            onemg_row[:], gate_row[:], 1.0, -1.0, ALU.subtract, ALU.mult),
            waits=[("SC", sc_gate)], inc="V")
        em.emit("vector", lambda e: nc.vector.reciprocal(gam_row[:], onemg_row[:]), inc="V")
        em.emit("vector", lambda e: nc.vector.tensor_mul(gam_row[:], gate_row[:], gam_row[:]), inc="V")
        em.emit("vector", lambda e: nc.vector.tensor_copy(onemg_b[:], onemg_row[:]), inc="V")
        for a in range(NA):
            em.emit("vector", lambda e, a=a: nc.vector.tensor_copy(
                gam65_b[:, a * 65:a * 65 + 64], gam_row[:, a * 64:(a + 1) * 64]), inc="V")
        em.emit("vector", lambda e: nc.vector.memset(
            gam65_b[:].rearrange("p (a r) -> p a r", a=NA)[:, :, 64:65], 1.0), inc="V")
        v_gate = em.at("V")

        em.scope = "ln"
        with contextlib.ExitStack() as ph:
            xf = [sbt(ph, f"xf{i}", [128, TL], F32) for i in range(2)]
            xbf = sbt(ph, "xbf", [128, 8 * TL], BF16)
            x2 = sbt(ph, "x2", [128, TL], BF16)
            msb = sbt(ph, "msb", [128, TL], F32)
            vsb = sbt(ph, "vsb", [128, TL], F32)
            rstd = sbt(ph, "rstd", [128, TL], F32)
            ps_sx = ph.enter_context(nc.psum_tensor("ps_sx", [128, TL], F32))
            ps_sx2 = ph.enter_context(nc.psum_tensor("ps_sx2", [128, TL], F32))

            v_free = [0, 0]
            pe_x2done = [0] * 8
            for kc in range(8):
                q, qs = (("sync", "dS") if kc % 2 == 0 else ("gpsimd", "dG"))
                em.emit(q, lambda e, kc=kc: e.dma_start(
                    out=xf[kc % 2][:], in_=xT[kc * 128:(kc + 1) * 128, :]),
                    waits=[("V", v_free[kc % 2])], inc=qs, k=16)
                dld = em.at(qs)
                em.emit("vector", lambda e, kc=kc: nc.vector.tensor_copy(
                    xbf[:, kc * TL:(kc + 1) * TL], xf[kc % 2][:]),
                    waits=[(qs, dld)], inc="V")
                v_cast = em.at("V")
                v_free[kc % 2] = v_cast
                em.emit("scalar", lambda e, kc=kc: nc.scalar.activation(
                    x2[:], xbf[:, kc * TL:(kc + 1) * TL], AF.Square),
                    waits=[("V", v_cast)] + ([("PE", pe_x2done[kc - 1])] if kc >= 1 else []),
                    inc="SC")
                sc_sq = em.at("SC")

                def lnmm(e, kc=kc):
                    for nt in range(4):
                        nc.tensor.matmul(
                            ps_sx[:, nt * 512:(nt + 1) * 512], ones_b[:],
                            xbf[:, kc * TL + nt * 512: kc * TL + (nt + 1) * 512],
                            start=(kc == 0), stop=(kc == 7), skip_group_check=True)
                    for nt in range(4):
                        ins = nc.tensor.matmul(
                            ps_sx2[:, nt * 512:(nt + 1) * 512], ones_b[:],
                            x2[:, nt * 512:(nt + 1) * 512],
                            start=(kc == 0), stop=(kc == 7), skip_group_check=True)
                    return ins
                em.emit("tensor", lnmm, waits=[("V", v_cast), ("SC", sc_sq)], inc="PE")
                pe_x2done[kc] = em.at("PE")
            pe_ln = em.at("PE")

            em.emit("vector", lambda e: nc.vector.tensor_scalar(
                msb[:], ps_sx[:], 1.0 / 1024.0, None, ALU.mult),
                waits=[("PE", pe_ln)], inc="V")
            em.emit("vector", lambda e: nc.vector.tensor_mul(vsb[:], msb[:], msb[:]), inc="V")
            em.emit("vector", lambda e: nc.vector.scalar_tensor_tensor(
                vsb[:], ps_sx2[:], 1.0 / 1024.0, vsb[:], ALU.mult, ALU.subtract), inc="V")
            em.emit("vector", lambda e: nc.vector.tensor_scalar(
                vsb[:], vsb[:], 1e-5, None, ALU.add), inc="V")
            em.emit("vector", lambda e: nc.vector.reciprocal(vsb[:], vsb[:]), inc="V")
            v_var = em.at("V")
            em.emit("scalar", lambda e: nc.scalar.activation(rstd[:], vsb[:], AF.Sqrt),
                    waits=[("V", v_var)], inc="SC")
            sc_rstd = em.at("SC")
            for kc in range(8):
                em.emit("vector", lambda e, kc=kc: nc.vector.tensor_sub(
                    vsb[:], xbf[:, kc * TL:(kc + 1) * TL], msb[:]),
                    waits=([("SC", sc_rstd)] if kc == 0 else None), inc="V")
                em.emit("vector", lambda e, kc=kc: nc.vector.tensor_mul(
                    xn[:, kc * TL:(kc + 1) * TL], vsb[:], rstd[:]), inc="V")
            v_xn = em.at("V")

        if STOP_AFTER == 'ln':
            _finale(nc, em, xn, outT, v_xn)
            _replay(nc, em)
            return nc

        em.scope = "attn_pre"
        with contextlib.ExitStack() as ph:
            Q2A = sbt(ph, "Q2A", [128, NSEG * 256], BF16)
            Q2B = sbt(ph, "Q2B", [128, NSEG * 256], BF16)
            K2 = sbt(ph, "K2", [128, NSEG * 256], BF16)
            K2X = sbt(ph, "K2X", [128, NSEG * 256], BF16)
            Vscr = sbt(ph, "Vscr", [128, NSEG * 256], BF16)
            VscrX = sbt(ph, "VscrX", [64, NSEG * 256], BF16)
            vgcol = sbt(ph, "vgcol", [128, 1], F32)
            att_bf = sbt(ph, "att_bf", [128, 8 * 512], BF16)
            A_scr = sbt(ph, "A_scr", [128, 8 * 512], BF16)
            P_sb = sbt(ph, "P_sb", [128, 2048], BF16)
            PT_sb = sbt(ph, "PT_sb", [128, 2048], BF16)
            sq_scr = sbt(ph, "sq_scr", [128, 256], BF16)
            skE = sbt(ph, "skE", [64, 256], BF16)
            skO = sbt(ph, "skO", [64, 256], BF16)
            elu_t = sbt(ph, "elu_t", [128, 256], F32)
            skT_sb = sbt(ph, "skT_sb", [128, 512], BF16)
            M_master = sbt(ph, "M_master", [128, NA * 65], F32)
            M_dupA = sbt(ph, "M_dupA", [128, 130], BF16)
            M_dupB = sbt(ph, "M_dupB", [128, 130], BF16)
            gamrep = sbt(ph, "gamrep", [128, NA * 65], BF16)
            maug2 = sbt(ph, "maug2", [128, NA * 65], F32)
            rdl_a = sbt(ph, "rdl_a", [1, 512], BF16)
            rdl_b = sbt(ph, "rdl_b", [1, 512], BF16)
            rr_sb = sbt(ph, "rr_sb", [128, 512], BF16)
            rl_sb = sbt(ph, "rl_sb", [128, 512], BF16)
            t1_sb = sbt(ph, "t1_sb", [128, 512], BF16)
            t2_sb = sbt(ph, "t2_sb", [128, 512], BF16)

            for a in range(NA):
                for hb in range(2):
                    em.emit("sync", lambda e, a=a, hb=hb: e.dma_start(
                        out=maug2[hb * 64:(hb + 1) * 64, a * 65:(a + 1) * 65],
                        in_=maug_d[a]),
                        waits=([("V", v_xn)] if (a == 0 and hb == 0) else None),
                        inc="dS", k=16)
            d_maug = em.at("dS")

            em.emit("vector", lambda e: nc.vector.memset(Q2A[64:128, :], 0.0),
                    waits=[("V", v_xn)], inc="V")
            em.emit("vector", lambda e: nc.vector.memset(Q2B[0:64, :], 0.0), inc="V")
            em.emit("vector", lambda e: nc.vector.memset(M_dupA[64:128, :], 0.0), inc="V")
            em.emit("vector", lambda e: nc.vector.memset(M_dupB[0:64, :], 0.0), inc="V")
            v_zinit = em.at("V")

            with nc.psum_tensor("ps_small", [128, 512], F32) as ps_small:
                def g65(e):
                    for a in range(NA):
                        ins = nc.tensor.matmul(
                            ps_small[:, a * 65:(a + 1) * 65], ones_b[0:1, :],
                            gam65_b[:, a * 65:(a + 1) * 65],
                            start=(a == 0), stop=(a == NA - 1), skip_group_check=True)
                    return ins
                em.emit("tensor", g65, waits=[("V", em.at("V"))], inc="PE")
                pe_g65 = em.at("PE")
                em.emit("vector", lambda e: nc.vector.tensor_copy(
                    gamrep[:], ps_small[:, 0:NA * 65]),
                    waits=[("PE", pe_g65)], inc="V")
                v_g65 = em.at("V")
            with nc.psum_tensor("ps_small2", [128, 512], F32) as ps_small2:
                def m0rep(e):
                    for a in range(NA):
                        ins = nc.tensor.matmul(
                            ps_small2[:, a * 64:(a + 1) * 64], ones_b[0:1, :],
                            onemg_b[:, a * 64:(a + 1) * 64],
                            start=(a == 0), stop=(a == NA - 1), skip_group_check=True)
                    return ins
                em.emit("tensor", m0rep, waits=[("V", max(v_gate, v_g65))], inc="PE")
                pe_m0 = em.at("PE")
                for a in range(NA):
                    em.emit("vector", lambda e, a=a: nc.vector.tensor_tensor(
                        M_master[:, a * 65:a * 65 + 64], maug2[:, a * 65:a * 65 + 64],
                        ps_small2[:, a * 64:(a + 1) * 64], ALU.mult),
                        waits=[("PE", pe_m0), ("dS", d_maug)], inc="V")
                    em.emit("vector", lambda e, a=a: nc.vector.tensor_copy(
                        M_master[:, a * 65 + 64:a * 65 + 65],
                        maug2[:, a * 65 + 64:a * 65 + 65]), inc="V")
                v_minit = em.at("V")

            pe_prev_a = 0
            v_mup = v_minit

            for a in range(A_RUN):
                em.scope = "proj"
                em.emit("gpsimd", lambda e: wload(e, wvo_b, wv_d, 0),
                        waits=[("PE", pe_prev_a)], inc="dG", k=16)
                d_wv_h = [em.at("dG"), 0]

                pha = contextlib.ExitStack()
                xn_a = pha.enter_context(nc.sbuf_tensor(f"xn_a{a}", [128, 8 * 512], BF16))
                for kc in range(8):
                    src = xn[:].rearrange("p (k s ar j) -> p k s ar j", k=8, s=16, ar=4)[
                        :, kc:kc + 1, :, a:a + 1, :].squeeze(1).squeeze(2)
                    em.emit("vector", lambda e, kc=kc, src=src: nc.vector.tensor_copy(
                        xn_a[:, kc * 512:(kc + 1) * 512].rearrange("p (s j) -> p s j", s=16),
                        src),
                        waits=([("V", v_xn), ("PE", pe_prev_a)] if kc == 0 else None),
                        inc="V")
                v_xna = em.at("V")

                with nc.psum_tensor(f"ps_vgc{a}", [64, 8], F32) as ps_vgc:
                    em.emit("tensor", lambda e, a=a: nc.tensor.transpose(
                        ps_vgc[0:64, 0:1], onemg_row[0:1, a * 64:(a + 1) * 64],
                        identf[0:1, 0:1]),
                        waits=[("V", em.at("V"))], inc="PE")
                    pe_vg = em.at("PE")
                    em.emit("vector", lambda e: nc.vector.tensor_copy(
                        vgcol[0:64, :], ps_vgc[:, 0:1]), waits=[("PE", pe_vg)], inc="V")
                    em.emit("vector", lambda e: e.drain() or e.engine_nop(), inc="V")
                    v_vg0 = em.at("V")
                    em.emit("gpsimd", lambda e: e.dma_start(
                        out=vgcol[64:128, :], in_=vgcol[0:64, :]),
                        waits=[("V", v_vg0)], inc="dG", k=16)
                    d_vg = em.at("dG")
                    v_vg = v_vg0

                with nc.psum_tensor(f"ps_proj{a}", [128, 1024], F32) as ps_proj:
                    v_evac_half = [v_vg, v_vg]
                    dmap = {0: d_wq_h, 1: d_wk_h, 2: d_wv_h}
                    d_wo_h = [0, 0]
                    for wh in range(2):
                        for ti, wb in enumerate((wq_b, wk_b, wvo_b)):
                            for ccl in range(4):
                                cc = wh * 4 + ccl
                                half = ccl % 2

                                def projmm(e, wb=wb, ccl=ccl, half=half):
                                    for kc in range(8):
                                        ins = nc.tensor.matmul(
                                            ps_proj[:, half * 512:(half + 1) * 512],
                                            wb[:, kc * 512 + ccl * 128: kc * 512 + (ccl + 1) * 128],
                                            xn_a[:, kc * 512:(kc + 1) * 512],
                                            start=(kc == 0), stop=(kc == 7))
                                    return ins
                                wl = [("V", max(v_xna, v_evac_half[half]))]
                                if ccl == 0:
                                    wl.append(("dG", dmap[ti][wh]))
                                em.emit("tensor", projmm, waits=wl, inc="PE")
                                pe_p = em.at("PE")
                                if ccl == 3:
                                    if wh == 0:
                                        em.emit("gpsimd", lambda e, wb=wb, ti=ti: wload(
                                            e, wb, (wq_d, wk_d, wv_d)[ti], 1),
                                            waits=[("PE", pe_p)], inc="dG", k=16)
                                        dmap[ti][1] = em.at("dG")
                                    elif ti < 2 and a + 1 < A_RUN:
                                        em.emit("gpsimd", lambda e, wb=wb, ti=ti: wload(
                                            e, wb, (wq_d, wk_d)[ti], 0),
                                            waits=[("PE", pe_p)], inc="dG", k=16)
                                        dmap[ti][0] = em.at("dG")
                                    elif ti == 2:
                                        em.emit("gpsimd", lambda e: wload(e, wvo_b, wo_d, 0),
                                                waits=[("PE", pe_p)], inc="dG", k=16)
                                        d_wo_h[0] = em.at("dG")
                                for hl in range(2):
                                    if ti == 0:
                                        def scram(e, cc=cc, hl=hl, half=half):
                                            qdst = Q2A if hl == 0 else Q2B
                                            dst = qdst[hl * 64:(hl + 1) * 64, :].rearrange(
                                                "p (s c8 j) -> p s c8 j", s=16, c8=8, j=32)[
                                                :, :, cc:cc + 1, :].squeeze(2)
                                            src = ps_proj[hl * 64:(hl + 1) * 64,
                                                          half * 512:(half + 1) * 512].rearrange(
                                                "p (s j) -> p s j", s=16)
                                            return nc.vector.tensor_copy(dst, src)
                                    elif ti == 1:
                                        def scram(e, cc=cc, hl=hl, half=half):
                                            dst = K2[hl * 64:(hl + 1) * 64, :].rearrange(
                                                "p (s j c8) -> p s j c8", s=16, j=32, c8=8)[
                                                :, :, :, cc:cc + 1].squeeze(3)
                                            src = ps_proj[hl * 64:(hl + 1) * 64,
                                                          half * 512:(half + 1) * 512].rearrange(
                                                "p (s j) -> p s j", s=16)
                                            return nc.vector.tensor_copy(dst, src)
                                    else:
                                        def scram(e, cc=cc, hl=hl, half=half):
                                            dst = Vscr[hl * 64:(hl + 1) * 64, :].rearrange(
                                                "p (s j c8) -> p s j c8", s=16, j=32, c8=8)[
                                                :, :, :, cc:cc + 1].squeeze(3)
                                            src = ps_proj[hl * 64:(hl + 1) * 64,
                                                          half * 512:(half + 1) * 512].rearrange(
                                                "p (s j) -> p s j", s=16)
                                            return nc.vector.tensor_scalar(
                                                dst, src, vgcol[hl * 64:(hl + 1) * 64, 0:1],
                                                None, ALU.mult)
                                    wl2 = [("PE", pe_p)]
                                    if ti == 2:
                                        wl2.append(("dG", d_vg))
                                    em.emit("vector", scram, waits=wl2, inc="V")
                                v_evac_half[half] = em.at("V")
                    v_qk = em.at("V")
                pha.close()

                pe_proj_done = em.at("PE")
                em.emit("gpsimd", lambda e: e.dma_start(out=K2X[0:64, :], in_=K2[64:128, :]),
                        waits=[("V", v_qk), ("PE", pe_proj_done)], inc="dG", k=16)
                em.emit("gpsimd", lambda e: e.dma_start(out=K2X[64:128, :], in_=K2[0:64, :]),
                        inc="dG", k=16)
                em.emit("gpsimd", lambda e: e.dma_start(out=VscrX[0:64, :], in_=Vscr[64:128, :]),
                        inc="dG", k=16)
                d_k2x = em.at("dG")

                em.scope = "rec"
                with contextlib.ExitStack() as rec:
                    ps_S = rec.enter_context(nc.psum_tensor(f"ps_S{a}", [128, 1024], F32))
                    ps_pt = rec.enter_context(nc.psum_tensor(f"ps_pt{a}", [128, 2048], BF16))
                    ps_num = rec.enter_context(nc.psum_tensor(f"ps_num{a}", [128, 512], F32))
                    ps_dot = rec.enter_context(nc.psum_tensor(f"ps_dot{a}", [128, 512], F32))
                    ps_msc = rec.enter_context(nc.psum_tensor(f"ps_msc{a}", [128, 512], F32))
                    ps_rr = rec.enter_context(nc.psum_tensor(f"ps_rr{a}", [128, 512], F32))

                    v_comb = em.at("V")
                    v_e1_prev = 0
                    v_rr_prev = 0
                    v_skt_prev = 0
                    sc_p_prev = 0
                    sc_pw_seg = {}

                    def emit_swave(s):
                        nonlocal sc_p_prev
                        sc_pw_seg[s] = [0, 0]
                        for wave in range(2):
                            def swave(e, s=s, wave=wave):
                                for beta in (wave * 2, wave * 2 + 1):
                                    hl, ccg = beta // 2, beta % 2
                                    bk = (beta - wave * 2) * 512
                                    qsrc = Q2A if hl == 0 else Q2B
                                    for hlp in range(2):
                                        ksrc = K2 if hlp == hl else K2X
                                        nc.tensor.matmul(
                                            ps_S[:, bk + hlp * 256: bk + (hlp + 1) * 256],
                                            qsrc[:, s * 256 + ccg * 128: s * 256 + (ccg + 1) * 128],
                                            ksrc[:, s * 256:(s + 1) * 256],
                                            start=(hlp == 0), stop=False,
                                            skip_group_check=True)
                                    ins = nc.tensor.matmul(
                                        ps_S[:, bk:bk + 512], ident[:], masks_b[beta][:],
                                        start=False, stop=True, skip_group_check=True)
                                return ins
                            em.emit("tensor", swave,
                                    waits=[("SC", sc_p_prev), ("dG", d_k2x)], inc="PE")
                            pe_w = em.at("PE")

                            def pexp(e, wave=wave):
                                for bi in range(2):
                                    beta = wave * 2 + bi
                                    ins = nc.scalar.activation(
                                        P_sb[:, beta * 512:(beta + 1) * 512],
                                        ps_S[:, bi * 512:(bi + 1) * 512], AF.Exp,
                                        scale=0.125)
                                return ins
                            em.emit("scalar", pexp,
                                    waits=[("PE", pe_w)], inc="SC")
                            sc_p_prev = em.at("SC")
                            sc_pw_seg[s][wave] = sc_p_prev
                    sc_el_prev = 0
                    pe_pt_prev = 0
                    pe_numden_prev = 0
                    pe_dm_prev = 0
                    pe_rep_prev = 0
                    for s in range(SEG_RUN):
                        em.emit("vector", lambda e, s=s: nc.vector.tensor_scalar_min(
                            elu_t[0:64, :], Q2A[0:64, s * 256:(s + 1) * 256], 0.0),
                            waits=[("V", max(v_qk, v_zinit)), ("SC", sc_el_prev),
                                   ("PE", pe_numden_prev)], inc="V")
                        em.emit("vector", lambda e, s=s: nc.vector.tensor_scalar_min(
                            elu_t[64:128, :], Q2B[64:128, s * 256:(s + 1) * 256], 0.0),
                            inc="V")
                        v_el1 = em.at("V")
                        em.emit("scalar", lambda e: nc.scalar.activation(
                            elu_t[:], elu_t[:], AF.Exp),
                            waits=[("V", v_el1)], inc="SC")
                        sc_el = em.at("SC")
                        em.emit("vector", lambda e, s=s: nc.vector.scalar_tensor_tensor(
                            sq_scr[0:64, :], Q2A[0:64, s * 256:(s + 1) * 256], 0.0,
                            elu_t[0:64, :], ALU.max, ALU.add),
                            waits=[("SC", sc_el)], inc="V")
                        em.emit("vector", lambda e, s=s: nc.vector.scalar_tensor_tensor(
                            sq_scr[64:128, :], Q2B[64:128, s * 256:(s + 1) * 256], 0.0,
                            elu_t[64:128, :], ALU.max, ALU.add), inc="V")
                        em.emit("vector", lambda e, s=s: nc.vector.tensor_scalar_min(
                            elu_t[0:64, :], K2[0:64, s * 256:(s + 1) * 256], 0.0),
                            waits=[("PE", pe_pt_prev)], inc="V")
                        v_el2 = em.at("V")
                        em.emit("scalar", lambda e: nc.scalar.activation(
                            elu_t[0:64, :], elu_t[0:64, :], AF.Exp),
                            waits=[("V", v_el2)], inc="SC")
                        sc_el = em.at("SC")
                        em.emit("vector", lambda e, s=s: nc.vector.scalar_tensor_tensor(
                            skE[:], K2[0:64, s * 256:(s + 1) * 256], 0.0,
                            elu_t[0:64, :], ALU.max, ALU.add),
                            waits=[("SC", sc_el)], inc="V")
                        em.emit("vector", lambda e, s=s: nc.vector.tensor_scalar_min(
                            elu_t[0:64, :], K2X[0:64, s * 256:(s + 1) * 256], 0.0),
                            waits=[("dG", d_k2x)], inc="V")
                        v_el3 = em.at("V")
                        em.emit("scalar", lambda e: nc.scalar.activation(
                            elu_t[0:64, :], elu_t[0:64, :], AF.Exp),
                            waits=[("V", v_el3)], inc="SC")
                        sc_el = em.at("SC")
                        sc_el_prev = sc_el
                        em.emit("vector", lambda e, s=s: nc.vector.scalar_tensor_tensor(
                            skO[:], K2X[0:64, s * 256:(s + 1) * 256], 0.0,
                            elu_t[0:64, :], ALU.max, ALU.add),
                            waits=[("SC", sc_el)], inc="V")
                        v_sqsk = em.at("V")
                        if SEG_OPS <= 2:
                            continue

                        def mdup(e, a=a):
                            for hl, MD in ((0, M_dupA), (1, M_dupB)):
                                r0, r1 = hl * 64, hl * 64 + 64
                                for co in (0, 64):
                                    nc.vector.tensor_tensor(
                                        MD[r0:r1, co:co + 64],
                                        M_master[r0:r1, a * 65:a * 65 + 64],
                                        gamrep[r0:r1, a * 65:a * 65 + 64], ALU.mult)
                                ins = nc.vector.tensor_copy(
                                    MD[r0:r1, 128:129],
                                    M_master[r0:r1, a * 65 + 64:a * 65 + 65])
                            return ins
                        em.emit("vector", mdup,
                                waits=[("V", max(v_g65, v_mup)), ("PE", pe_numden_prev)],
                                inc="V")
                        v_mbf = em.at("V")
                        if SEG_OPS <= 3:
                            continue

                        if SEG_OPS <= 4:
                            continue

                        if s == 0 and SEG_OPS > 5:
                            emit_swave(0)
                        if SEG_OPS <= 5:
                            continue

                        for wv in range(2):
                            def ptrans(e, wv=wv):
                                for beta in (wv * 2, wv * 2 + 1):
                                    for kap in range(4):
                                        ins = nc.tensor.transpose(
                                            ps_pt[:, kap * 512 + beta * 128: kap * 512 + (beta + 1) * 128],
                                            P_sb[:, beta * 512 + kap * 128: beta * 512 + (kap + 1) * 128],
                                            ident[:])
                                return ins
                            em.emit("tensor", ptrans,
                                    waits=[("SC", sc_pw_seg[s][wv]), ("V", v_skt_prev)],
                                    inc="PE")
                        pe_pt = em.at("PE")
                        for kap in range(4):
                            em.emit("scalar", lambda e, kap=kap: nc.scalar.activation(
                                PT_sb[:, kap * 512:(kap + 1) * 512],
                                ps_pt[:, kap * 512:(kap + 1) * 512], AF.Copy),
                                waits=([("PE", pe_pt)] if kap == 0 else None), inc="SC")
                            if kap == 0:
                                sc_ptev_k0 = em.at("SC")
                        sc_ptev = em.at("SC")
                        if SEG_OPS <= 7:
                            continue

                        def sktr(e, s=s):
                            for kap in range(4):
                                hlp, jh = kap // 2, kap % 2
                                sksrc = skE if hlp == 0 else skO
                                nc.tensor.transpose(
                                    ps_pt[:, kap * 64:(kap + 1) * 64],
                                    sksrc[0:64, jh * 128:(jh + 1) * 128],
                                    ident[0:64, 0:64])
                            for kap in range(4):
                                hlp, jh = kap // 2, kap % 2
                                ins = nc.tensor.transpose(
                                    ps_pt[:, 256 + kap * 64:256 + (kap + 1) * 64],
                                    (Vscr if hlp == 0 else VscrX)[
                                        0:64, s * 256 + jh * 128: s * 256 + (jh + 1) * 128],
                                    ident[0:64, 0:64])
                            return ins
                        em.emit("tensor", sktr,
                                waits=[("SC", sc_ptev_k0), ("V", max(v_sqsk, v_qk)),
                                       ("dG", d_k2x)], inc="PE")
                        pe_skt = em.at("PE")
                        pe_pt_prev = pe_skt
                        em.emit("vector", lambda e: nc.vector.tensor_copy(
                            skT_sb[:], ps_pt[:, 0:512]),
                            waits=[("PE", pe_skt)], inc="V")
                        v_skt = em.at("V")
                        v_skt_prev = v_skt
                        if SEG_OPS <= 9:
                            continue

                        def dmm(e, s=s):
                            for ob in range(2):
                                for kap in range(4):
                                    nc.tensor.matmul(
                                        ps_rr[ob * 64:(ob + 1) * 64, 0:64],
                                        skT_sb[:, kap * 64:(kap + 1) * 64],
                                        skT_sb[:, 256 + kap * 64:256 + (kap + 1) * 64],
                                        start=(kap == 0), stop=False,
                                        skip_group_check=True)
                                for kap in range(4):
                                    ins = nc.tensor.matmul(
                                        ps_rr[ob * 64:(ob + 1) * 64, 64:65],
                                        skT_sb[:, kap * 64:(kap + 1) * 64],
                                        ones_b[:, 0:1],
                                        start=False, stop=(kap == 3),
                                        skip_group_check=True)
                            return ins
                        em.emit("tensor", dmm,
                                waits=[("V", max(v_skt, v_rr_prev))], inc="PE")
                        pe_dm = em.at("PE")
                        pe_dm_prev = pe_dm
                        em.emit("vector", lambda e, a=a: nc.vector.tensor_tensor(
                            M_master[:, a * 65:(a + 1) * 65], M_master[:, a * 65:(a + 1) * 65],
                            ps_rr[:, 0:65], ALU.add),
                            waits=[("PE", pe_dm), ("V", v_mbf)], inc="V")
                        v_mup = em.at("V")

                        def numden(e, s=s):
                            for hl, MD in ((0, M_dupA), (1, M_dupB)):
                                nc.tensor.matmul(
                                    ps_num[:, hl * 256:(hl + 1) * 256],
                                    MD[:, 0:128], sq_scr[:, :],
                                    start=True, stop=True, skip_group_check=True)
                            for hl, MD in ((0, M_dupA), (1, M_dupB)):
                                ins = nc.tensor.matmul(
                                    ps_msc[0:1, hl * 256:(hl + 1) * 256],
                                    MD[:, 128:129], sq_scr[:, :],
                                    start=True, stop=True, skip_group_check=True)
                            return ins
                        em.emit("tensor", numden,
                                waits=[("V", max(v_mbf, v_sqsk, v_e1_prev))], inc="PE")
                        pe_num = em.at("PE")
                        pe_numden_prev = pe_num

                        def lammm(e, s=s):
                            for kap in range(4):
                                ins = nc.tensor.matmul(
                                    ps_rr[0:1, :], ones_b[:, 0:1],
                                    PT_sb[:, kap * 512:(kap + 1) * 512],
                                    start=(kap == 0), stop=(kap == 3),
                                    skip_group_check=True)
                            return ins
                        em.emit("tensor", lammm,
                                waits=[("V", max(v_comb, v_e1_prev, v_skt, v_mup)),
                                       ("SC", sc_ptev)],
                                inc="PE")
                        pe_lam = em.at("PE")

                        def dotmm(e, s=s):
                            for ob in range(2):
                                for kap in range(4):
                                    ins = nc.tensor.matmul(
                                        ps_dot[ob * 64:(ob + 1) * 64, :],
                                        skT_sb[:, 256 + kap * 64:256 + (kap + 1) * 64],
                                        PT_sb[:, kap * 512:(kap + 1) * 512],
                                        start=(kap == 0), stop=(kap == 3),
                                        skip_group_check=True)
                            return ins
                        em.emit("tensor", dotmm, inc="PE")
                        pe_dot = em.at("PE")
                        if s + 1 < SEG_RUN:
                            emit_swave(s + 1)
                        if SEG_OPS <= 10:
                            continue

                        def recips_a(e):
                            with nc.allow_low_precision(reason="bf16 matmul operand"):
                                return nc.vector.reciprocal(rdl_a[:], ps_msc[0:1, :])
                        def recips_b(e):
                            with nc.allow_low_precision(reason="bf16 matmul operand"):
                                return nc.vector.reciprocal(rdl_b[:], ps_rr[0:1, :])
                        em.emit("vector", recips_a, waits=[("PE", pe_num)], inc="V")
                        em.emit("vector", recips_b, waits=[("PE", pe_lam)], inc="V")
                        v_rdl = em.at("V")
                        if SEG_OPS <= 11:
                            continue

                        def rrmm(e):
                            nc.tensor.matmul(ps_rr[:], ones_b[0:1, :], rdl_a[:],
                                             start=True, stop=True, skip_group_check=True)
                            return nc.tensor.matmul(ps_msc[:], ones_b[0:1, :], rdl_b[:],
                                                    start=True, stop=True,
                                                    skip_group_check=True)
                        em.emit("tensor", rrmm,
                                waits=[("V", v_rdl)], inc="PE")
                        pe_rep = em.at("PE")
                        pe_rep_prev = pe_rep
                        if SEG_OPS <= 12:
                            continue
                        em.emit("vector", lambda e: nc.vector.tensor_copy(
                            rr_sb[:], ps_rr[:]), waits=[("PE", pe_rep)], inc="V")
                        em.emit("vector", lambda e: nc.vector.tensor_copy(
                            rl_sb[:], ps_msc[:]), inc="V")
                        v_rr = em.at("V")
                        v_rr_prev = v_rr
                        em.emit("vector", lambda e: nc.vector.tensor_tensor(
                            t1_sb[:], ps_num[:], rr_sb[:], ALU.mult), inc="V")
                        v_e1 = em.at("V")
                        v_e1_prev = v_e1
                        em.emit("vector", lambda e: nc.vector.tensor_tensor(
                            t2_sb[:], ps_dot[:], rl_sb[:], ALU.mult), inc="V")
                        em.emit("vector", lambda e, s=s: nc.vector.tensor_tensor(
                            att_bf[:, (s % 8) * 512:(s % 8 + 1) * 512], t1_sb[:], t2_sb[:],
                            ALU.add), inc="V")
                        v_comb = em.at("V")
                        if DBG_SEG0 == 1 and a == 0 and s == 0:
                            dumps = [
                                (outT[0:128, 0:512], att_bf[:, 0:512]),
                                (outT[0:128, 512:1024], t1_sb[:]),
                                (outT[0:128, 1024:1536], t2_sb[:]),
                                (outT[0:128, 1536:2048], skT_sb[:]),
                                (outT[128:256, 0:512], rr_sb[:]),
                                (outT[128:256, 512:1024], rl_sb[:]),
                                (outT[128:256, 1024:1280], sq_scr[:]),
                                (outT[128:256, 1280:1792], P_sb[:, 0:512]),
                                (outT[256:384, :], PT_sb[:]),
                                (outT[384:512, 0:129], M_dupA[:, 0:129]),
                                (outT[384:512, 512:641], M_dupB[:, 0:129]),
                                (outT[384:512, 1024:1089], gamrep[:, 0:65]),
                            ]
                            for i, (dst, src) in enumerate(dumps):
                                em.emit("gpsimd", lambda e, dst=dst, src=src: e.dma_start(
                                    out=dst, in_=src),
                                    waits=([("V", v_comb)] if i == 0 else None),
                                    inc="dO", k=16)
                            em.emit("gpsimd", lambda e: e.dma_start(
                                out=outT[512:640, 0:65], in_=M_master[:, 0:65]),
                                inc="dO", k=16)
                            d_dump = em.at("dO")
                            em.emit("vector", lambda e: e.engine_nop(),
                                    waits=[("dO", d_dump)], inc="V")
                            break
                        if SEG_OPS <= 13:
                            continue

                        if s in (7, 15):
                            base = 0 if s == 7 else 256
                            for h in range(16):
                                hl = h % 2
                                cc = h // 2
                                ccg, ccl = cc // 4, cc % 4
                                moff = hl * 256 + ccg * 128 + ccl * 32

                                def unscr(e, h=h, hl=hl, base=base, moff=moff):
                                    ch = h // 2
                                    dst = A_scr[hl * 64:(hl + 1) * 64,
                                                ch * 512 + base: ch * 512 + base + 256
                                                ].rearrange("p (s j) -> p s j", s=8)
                                    src = att_bf[hl * 64:(hl + 1) * 64, :].rearrange(
                                        "p (s m) -> p s m", s=8)[:, :, moff:moff + 32]
                                    return nc.vector.tensor_copy(dst, src)
                                em.emit("vector", unscr, inc="V")
                            v_comb = em.at("V")
                    v_unscr = em.at("V")

                if DBG_SEG0 == 2 and a == 0:
                    dumps2 = [
                        (outT[0:128, :], att_bf[:, 0:2048]),
                        (outT[128:256, :], att_bf[:, 2048:4096]),
                        (outT[256:384, :], A_scr[:, 0:2048]),
                        (outT[384:512, :], A_scr[:, 2048:4096]),
                    ]
                    for i, (dst, src) in enumerate(dumps2):
                        em.emit("gpsimd", lambda e, dst=dst, src=src: e.dma_start(
                            out=dst, in_=src),
                            waits=([("V", v_unscr)] if i == 0 else None),
                            inc="dO", k=16)
                    d_dump2 = em.at("dO")
                    em.emit("vector", lambda e: e.engine_nop(),
                            waits=[("dO", d_dump2)], inc="V")

                em.scope = "wo"
                if DBG_SEG0:
                    continue
                with nc.psum_tensor(f"ps_wo{a}", [128, 1024], F32) as ps_wo:
                    v_woev = [v_unscr, v_unscr]
                    for oc in range(8):
                        half = oc % 2
                        wh = oc // 4
                        ocl = oc % 4

                        def womm(e, ocl=ocl, half=half):
                            for ch in range(8):
                                ins = nc.tensor.matmul(
                                    ps_wo[:, half * 512:(half + 1) * 512],
                                    wvo_b[:, ch * 512 + ocl * 128: ch * 512 + (ocl + 1) * 128],
                                    A_scr[:, ch * 512:(ch + 1) * 512],
                                    start=(ch == 0), stop=(ch == 7))
                            return ins
                        wl = [("V", max(v_unscr, v_woev[half]))]
                        if ocl == 0:
                            wl.append(("dG", d_wo_h[wh]))
                        em.emit("tensor", womm, waits=wl, inc="PE")
                        pe_wo = em.at("PE")
                        if oc == 3:
                            em.emit("gpsimd", lambda e: wload(e, wvo_b, wo_d, 1),
                                    waits=[("PE", pe_wo)], inc="dG", k=16)
                            d_wo_h[1] = em.at("dG")

                        def woev(e, oc=oc, half=half, a=a):
                            dst = att_out[:].rearrange(
                                "p (o s ar j) -> p o s ar j", o=8, s=16, ar=4)[
                                :, oc:oc + 1, :, a:a + 1, :].squeeze(1).squeeze(2)
                            src = ps_wo[:, half * 512:(half + 1) * 512].rearrange(
                                "p (s j) -> p s j", s=16)
                            return nc.vector.tensor_copy(dst, src)
                        em.emit("vector", woev, waits=[("PE", pe_wo)], inc="V")
                        v_woev[half] = em.at("V")
                pe_prev_a = em.at("PE")
            v_attn = em.at("V")
        p2w.close()

        if STOP_AFTER == 'attn':
            if not DBG_SEG0:
                _finale(nc, em, att_out, outT, v_attn)
            _replay(nc, em)
            return nc

        em.scope = "ffn"
        with contextlib.ExitStack() as ph:
            acc = sbt(ph, "acc", [128, 8 * TL], F32)
            combT = sbt(ph, "combT", [16, TL], BF16)

            with contextlib.ExitStack() as lgs:
                lg_sb = sbt(lgs, "lg_sb", [16, TL], F32)
                lgt = sbt(lgs, "lgt", [128, 256], F32)
                est = sbt(lgs, "est", [128, 256], F32)
                m1s = sbt(lgs, "m1s", [128, 16], F32)
                nm1 = sbt(lgs, "nm1", [128, 16], F32)
                ssum = sbt(lgs, "ssum", [128, 16], F32)
                rs = sbt(lgs, "rs", [128, 16], F32)
                eqm = sbt(lgs, "eqm", [128, 256], F32)
                m2s = sbt(lgs, "m2s", [128, 16], F32)
                keep = sbt(lgs, "keep", [128, 256], F32)
                comb = sbt(lgs, "comb", [128, 256], F32)
                comb_b = sbt(lgs, "comb_b", [128, 256], BF16)

                with nc.psum_tensor("ps_lg", [16, TL], F32) as ps_lg:
                    def lgmm(e):
                        for kc in range(8):
                            for nt in range(4):
                                ins = nc.tensor.matmul(
                                    ps_lg[:, nt * 512:(nt + 1) * 512],
                                    gw_b[:, kc * NE:(kc + 1) * NE],
                                    xn[:, kc * TL + nt * 512: kc * TL + (nt + 1) * 512],
                                    start=(kc == 0), stop=(kc == 7), skip_group_check=True)
                        return ins
                    em.emit("tensor", lgmm, waits=[("V", v_attn)], inc="PE")
                    pe_lg = em.at("PE")
                    em.emit("vector", lambda e: nc.vector.tensor_copy(lg_sb[:], ps_lg[:]),
                            waits=[("PE", pe_lg)], inc="V")
                    v_lg = em.at("V")
                with nc.psum_tensor("ps_lgt", [128, 256], F32) as ps_lgt:
                    def lgtr(e):
                        for tc in range(16):
                            ins = nc.tensor.transpose(
                                ps_lgt[:, tc * 16:(tc + 1) * 16],
                                lg_sb[:, tc * 128:(tc + 1) * 128], identf[:])
                        return ins
                    em.emit("tensor", lgtr, waits=[("V", v_lg)], inc="PE")
                    pe_lgt = em.at("PE")
                    em.emit("vector", lambda e: nc.vector.tensor_copy(lgt[:], ps_lgt[:]),
                            waits=[("PE", pe_lgt)], inc="V")

                    def m1r(e):
                        for tc in range(16):
                            ins = nc.vector.reduce_max(
                                m1s[:, tc:tc + 1], lgt[:, tc * 16:(tc + 1) * 16], AX.X)
                        return ins
                    em.emit("vector", m1r, inc="V")
                    em.emit("vector", lambda e: nc.vector.tensor_scalar(
                        nm1[:], m1s[:], -1.0, None, ALU.mult), inc="V")
                    em.emit("vector", lambda e: e.drain() or e.engine_nop(), inc="V")
                    v_m1 = em.at("V")

                    def sexp(e):
                        for tc in range(16):
                            ins = nc.scalar.activation(
                                est[:, tc * 16:(tc + 1) * 16], lgt[:, tc * 16:(tc + 1) * 16],
                                AF.Exp, bias=nm1[:, tc:tc + 1], accum_out=ssum[:, tc:tc + 1])
                        return ins
                    em.emit("scalar", sexp, waits=[("V", v_m1)], inc="SC")
                    sc_e = em.at("SC")
                    em.emit("vector", lambda e: nc.vector.reciprocal(rs[:], ssum[:]),
                            waits=[("SC", sc_e)], inc="V")

                    def eqms(e):
                        for tc in range(16):
                            ins = nc.vector.tensor_scalar(
                                eqm[:, tc * 16:(tc + 1) * 16], lgt[:, tc * 16:(tc + 1) * 16],
                                m1s[:, tc:tc + 1], None, ALU.is_equal)
                        return ins
                    em.emit("vector", eqms, inc="V")
                    em.emit("vector", lambda e: nc.vector.scalar_tensor_tensor(
                        eqm[:], eqm[:], NEGC, lgt[:], ALU.mult, ALU.add), inc="V")

                    def m2r(e):
                        for tc in range(16):
                            ins = nc.vector.reduce_max(
                                m2s[:, tc:tc + 1], eqm[:, tc * 16:(tc + 1) * 16], AX.X)
                        return ins
                    em.emit("vector", m2r, inc="V")
                    em.emit("vector", lambda e: e.drain() or e.engine_nop(), inc="V")

                    def keeps(e):
                        for tc in range(16):
                            ins = nc.vector.tensor_scalar(
                                keep[:, tc * 16:(tc + 1) * 16], lgt[:, tc * 16:(tc + 1) * 16],
                                m2s[:, tc:tc + 1], None, ALU.is_ge)
                        return ins
                    em.emit("vector", keeps, inc="V")

                    def combs(e):
                        for tc in range(16):
                            ins = nc.vector.scalar_tensor_tensor(
                                comb[:, tc * 16:(tc + 1) * 16], est[:, tc * 16:(tc + 1) * 16],
                                rs[:, tc:tc + 1], keep[:, tc * 16:(tc + 1) * 16],
                                ALU.mult, ALU.mult)
                        return ins
                    em.emit("vector", combs, inc="V")
                    em.emit("vector", lambda e: nc.vector.tensor_copy(comb_b[:], comb[:]), inc="V")
                    v_comb2 = em.at("V")
                with nc.psum_tensor("ps_cbt", [16, TL], BF16) as ps_cbt:
                    def cbtr(e):
                        for tc in range(16):
                            ins = nc.tensor.transpose(
                                ps_cbt[:, tc * 128:(tc + 1) * 128],
                                comb_b[:, tc * 16:(tc + 1) * 16], ident[:])
                        return ins
                    em.emit("tensor", cbtr, waits=[("V", v_comb2)], inc="PE")
                    pe_cbt = em.at("PE")
                    em.emit("vector", lambda e: nc.vector.tensor_copy(combT[:], ps_cbt[:]),
                            waits=[("PE", pe_cbt)], inc="V")
                    v_cbt = em.at("V")

            em.scope = "ffn_sh"
            with contextlib.ExitStack() as shs:
                wsg_b = sbt(shs, "wsg_b", [128, 8 * D], BF16)
                wsu_b = sbt(shs, "wsu_b", [128, 8 * D], BF16)
                wsd_b = sbt(shs, "wsd_b", [128, 8 * D], BF16)
                act_sh = sbt(shs, "act_sh", [128, 4 * 1024], BF16)
                sg_t = sbt(shs, "sg_t", [128, 1024], BF16)
                first = True
                for dst, src in ((wsg_b, wsg_d), (wsu_b, wsu_d), (wsd_b, wsd_d)):
                    em.emit("gpsimd", lambda e, dst=dst, src=src: e.dma_start(
                        out=dst[:].rearrange("p (r c) -> p r c", r=8),
                        in_=src.rearrange("(r p) c -> p r c", p=128)),
                        waits=([("PE", em.at("PE")), ("V", em.at("V"))] if first else None),
                        inc="dG", k=16)
                    first = False
                d_ws = em.at("dG")

                with contextlib.ExitStack() as pss:
                    ps_g = pss.enter_context(nc.psum_tensor("ps_sg", [128, 1024], F32))
                    ps_u = pss.enter_context(nc.psum_tensor("ps_su", [128, 1024], F32))
                    ps_d2 = pss.enter_context(nc.psum_tensor("ps_sd2", [128, 1024], F32))
                    for th in range(2):
                        for ih in range(2):
                            for icl in range(4):
                                ic = ih * 4 + icl

                                def gumm(e, th=th, ic=ic):
                                    for kc in range(8):
                                        for nh in range(2):
                                            nc.tensor.matmul(
                                                ps_g[:, nh * 512:(nh + 1) * 512],
                                                wsg_b[:, kc * D + ic * 128: kc * D + (ic + 1) * 128],
                                                xn[:, kc * TL + th * 1024 + nh * 512:
                                                   kc * TL + th * 1024 + (nh + 1) * 512],
                                                start=(kc == 0), stop=(kc == 7),
                                                skip_group_check=True)
                                    for kc in range(8):
                                        for nh in range(2):
                                            ins = nc.tensor.matmul(
                                                ps_u[:, nh * 512:(nh + 1) * 512],
                                                wsu_b[:, kc * D + ic * 128: kc * D + (ic + 1) * 128],
                                                xn[:, kc * TL + th * 1024 + nh * 512:
                                                   kc * TL + th * 1024 + (nh + 1) * 512],
                                                start=(kc == 0), stop=(kc == 7),
                                                skip_group_check=True)
                                    return ins
                                wl = [("V", em.at("V")), ("SC", em.at("SC"))]
                                if th == 0 and ic == 0:
                                    wl.append(("dG", d_ws))
                                em.emit("tensor", gumm, waits=wl, inc="PE")
                                pe_gu = em.at("PE")
                                em.emit("scalar", lambda e: nc.scalar.activation(
                                    sg_t[:], ps_g[:], AF.Silu),
                                    waits=[("PE", pe_gu)], inc="SC")
                                sc_sg = em.at("SC")
                                em.emit("vector", lambda e, icl=icl: nc.vector.tensor_tensor(
                                    act_sh[:, icl * 1024:(icl + 1) * 1024], sg_t[:], ps_u[:],
                                    ALU.mult),
                                    waits=[("SC", sc_sg)], inc="V")
                            v_actd = em.at("V")
                            v_dev = v_actd
                            for oc in range(8):
                                def sdmm(e, oc=oc, ih=ih):
                                    for icl in range(4):
                                        ic = ih * 4 + icl
                                        for nh in range(2):
                                            ins = nc.tensor.matmul(
                                                ps_d2[:, nh * 512:(nh + 1) * 512],
                                                wsd_b[:, ic * D + oc * 128: ic * D + (oc + 1) * 128],
                                                act_sh[:, icl * 1024 + nh * 512:
                                                       icl * 1024 + (nh + 1) * 512],
                                                start=(icl == 0), stop=(icl == 3),
                                                skip_group_check=True)
                                    return ins
                                em.emit("tensor", sdmm, waits=[("V", v_dev)], inc="PE")
                                pe_sd = em.at("PE")
                                if ih == 0:
                                    em.emit("vector", lambda e, oc=oc, th=th: nc.vector.tensor_copy(
                                        acc[:, oc * TL + th * 1024: oc * TL + (th + 1) * 1024],
                                        ps_d2[:]),
                                        waits=[("PE", pe_sd)], inc="V")
                                else:
                                    em.emit("vector", lambda e, oc=oc, th=th: nc.vector.tensor_tensor(
                                        acc[:, oc * TL + th * 1024: oc * TL + (th + 1) * 1024],
                                        acc[:, oc * TL + th * 1024: oc * TL + (th + 1) * 1024],
                                        ps_d2[:], ALU.add),
                                        waits=[("PE", pe_sd)], inc="V")
                                v_dev = em.at("V")

            em.scope = "ffn_expw"
            with contextlib.ExitStack() as exs:
                wgb = [sbt(exs, f"wgb{i}", [128, 8 * EI], BF16) for i in range(2)]
                wub = [sbt(exs, f"wub{i}", [128, 8 * EI], BF16) for i in range(2)]
                wdb = [sbt(exs, f"wdb{i}", [128, 4 * D], BF16) for i in range(2)]
                act_e = sbt(exs, "act_e", [128, 4 * 1024], BF16)
                sg_e = sbt(exs, "sg_e", [128, 1024], BF16)
                uc_e = sbt(exs, "uc_e", [128, 1024], BF16)
                crep = sbt(exs, "crep", [128, 2 * TL], BF16)

                d_exp = [0] * NE
                d_crep = [0] * NE
                pe_wuse = [0] * NE
                v_uexp = [0] * NE

                def emit_exp_load(ei):
                    sl = ei % 2
                    wl = ([("PE", pe_wuse[ei - 2])] if ei >= 2
                          else [("PE", em.at("PE")), ("V", em.at("V"))])
                    em.emit("gpsimd", lambda e, ei=ei, sl=sl: e.dma_start(
                        out=wgb[sl][:].rearrange("p (r c) -> p r c", r=8),
                        in_=wg_d[ei].rearrange("(r p) c -> p r c", p=128)),
                        waits=wl, inc="dG", k=16)
                    em.emit("gpsimd", lambda e, ei=ei, sl=sl: e.dma_start(
                        out=wub[sl][:].rearrange("p (r c) -> p r c", r=8),
                        in_=wu_d[ei].rearrange("(r p) c -> p r c", p=128)), inc="dG", k=16)
                    em.emit("gpsimd", lambda e, ei=ei, sl=sl: e.dma_start(
                        out=wdb[sl][:].rearrange("p (r c) -> p r c", r=4),
                        in_=wd_d[ei].rearrange("(r p) c -> p r c", p=128)), inc="dG", k=16)
                    d_exp[ei] = em.at("dG")

                def emit_crep_build(ei):
                    sl = ei % 2
                    base = sl * TL
                    wl = [("V", v_cbt)]
                    if ei >= 2:
                        wl.append(("V", v_uexp[ei - 2]))
                    em.emit("sync", lambda e, ei=ei, base=base: e.dma_start(
                        out=crep[0:1, base:base + TL], in_=combT[ei:ei + 1, :]),
                        waits=wl, inc="dS", k=16)
                    for step in (1, 2, 4, 8, 16, 32, 64):
                        dprev = em.at("dS")
                        em.emit("sync", lambda e, step=step, base=base: e.dma_start(
                            out=crep[step:2 * step, base:base + TL],
                            in_=crep[0:step, base:base + TL]),
                            waits=[("dS", dprev)], inc="dS", k=16)
                    d_crep[ei] = em.at("dS")

                emit_exp_load(0)
                if NE_RUN > 1:
                    emit_exp_load(1)

                with contextlib.ExitStack() as pse:
                    ps_g = pse.enter_context(nc.psum_tensor("ps_eg", [128, 1024], F32))
                    ps_u = pse.enter_context(nc.psum_tensor("ps_eu", [128, 1024], F32))
                    ps_d2 = pse.enter_context(nc.psum_tensor("ps_ed", [128, 1024], F32))

                    em.scope = "ffn_exp"
                    half_user = {0: (0, 0), 1: (0, 0)}
                    sg_user = {0: 0, 1: 0}
                    act_user = {0: 0, 1: 0}
                    gp_act = {}
                    v_dev = [em.at("V"), em.at("V")]
                    pending_ed = None

                    def emit_gu(ei, tp):
                        sl = ei % 2
                        for ic in range(4):
                            h = ic % 2

                            def gumm(e, sl=sl, tp=tp, ic=ic, h=h):
                                for kc in range(8):
                                    nc.tensor.matmul(
                                        ps_g[:, h * 512:(h + 1) * 512],
                                        wgb[sl][:, kc * EI + ic * 128: kc * EI + (ic + 1) * 128],
                                        xn[:, kc * TL + tp * 512: kc * TL + (tp + 1) * 512],
                                        start=(kc == 0), stop=(kc == 7),
                                        skip_group_check=True)
                                for kc in range(8):
                                    ins = nc.tensor.matmul(
                                        ps_u[:, h * 512:(h + 1) * 512],
                                        wub[sl][:, kc * EI + ic * 128: kc * EI + (ic + 1) * 128],
                                        xn[:, kc * TL + tp * 512: kc * TL + (tp + 1) * 512],
                                        start=(kc == 0), stop=(kc == 7),
                                        skip_group_check=True)
                                return ins
                            sc_u, v_u = half_user[h]
                            wl = [("SC", sc_u), ("V", v_u)]
                            if tp == 0 and ic == 0:
                                wl.append(("dG", d_exp[ei]))
                            em.emit("tensor", gumm, waits=wl, inc="PE")
                            pe_gu = em.at("PE")
                            em.emit("scalar", lambda e, h=h: nc.scalar.activation(
                                sg_e[:, h * 512:(h + 1) * 512],
                                ps_g[:, h * 512:(h + 1) * 512], AF.Silu),
                                waits=[("PE", pe_gu), ("GP", sg_user[h])], inc="SC")
                            sc_sg = em.at("SC")
                            wl2 = [("PE", pe_gu), ("GP", sg_user[h])]
                            if tp == 0 and ic == 0:
                                wl2.append(("dS", d_crep[ei]))

                            def ucop(e, h=h, tp=tp, sl=sl):
                                return nc.vector.scalar_tensor_tensor(
                                    uc_e[:, h * 512:(h + 1) * 512],
                                    ps_u[:, h * 512:(h + 1) * 512], 0.0,
                                    crep[:, sl * TL + tp * 512: sl * TL + (tp + 1) * 512],
                                    ALU.add, ALU.mult)
                            em.emit("vector", ucop, waits=wl2, inc="V")
                            v_uc = em.at("V")
                            half_user[h] = (sc_sg, v_uc)
                            asl = tp % 2
                            em.emit("gpsimd", lambda e, h=h, ic=ic, asl=asl: e.tensor_tensor(
                                act_e[:, asl * 2048 + ic * 512: asl * 2048 + (ic + 1) * 512],
                                sg_e[:, h * 512:(h + 1) * 512],
                                uc_e[:, h * 512:(h + 1) * 512], ALU.mult),
                                waits=[("SC", sc_sg), ("V", v_uc),
                                       ("PE", act_user[asl])], inc="GP")
                            sg_user[h] = em.at("GP")
                        gp_act[(ei, tp)] = em.at("GP")
                        if tp == 3:
                            v_uexp[ei] = em.at("V")

                    def emit_ed(ei, tp):
                        sl = ei % 2
                        asl = tp % 2
                        for oc in range(8):
                            h = oc % 2

                            def edmm(e, sl=sl, asl=asl, oc=oc, h=h):
                                for ic in range(4):
                                    ins = nc.tensor.matmul(
                                        ps_d2[:, h * 512:(h + 1) * 512],
                                        wdb[sl][:, ic * D + oc * 128: ic * D + (oc + 1) * 128],
                                        act_e[:, asl * 2048 + ic * 512: asl * 2048 + (ic + 1) * 512],
                                        start=(ic == 0), stop=(ic == 3),
                                        skip_group_check=True)
                                return ins
                            wl = [("V", v_dev[h])]
                            if oc == 0:
                                wl.append(("GP", gp_act[(ei, tp)]))
                            em.emit("tensor", edmm, waits=wl, inc="PE")
                            pe_ed = em.at("PE")

                            def dacc(e, oc=oc, tp=tp, h=h):
                                return nc.vector.tensor_tensor(
                                    acc[:, oc * TL + tp * 512: oc * TL + (tp + 1) * 512],
                                    acc[:, oc * TL + tp * 512: oc * TL + (tp + 1) * 512],
                                    ps_d2[:, h * 512:(h + 1) * 512], ALU.add)
                            em.emit("vector", dacc, waits=[("PE", pe_ed)], inc="V")
                            v_dev[h] = em.at("V")
                        act_user[asl] = em.at("PE")

                    emit_crep_build(0)
                    if NE_RUN > 1:
                        emit_crep_build(1)
                    for ei in range(NE_RUN):
                        emit_gu(ei, 0)
                        if pending_ed is not None:
                            emit_ed(*pending_ed)
                            pe_wuse[pending_ed[0]] = em.at("PE")
                            if pending_ed[0] + 2 < NE_RUN:
                                emit_exp_load(pending_ed[0] + 2)
                                emit_crep_build(pending_ed[0] + 2)
                        emit_gu(ei, 1)
                        emit_ed(ei, 0)
                        emit_gu(ei, 2)
                        emit_ed(ei, 1)
                        emit_gu(ei, 3)
                        emit_ed(ei, 2)
                        pending_ed = (ei, 3)
                    emit_ed(*pending_ed)
                v_ffn = em.at("V")

            em.scope = "p4"
            with contextlib.ExitStack() as ph4:
                xf2 = [sbt(ph4, f"xf2_{i}", [128, TL], F32) for i in range(2)]
                osb = [sbt(ph4, f"osb{i}", [128, TL], F32) for i in range(2)]
                v_of = [v_ffn, v_ffn]
                d_of = [0, 0]
                st_sem = ["dO", "dO2"]
                for oc in range(8):
                    half = oc % 2
                    q, qs = (("sync", "dS") if half == 0 else ("gpsimd", "dG"))
                    em.emit(q, lambda e, oc=oc, half=half: e.dma_start(
                        out=xf2[half][:], in_=xT[oc * 128:(oc + 1) * 128, :]),
                        waits=[("V", v_of[half])], inc=qs, k=16)
                    d_x2 = em.at(qs)
                    em.emit("vector", lambda e, oc=oc, half=half: nc.vector.tensor_tensor(
                        osb[half][:], xf2[half][:], att_out[:, oc * TL:(oc + 1) * TL],
                        ALU.add),
                        waits=[(qs, d_x2), (st_sem[half], d_of[half]), ("V", v_ffn)],
                        inc="V")
                    em.emit("vector", lambda e, oc=oc, half=half: nc.vector.tensor_tensor(
                        osb[half][:], osb[half][:], acc[:, oc * TL:(oc + 1) * TL],
                        ALU.add), inc="V")
                    v_of[half] = em.at("V")
                    em.emit(q, lambda e, oc=oc, half=half: e.dma_start(
                        out=outT[oc * 128:(oc + 1) * 128, :], in_=osb[half][:]),
                        waits=[("V", v_of[half])], inc=st_sem[half], k=16)
                    d_of[half] = em.at(st_sem[half])
                d_all = em.at("dO")
                d_all2 = em.at("dO2")
                em.emit("sync", lambda e: e.nop(),
                        waits=[("dO", d_all), ("dO2", d_all2)], inc=None)

        _replay(nc, em)

    return nc


def _finale(nc, em, src_bf, outT, v_wait):
    for oc in range(8):
        em.emit("gpsimd", lambda e, oc=oc: e.dma_start(
            out=outT[oc * 128:(oc + 1) * 128, :],
            in_=src_bf[:, oc * TL:(oc + 1) * TL]),
            waits=([("V", v_wait)] if oc == 0 else None), inc="dO", k=16)
    d_all = em.at("dO")
    em.emit("gpsimd", lambda e: e.nop(), waits=[("dO", d_all)], inc=None)


def _run_ops(nc, e, ops):
    for sc, f in ops:
        if sc:
            with nc.named_scope(sc):
                f(e)
        else:
            f(e)


def _replay(nc, em):
    with nc.Block() as block:
        @block.sync
        def _(e):
            _run_ops(nc, e, em.ops["sync"])

        @block.gpsimd
        def _(e):
            _run_ops(nc, e, em.ops["gpsimd"])

        @block.scalar
        def _(e):
            _run_ops(nc, e, em.ops["scalar"])

        @block.vector
        def _(e):
            _run_ops(nc, e, em.ops["vector"])

        @block.tensor
        def _(e):
            _run_ops(nc, e, em.ops["tensor"])


_NC_CACHE = None


def kernel(L0_attn_x, L0_attn_Wq, L0_attn_Wk, L0_attn_Wv, L0_attn_Wo,
           L0_attn_betas, L0_attn_memory0, L0_attn_z0,
           L0_ffn_gate_w, L0_ffn_w_gate, L0_ffn_w_up, L0_ffn_w_down,
           L0_ffn_ws_gate, L0_ffn_ws_up, L0_ffn_ws_down):
    global _NC_CACHE
    x = np.asarray(L0_attn_x, dtype=np.float32)
    assert x.shape == (2, 8192, 1024)

    cont = np.ascontiguousarray
    shared = {
        "wq": cont(np.asarray(L0_attn_Wq, np.float32)),
        "wk": cont(np.asarray(L0_attn_Wk, np.float32)),
        "wv": cont(np.asarray(L0_attn_Wv, np.float32)),
        "wo": cont(np.asarray(L0_attn_Wo, np.float32)),
        "gwT": cont(np.asarray(L0_ffn_gate_w, np.float32).T),
        "wg": cont(np.asarray(L0_ffn_w_gate, np.float32)),
        "wu": cont(np.asarray(L0_ffn_w_up, np.float32)),
        "wd": cont(np.asarray(L0_ffn_w_down, np.float32)),
        "wsg": cont(np.asarray(L0_ffn_ws_gate, np.float32)),
        "wsu": cont(np.asarray(L0_ffn_ws_up, np.float32)),
        "wsd": cont(np.asarray(L0_ffn_ws_down, np.float32)),
    }
    betas = np.asarray(L0_attn_betas, np.float32)
    mem0 = np.asarray(L0_attn_memory0, np.float32)
    z0 = np.asarray(L0_attn_z0, np.float32)
    maug_full = np.concatenate([mem0[0], np.swapaxes(z0[0], 1, 2)], axis=2)

    in_maps = []
    for c in range(8):
        b, p = c // 4, c % 4
        xs = x[b].reshape(16, 16, 32, 1024)[:, 4 * p:4 * p + 4]
        m = dict(shared)
        m["xT"] = cont(xs.reshape(2048, 1024).T)
        m["betas"] = cont(betas[0, 4 * p:4 * p + 4, 0, :].reshape(1, 256))
        m["maug"] = cont(maug_full[4 * p:4 * p + 4])
        in_maps.append(m)

    if _NC_CACHE is None:
        _NC_CACHE = build_program()
    res = run_bass_kernel_spmd(_NC_CACHE, in_maps, core_ids=list(range(8)))

    out = np.empty((2, 8192, 1024), np.float32)
    ov = out.reshape(2, 16, 16, 32, 1024)
    for c in range(8):
        b, p = c // 4, c % 4
        o = np.asarray(res.results[c]["outT"], np.float32)
        ov[b, :, 4 * p:4 * p + 4] = o.T.reshape(16, 4, 32, 1024)
    return out


# version 11 (final)
# speedup vs baseline: 1.0104x; 1.0002x over previous
import contextlib
import numpy as np
import ml_dtypes
import concourse.bass as bass
import concourse.mybir as mybir
from concourse.bass_utils import run_bass_kernel_spmd

F32 = mybir.dt.float32
BF16 = mybir.dt.bfloat16
AF = mybir.ActivationFunctionType
ALU = mybir.AluOpType
AX = mybir.AxisListType

ENGS = ("sync", "gpsimd", "scalar", "vector", "tensor")

D = 1024
TL = 2048
NSEG = 16
NA = 4
NE = 16
NE_RUN = 16
STOP_AFTER = None
A_RUN = 4
SEG_RUN = 16
SEG_OPS = 99
DBG_SEG0 = False
EI = 512
NEGC = -1.0e30


def _masks_np():
    out = np.zeros((4, 128, 512), np.float32)
    for beta in range(4):
        hl, ccg = beta // 2, beta % 2
        for ccl in range(4):
            for j in range(32):
                r = ccl * 32 + j
                m = 16 * j + 2 * (4 * ccg + ccl) + hl
                for hlp in range(2):
                    for jp in range(32):
                        for ccp in range(8):
                            c = hlp * 256 + (jp // 16) * 128 + (jp % 16) * 8 + ccp
                            mp = 16 * jp + 2 * ccp + hlp
                            if mp > m:
                                out[beta, r, c] = NEGC
    return out.astype(ml_dtypes.bfloat16)


class Em:
    def __init__(self, sems):
        self.sems = sems
        self.cnt = {k: 0 for k in sems}
        self.ops = {e: [] for e in ENGS}
        self.scope = None

    def emit(self, eng, f, waits=None, inc=None, k=1):
        sems = self.sems
        wl = tuple((s, v) for (s, v) in (waits or ()) if v > 0)

        def g(engobj, wl=wl, inc=inc, k=k, f=f):
            for s, v in wl:
                engobj.wait_ge(sems[s], v)
            ins = f(engobj)
            if inc is not None:
                ins.then_inc(sems[inc], k)
        self.ops[eng].append((self.scope, g))
        if inc is not None:
            self.cnt[inc] += k
        return self.cnt[inc] if inc is not None else None

    def at(self, s):
        return self.cnt[s]


def build_program():
    nc = bass.Bass("TRN2", target_bir_lowering=False,
                   detect_race_conditions=False)

    xT = nc.dram_tensor("xT", [D, TL], F32, kind="ExternalInput").ap()
    wq_d = nc.dram_tensor("wq", [D, D], F32, kind="ExternalInput").ap()
    wk_d = nc.dram_tensor("wk", [D, D], F32, kind="ExternalInput").ap()
    wv_d = nc.dram_tensor("wv", [D, D], F32, kind="ExternalInput").ap()
    wo_d = nc.dram_tensor("wo", [D, D], F32, kind="ExternalInput").ap()
    gwT_d = nc.dram_tensor("gwT", [D, NE], F32, kind="ExternalInput").ap()
    wg_d = nc.dram_tensor("wg", [NE, D, EI], F32, kind="ExternalInput").ap()
    wu_d = nc.dram_tensor("wu", [NE, D, EI], F32, kind="ExternalInput").ap()
    wd_d = nc.dram_tensor("wd", [NE, EI, D], F32, kind="ExternalInput").ap()
    wsg_d = nc.dram_tensor("wsg", [D, D], F32, kind="ExternalInput").ap()
    wsu_d = nc.dram_tensor("wsu", [D, D], F32, kind="ExternalInput").ap()
    wsd_d = nc.dram_tensor("wsd", [D, D], F32, kind="ExternalInput").ap()
    betas_d = nc.dram_tensor("betas", [1, 256], F32, kind="ExternalInput").ap()
    maug_d = nc.dram_tensor("maug", [NA, 64, 65], F32, kind="ExternalInput").ap()
    outT = nc.dram_tensor("outT", [D, TL], F32, kind="ExternalOutput").ap()

    ident_d = nc.inline_tensor(np.eye(128, dtype=ml_dtypes.bfloat16), "c_ident").ap()
    identf_d = nc.inline_tensor(np.eye(16, dtype=np.float32), "c_identf").ap()
    ones_d = nc.inline_tensor(np.ones((128, 128), dtype=ml_dtypes.bfloat16), "c_ones").ap()
    masks_d = nc.inline_tensor(np.ascontiguousarray(_masks_np().reshape(4 * 128, 512)),
                               "c_masks").ap()
    sel_np = np.zeros((16, 16, 128), np.float32)
    for e_ in range(16):
        sel_np[e_, e_, :] = 1.0
    sel_d = nc.inline_tensor(np.ascontiguousarray(
        sel_np.transpose(1, 0, 2).reshape(16, 16 * 128)).astype(ml_dtypes.bfloat16),
        "c_sel").ap()

    ctx = contextlib.ExitStack()
    with ctx:
        def sbt(st, name, shape, dt):
            return st.enter_context(nc.sbuf_tensor(name, shape, dt))

        sems = {}
        for s_ in ("dS", "dG", "PE", "V", "SC", "GP", "dO", "dO2"):
            sems[s_] = ctx.enter_context(nc.semaphore(s_))
        em = Em(sems)

        ident = sbt(ctx, "ident", [128, 128], BF16)
        identf = sbt(ctx, "identf", [16, 16], F32)
        ones_b = sbt(ctx, "ones_b", [128, 128], BF16)

        betas_sb = sbt(ctx, "betas_sb", [1, 256], F32)
        gw_b = sbt(ctx, "gw_b", [128, 8 * NE], BF16)
        xn = sbt(ctx, "xn", [128, 8 * TL], BF16)
        att_out = sbt(ctx, "att_out", [128, 8 * TL], BF16)
        gate_row = sbt(ctx, "gate_row", [1, 256], F32)
        onemg_row = sbt(ctx, "onemg_row", [1, 256], F32)
        onemg_b = sbt(ctx, "onemg_b", [1, 256], BF16)
        gam_row = sbt(ctx, "gam_row", [1, 256], F32)
        gam65_b = sbt(ctx, "gam65_b", [1, NA * 65], BF16)

        p2w = ctx.enter_context(contextlib.ExitStack())
        masks_b = [sbt(p2w, f"mask{i}", [128, 512], BF16) for i in range(4)]
        wq_b = sbt(p2w, "wq_b", [128, 8 * 512], BF16)
        wk_b = sbt(p2w, "wk_b", [128, 8 * 512], BF16)
        wvo_b = sbt(p2w, "wvo_b", [128, 8 * 512], BF16)

        def wload(e, buf, src_d, h):
            return e.dma_start(
                out=buf[:].rearrange("p (r c) -> p r c", r=8),
                in_=src_d.rearrange("(r p) c -> p r c", p=128)[
                    :, :, h * 512:(h + 1) * 512])

        em.scope = "p0"
        for dst, src in ((ident, ident_d), (identf, identf_d), (ones_b, ones_d),
                         (betas_sb, betas_d)):
            em.emit("sync", lambda e, dst=dst, src=src: e.dma_start(out=dst[:], in_=src),
                    inc="dS", k=16)
        for i in range(4):
            em.emit("sync", lambda e, i=i: e.dma_start(
                out=masks_b[i][:], in_=masks_d[i * 128:(i + 1) * 128, :]), inc="dS", k=16)
        d_consts = em.at("dS")

        em.emit("gpsimd", lambda e: e.dma_start(
            out=gw_b[:].rearrange("p (r c) -> p r c", r=8),
            in_=gwT_d.rearrange("(r p) c -> p r c", p=128)), inc="dG", k=16)
        em.emit("gpsimd", lambda e: wload(e, wq_b, wq_d, 0), inc="dG", k=16)
        d_wq_h = [em.at("dG"), 0]
        em.emit("gpsimd", lambda e: wload(e, wk_b, wk_d, 0), inc="dG", k=16)
        d_wk_h = [em.at("dG"), 0]

        em.emit("scalar", lambda e: nc.scalar.activation(gate_row[:], betas_sb[:], AF.Sigmoid),
                waits=[("dS", d_consts)], inc="SC")
        sc_gate = em.at("SC")
        em.emit("vector", lambda e: nc.vector.tensor_scalar(
            onemg_row[:], gate_row[:], 1.0, -1.0, ALU.subtract, ALU.mult),
            waits=[("SC", sc_gate)], inc="V")
        em.emit("vector", lambda e: nc.vector.reciprocal(gam_row[:], onemg_row[:]), inc="V")
        em.emit("vector", lambda e: nc.vector.tensor_mul(gam_row[:], gate_row[:], gam_row[:]), inc="V")
        em.emit("vector", lambda e: nc.vector.tensor_copy(onemg_b[:], onemg_row[:]), inc="V")
        for a in range(NA):
            em.emit("vector", lambda e, a=a: nc.vector.tensor_copy(
                gam65_b[:, a * 65:a * 65 + 64], gam_row[:, a * 64:(a + 1) * 64]), inc="V")
        em.emit("vector", lambda e: nc.vector.memset(
            gam65_b[:].rearrange("p (a r) -> p a r", a=NA)[:, :, 64:65], 1.0), inc="V")
        v_gate = em.at("V")

        em.scope = "ln"
        with contextlib.ExitStack() as ph:
            xf = [sbt(ph, f"xf{i}", [128, TL], F32) for i in range(2)]
            xbf = sbt(ph, "xbf", [128, 8 * TL], BF16)
            x2 = sbt(ph, "x2", [128, 2 * TL], BF16)
            msb = sbt(ph, "msb", [128, TL], F32)
            vsb = sbt(ph, "vsb", [128, TL], F32)
            rstd = sbt(ph, "rstd", [128, TL], F32)
            ps_sx = ph.enter_context(nc.psum_tensor("ps_sx", [128, TL], F32))
            ps_sx2 = ph.enter_context(nc.psum_tensor("ps_sx2", [128, TL], F32))

            v_free = [0, 0]
            pe_x2done = [0] * 8
            for kc in range(8):
                q, qs = (("sync", "dS") if kc % 2 == 0 else ("gpsimd", "dG"))
                em.emit(q, lambda e, kc=kc: e.dma_start(
                    out=xf[kc % 2][:], in_=xT[kc * 128:(kc + 1) * 128, :]),
                    waits=[("V", v_free[kc % 2])], inc=qs, k=16)
                dld = em.at(qs)
                em.emit("vector", lambda e, kc=kc: nc.vector.tensor_copy(
                    xbf[:, kc * TL:(kc + 1) * TL], xf[kc % 2][:]),
                    waits=[(qs, dld)], inc="V")
                v_cast = em.at("V")
                v_free[kc % 2] = v_cast
                xo = (kc % 2) * TL
                em.emit("scalar", lambda e, kc=kc, xo=xo: nc.scalar.activation(
                    x2[:, xo:xo + TL], xbf[:, kc * TL:(kc + 1) * TL], AF.Square),
                    waits=[("V", v_cast)] + ([("PE", pe_x2done[kc - 2])] if kc >= 2 else []),
                    inc="SC")
                sc_sq = em.at("SC")

                def lnmm(e, kc=kc, xo=xo):
                    for nt in range(4):
                        nc.tensor.matmul(
                            ps_sx[:, nt * 512:(nt + 1) * 512], ones_b[:],
                            xbf[:, kc * TL + nt * 512: kc * TL + (nt + 1) * 512],
                            start=(kc == 0), stop=(kc == 7), skip_group_check=True)
                    for nt in range(4):
                        ins = nc.tensor.matmul(
                            ps_sx2[:, nt * 512:(nt + 1) * 512], ones_b[:],
                            x2[:, xo + nt * 512: xo + (nt + 1) * 512],
                            start=(kc == 0), stop=(kc == 7), skip_group_check=True)
                    return ins
                em.emit("tensor", lnmm, waits=[("V", v_cast), ("SC", sc_sq)], inc="PE")
                pe_x2done[kc] = em.at("PE")
            pe_ln = em.at("PE")

            em.emit("vector", lambda e: nc.vector.tensor_scalar(
                msb[:], ps_sx[:], 1.0 / 1024.0, None, ALU.mult),
                waits=[("PE", pe_ln)], inc="V")
            em.emit("vector", lambda e: nc.vector.tensor_mul(vsb[:], msb[:], msb[:]), inc="V")
            em.emit("vector", lambda e: nc.vector.scalar_tensor_tensor(
                vsb[:], ps_sx2[:], 1.0 / 1024.0, vsb[:], ALU.mult, ALU.subtract), inc="V")
            em.emit("vector", lambda e: nc.vector.tensor_scalar(
                vsb[:], vsb[:], 1e-5, None, ALU.add), inc="V")
            em.emit("vector", lambda e: nc.vector.reciprocal(vsb[:], vsb[:]), inc="V")
            v_var = em.at("V")
            em.emit("scalar", lambda e: nc.scalar.activation(rstd[:], vsb[:], AF.Sqrt),
                    waits=[("V", v_var)], inc="SC")
            sc_rstd = em.at("SC")
            for kc in range(8):
                em.emit("vector", lambda e, kc=kc: nc.vector.tensor_sub(
                    vsb[:], xbf[:, kc * TL:(kc + 1) * TL], msb[:]),
                    waits=([("SC", sc_rstd)] if kc == 0 else None), inc="V")
                em.emit("vector", lambda e, kc=kc: nc.vector.tensor_mul(
                    xn[:, kc * TL:(kc + 1) * TL], vsb[:], rstd[:]), inc="V")
            v_xn = em.at("V")

        if STOP_AFTER == 'ln':
            _finale(nc, em, xn, outT, v_xn)
            _replay(nc, em)
            return nc

        em.scope = "attn_pre"
        with contextlib.ExitStack() as ph:
            Q2A = sbt(ph, "Q2A", [128, NSEG * 256], BF16)
            Q2B = sbt(ph, "Q2B", [128, NSEG * 256], BF16)
            K2 = sbt(ph, "K2", [128, NSEG * 256], BF16)
            K2X = sbt(ph, "K2X", [128, NSEG * 256], BF16)
            Vscr = sbt(ph, "Vscr", [128, NSEG * 256], BF16)
            VscrX = sbt(ph, "VscrX", [64, NSEG * 256], BF16)
            vgcol = sbt(ph, "vgcol", [128, 1], F32)
            att_bf = sbt(ph, "att_bf", [128, 8 * 512], BF16)
            A_scr = sbt(ph, "A_scr", [128, 8 * 512], BF16)
            P_sb = sbt(ph, "P_sb", [128, 2048], BF16)
            PT_sb = sbt(ph, "PT_sb", [128, 2048], BF16)
            sq_scr = sbt(ph, "sq_scr", [128, 256], BF16)
            skE = sbt(ph, "skE", [64, 256], BF16)
            skO = sbt(ph, "skO", [64, 256], BF16)
            elu_t = sbt(ph, "elu_t", [128, 256], F32)
            skT_sb = sbt(ph, "skT_sb", [128, 512], BF16)
            M_master = sbt(ph, "M_master", [128, NA * 65], F32)
            M_dupA = sbt(ph, "M_dupA", [128, 130], BF16)
            M_dupB = sbt(ph, "M_dupB", [128, 130], BF16)
            gamrep = sbt(ph, "gamrep", [128, NA * 65], BF16)
            maug2 = sbt(ph, "maug2", [128, NA * 65], F32)
            rdl_a = sbt(ph, "rdl_a", [1, 512], BF16)
            rdl_b = sbt(ph, "rdl_b", [1, 512], BF16)
            rr_sb = sbt(ph, "rr_sb", [128, 512], BF16)
            rl_sb = sbt(ph, "rl_sb", [128, 512], BF16)
            t1_sb = sbt(ph, "t1_sb", [128, 512], BF16)
            t2_sb = sbt(ph, "t2_sb", [128, 512], BF16)

            for a in range(NA):
                for hb in range(2):
                    em.emit("sync", lambda e, a=a, hb=hb: e.dma_start(
                        out=maug2[hb * 64:(hb + 1) * 64, a * 65:(a + 1) * 65],
                        in_=maug_d[a]),
                        waits=([("V", v_xn)] if (a == 0 and hb == 0) else None),
                        inc="dS", k=16)
            d_maug = em.at("dS")

            em.emit("vector", lambda e: nc.vector.memset(Q2A[64:128, :], 0.0),
                    waits=[("V", v_xn)], inc="V")
            em.emit("vector", lambda e: nc.vector.memset(Q2B[0:64, :], 0.0), inc="V")
            em.emit("vector", lambda e: nc.vector.memset(M_dupA[64:128, :], 0.0), inc="V")
            em.emit("vector", lambda e: nc.vector.memset(M_dupB[0:64, :], 0.0), inc="V")
            v_zinit = em.at("V")

            with nc.psum_tensor("ps_small", [128, 512], F32) as ps_small:
                def g65(e):
                    for a in range(NA):
                        ins = nc.tensor.matmul(
                            ps_small[:, a * 65:(a + 1) * 65], ones_b[0:1, :],
                            gam65_b[:, a * 65:(a + 1) * 65],
                            start=(a == 0), stop=(a == NA - 1), skip_group_check=True)
                    return ins
                em.emit("tensor", g65, waits=[("V", em.at("V"))], inc="PE")
                pe_g65 = em.at("PE")
                em.emit("vector", lambda e: nc.vector.tensor_copy(
                    gamrep[:], ps_small[:, 0:NA * 65]),
                    waits=[("PE", pe_g65)], inc="V")
                v_g65 = em.at("V")
            with nc.psum_tensor("ps_small2", [128, 512], F32) as ps_small2:
                def m0rep(e):
                    for a in range(NA):
                        ins = nc.tensor.matmul(
                            ps_small2[:, a * 64:(a + 1) * 64], ones_b[0:1, :],
                            onemg_b[:, a * 64:(a + 1) * 64],
                            start=(a == 0), stop=(a == NA - 1), skip_group_check=True)
                    return ins
                em.emit("tensor", m0rep, waits=[("V", max(v_gate, v_g65))], inc="PE")
                pe_m0 = em.at("PE")
                for a in range(NA):
                    em.emit("vector", lambda e, a=a: nc.vector.tensor_tensor(
                        M_master[:, a * 65:a * 65 + 64], maug2[:, a * 65:a * 65 + 64],
                        ps_small2[:, a * 64:(a + 1) * 64], ALU.mult),
                        waits=[("PE", pe_m0), ("dS", d_maug)], inc="V")
                    em.emit("vector", lambda e, a=a: nc.vector.tensor_copy(
                        M_master[:, a * 65 + 64:a * 65 + 65],
                        maug2[:, a * 65 + 64:a * 65 + 65]), inc="V")
                v_minit = em.at("V")

            pe_prev_a = 0
            v_mup = v_minit

            for a in range(A_RUN):
                em.scope = "proj"
                em.emit("gpsimd", lambda e: wload(e, wvo_b, wv_d, 0),
                        waits=[("PE", pe_prev_a)], inc="dG", k=16)
                d_wv_h = [em.at("dG"), 0]

                pha = contextlib.ExitStack()
                xn_a = pha.enter_context(nc.sbuf_tensor(f"xn_a{a}", [128, 8 * 512], BF16))
                for kc in range(8):
                    src = xn[:].rearrange("p (k s ar j) -> p k s ar j", k=8, s=16, ar=4)[
                        :, kc:kc + 1, :, a:a + 1, :].squeeze(1).squeeze(2)
                    em.emit("vector", lambda e, kc=kc, src=src: nc.vector.tensor_copy(
                        xn_a[:, kc * 512:(kc + 1) * 512].rearrange("p (s j) -> p s j", s=16),
                        src),
                        waits=([("V", v_xn), ("PE", pe_prev_a)] if kc == 0 else None),
                        inc="V")
                v_xna = em.at("V")

                with nc.psum_tensor(f"ps_vgc{a}", [64, 8], F32) as ps_vgc:
                    em.emit("tensor", lambda e, a=a: nc.tensor.transpose(
                        ps_vgc[0:64, 0:1], onemg_row[0:1, a * 64:(a + 1) * 64],
                        identf[0:1, 0:1]),
                        waits=[("V", em.at("V"))], inc="PE")
                    pe_vg = em.at("PE")
                    em.emit("vector", lambda e: nc.vector.tensor_copy(
                        vgcol[0:64, :], ps_vgc[:, 0:1]), waits=[("PE", pe_vg)], inc="V")
                    em.emit("vector", lambda e: e.drain() or e.engine_nop(), inc="V")
                    v_vg0 = em.at("V")
                    em.emit("gpsimd", lambda e: e.dma_start(
                        out=vgcol[64:128, :], in_=vgcol[0:64, :]),
                        waits=[("V", v_vg0)], inc="dG", k=16)
                    d_vg = em.at("dG")
                    v_vg = v_vg0

                with nc.psum_tensor(f"ps_proj{a}", [128, 1024], F32) as ps_proj:
                    v_evac_half = [v_vg, v_vg]
                    dmap = {0: d_wq_h, 1: d_wk_h, 2: d_wv_h}
                    d_wo_h = [0, 0]
                    for wh in range(2):
                        for ti, wb in enumerate((wq_b, wk_b, wvo_b)):
                            for ccl in range(4):
                                cc = wh * 4 + ccl
                                half = ccl % 2

                                def projmm(e, wb=wb, ccl=ccl, half=half):
                                    for kc in range(8):
                                        ins = nc.tensor.matmul(
                                            ps_proj[:, half * 512:(half + 1) * 512],
                                            wb[:, kc * 512 + ccl * 128: kc * 512 + (ccl + 1) * 128],
                                            xn_a[:, kc * 512:(kc + 1) * 512],
                                            start=(kc == 0), stop=(kc == 7))
                                    return ins
                                wl = [("V", max(v_xna, v_evac_half[half]))]
                                if ccl == 0:
                                    wl.append(("dG", dmap[ti][wh]))
                                em.emit("tensor", projmm, waits=wl, inc="PE")
                                pe_p = em.at("PE")
                                if ccl == 3:
                                    if wh == 0:
                                        em.emit("gpsimd", lambda e, wb=wb, ti=ti: wload(
                                            e, wb, (wq_d, wk_d, wv_d)[ti], 1),
                                            waits=[("PE", pe_p)], inc="dG", k=16)
                                        dmap[ti][1] = em.at("dG")
                                    elif ti < 2 and a + 1 < A_RUN:
                                        em.emit("gpsimd", lambda e, wb=wb, ti=ti: wload(
                                            e, wb, (wq_d, wk_d)[ti], 0),
                                            waits=[("PE", pe_p)], inc="dG", k=16)
                                        dmap[ti][0] = em.at("dG")
                                    elif ti == 2:
                                        em.emit("gpsimd", lambda e: wload(e, wvo_b, wo_d, 0),
                                                waits=[("PE", pe_p)], inc="dG", k=16)
                                        d_wo_h[0] = em.at("dG")
                                for hl in range(2):
                                    if ti == 0:
                                        def scram(e, cc=cc, hl=hl, half=half):
                                            qdst = Q2A if hl == 0 else Q2B
                                            dst = qdst[hl * 64:(hl + 1) * 64, :].rearrange(
                                                "p (s c8 j) -> p s c8 j", s=16, c8=8, j=32)[
                                                :, :, cc:cc + 1, :].squeeze(2)
                                            src = ps_proj[hl * 64:(hl + 1) * 64,
                                                          half * 512:(half + 1) * 512].rearrange(
                                                "p (s j) -> p s j", s=16)
                                            return nc.vector.tensor_copy(dst, src)
                                    elif ti == 1:
                                        def scram(e, cc=cc, hl=hl, half=half):
                                            dst = K2[hl * 64:(hl + 1) * 64, :].rearrange(
                                                "p (s j c8) -> p s j c8", s=16, j=32, c8=8)[
                                                :, :, :, cc:cc + 1].squeeze(3)
                                            src = ps_proj[hl * 64:(hl + 1) * 64,
                                                          half * 512:(half + 1) * 512].rearrange(
                                                "p (s j) -> p s j", s=16)
                                            return nc.vector.tensor_copy(dst, src)
                                    else:
                                        def scram(e, cc=cc, hl=hl, half=half):
                                            dst = Vscr[hl * 64:(hl + 1) * 64, :].rearrange(
                                                "p (s j c8) -> p s j c8", s=16, j=32, c8=8)[
                                                :, :, :, cc:cc + 1].squeeze(3)
                                            src = ps_proj[hl * 64:(hl + 1) * 64,
                                                          half * 512:(half + 1) * 512].rearrange(
                                                "p (s j) -> p s j", s=16)
                                            return nc.vector.tensor_scalar(
                                                dst, src, vgcol[hl * 64:(hl + 1) * 64, 0:1],
                                                None, ALU.mult)
                                    wl2 = [("PE", pe_p)]
                                    if ti == 2:
                                        wl2.append(("dG", d_vg))
                                    em.emit("vector", scram, waits=wl2, inc="V")
                                v_evac_half[half] = em.at("V")
                    v_qk = em.at("V")
                pha.close()

                pe_proj_done = em.at("PE")
                em.emit("gpsimd", lambda e: e.dma_start(out=K2X[0:64, :], in_=K2[64:128, :]),
                        waits=[("V", v_qk), ("PE", pe_proj_done)], inc="dG", k=16)
                em.emit("gpsimd", lambda e: e.dma_start(out=K2X[64:128, :], in_=K2[0:64, :]),
                        inc="dG", k=16)
                em.emit("gpsimd", lambda e: e.dma_start(out=VscrX[0:64, :], in_=Vscr[64:128, :]),
                        inc="dG", k=16)
                d_k2x = em.at("dG")

                em.scope = "rec"
                with contextlib.ExitStack() as rec:
                    ps_S = rec.enter_context(nc.psum_tensor(f"ps_S{a}", [128, 1024], F32))
                    ps_pt = rec.enter_context(nc.psum_tensor(f"ps_pt{a}", [128, 2048], BF16))
                    ps_num = rec.enter_context(nc.psum_tensor(f"ps_num{a}", [128, 512], F32))
                    ps_dot = rec.enter_context(nc.psum_tensor(f"ps_dot{a}", [128, 512], F32))
                    ps_msc = rec.enter_context(nc.psum_tensor(f"ps_msc{a}", [128, 512], F32))
                    ps_rr = rec.enter_context(nc.psum_tensor(f"ps_rr{a}", [128, 512], F32))

                    v_comb = em.at("V")
                    v_e1_prev = 0
                    v_rr_prev = 0
                    v_skt_prev = 0
                    sc_p_prev = 0
                    sc_pw_seg = {}

                    def emit_swave(s):
                        nonlocal sc_p_prev
                        sc_pw_seg[s] = [0, 0]
                        for wave in range(2):
                            def swave(e, s=s, wave=wave):
                                for beta in (wave * 2, wave * 2 + 1):
                                    hl, ccg = beta // 2, beta % 2
                                    bk = (beta - wave * 2) * 512
                                    qsrc = Q2A if hl == 0 else Q2B
                                    for hlp in range(2):
                                        ksrc = K2 if hlp == hl else K2X
                                        nc.tensor.matmul(
                                            ps_S[:, bk + hlp * 256: bk + (hlp + 1) * 256],
                                            qsrc[:, s * 256 + ccg * 128: s * 256 + (ccg + 1) * 128],
                                            ksrc[:, s * 256:(s + 1) * 256],
                                            start=(hlp == 0), stop=False,
                                            skip_group_check=True)
                                    ins = nc.tensor.matmul(
                                        ps_S[:, bk:bk + 512], ident[:], masks_b[beta][:],
                                        start=False, stop=True, skip_group_check=True)
                                return ins
                            em.emit("tensor", swave,
                                    waits=[("SC", sc_p_prev), ("dG", d_k2x)], inc="PE")
                            pe_w = em.at("PE")

                            def pexp(e, wave=wave):
                                for bi in range(2):
                                    beta = wave * 2 + bi
                                    ins = nc.scalar.activation(
                                        P_sb[:, beta * 512:(beta + 1) * 512],
                                        ps_S[:, bi * 512:(bi + 1) * 512], AF.Exp,
                                        scale=0.125)
                                return ins
                            em.emit("scalar", pexp,
                                    waits=[("PE", pe_w)], inc="SC")
                            sc_p_prev = em.at("SC")
                            sc_pw_seg[s][wave] = sc_p_prev
                    sc_el_prev = 0
                    pe_pt_prev = 0
                    pe_numden_prev = 0
                    pe_dm_prev = 0
                    pe_rep_prev = 0
                    for s in range(SEG_RUN):
                        em.emit("vector", lambda e, s=s: nc.vector.tensor_scalar_min(
                            elu_t[0:64, :], Q2A[0:64, s * 256:(s + 1) * 256], 0.0),
                            waits=[("V", max(v_qk, v_zinit)), ("SC", sc_el_prev),
                                   ("PE", pe_numden_prev)], inc="V")
                        em.emit("vector", lambda e, s=s: nc.vector.tensor_scalar_min(
                            elu_t[64:128, :], Q2B[64:128, s * 256:(s + 1) * 256], 0.0),
                            inc="V")
                        v_el1 = em.at("V")
                        em.emit("scalar", lambda e: nc.scalar.activation(
                            elu_t[:], elu_t[:], AF.Exp),
                            waits=[("V", v_el1)], inc="SC")
                        sc_el = em.at("SC")
                        em.emit("vector", lambda e, s=s: nc.vector.scalar_tensor_tensor(
                            sq_scr[0:64, :], Q2A[0:64, s * 256:(s + 1) * 256], 0.0,
                            elu_t[0:64, :], ALU.max, ALU.add),
                            waits=[("SC", sc_el)], inc="V")
                        em.emit("vector", lambda e, s=s: nc.vector.scalar_tensor_tensor(
                            sq_scr[64:128, :], Q2B[64:128, s * 256:(s + 1) * 256], 0.0,
                            elu_t[64:128, :], ALU.max, ALU.add), inc="V")
                        em.emit("vector", lambda e, s=s: nc.vector.tensor_scalar_min(
                            elu_t[0:64, :], K2[0:64, s * 256:(s + 1) * 256], 0.0),
                            waits=[("PE", pe_pt_prev)], inc="V")
                        v_el2 = em.at("V")
                        em.emit("scalar", lambda e: nc.scalar.activation(
                            elu_t[0:64, :], elu_t[0:64, :], AF.Exp),
                            waits=[("V", v_el2)], inc="SC")
                        sc_el = em.at("SC")
                        em.emit("vector", lambda e, s=s: nc.vector.scalar_tensor_tensor(
                            skE[:], K2[0:64, s * 256:(s + 1) * 256], 0.0,
                            elu_t[0:64, :], ALU.max, ALU.add),
                            waits=[("SC", sc_el)], inc="V")
                        em.emit("vector", lambda e, s=s: nc.vector.tensor_scalar_min(
                            elu_t[0:64, :], K2X[0:64, s * 256:(s + 1) * 256], 0.0),
                            waits=[("dG", d_k2x)], inc="V")
                        v_el3 = em.at("V")
                        em.emit("scalar", lambda e: nc.scalar.activation(
                            elu_t[0:64, :], elu_t[0:64, :], AF.Exp),
                            waits=[("V", v_el3)], inc="SC")
                        sc_el = em.at("SC")
                        sc_el_prev = sc_el
                        em.emit("vector", lambda e, s=s: nc.vector.scalar_tensor_tensor(
                            skO[:], K2X[0:64, s * 256:(s + 1) * 256], 0.0,
                            elu_t[0:64, :], ALU.max, ALU.add),
                            waits=[("SC", sc_el)], inc="V")
                        v_sqsk = em.at("V")
                        if SEG_OPS <= 2:
                            continue

                        def mdup(e, a=a):
                            for hl, MD in ((0, M_dupA), (1, M_dupB)):
                                r0, r1 = hl * 64, hl * 64 + 64
                                for co in (0, 64):
                                    nc.vector.tensor_tensor(
                                        MD[r0:r1, co:co + 64],
                                        M_master[r0:r1, a * 65:a * 65 + 64],
                                        gamrep[r0:r1, a * 65:a * 65 + 64], ALU.mult)
                                ins = nc.vector.tensor_copy(
                                    MD[r0:r1, 128:129],
                                    M_master[r0:r1, a * 65 + 64:a * 65 + 65])
                            return ins
                        em.emit("vector", mdup,
                                waits=[("V", max(v_g65, v_mup)), ("PE", pe_numden_prev)],
                                inc="V")
                        v_mbf = em.at("V")
                        if SEG_OPS <= 3:
                            continue

                        if SEG_OPS <= 4:
                            continue

                        if s == 0 and SEG_OPS > 5:
                            emit_swave(0)
                        if SEG_OPS <= 5:
                            continue

                        for wv in range(2):
                            def ptrans(e, wv=wv):
                                for beta in (wv * 2, wv * 2 + 1):
                                    for kap in range(4):
                                        ins = nc.tensor.transpose(
                                            ps_pt[:, kap * 512 + beta * 128: kap * 512 + (beta + 1) * 128],
                                            P_sb[:, beta * 512 + kap * 128: beta * 512 + (kap + 1) * 128],
                                            ident[:])
                                return ins
                            em.emit("tensor", ptrans,
                                    waits=[("SC", sc_pw_seg[s][wv]), ("V", v_skt_prev)],
                                    inc="PE")
                        pe_pt = em.at("PE")
                        for kap in range(4):
                            em.emit("scalar", lambda e, kap=kap: nc.scalar.activation(
                                PT_sb[:, kap * 512:(kap + 1) * 512],
                                ps_pt[:, kap * 512:(kap + 1) * 512], AF.Copy),
                                waits=([("PE", pe_pt)] if kap == 0 else None), inc="SC")
                            if kap == 0:
                                sc_ptev_k0 = em.at("SC")
                        sc_ptev = em.at("SC")
                        if SEG_OPS <= 7:
                            continue

                        def sktr(e, s=s):
                            for kap in range(4):
                                hlp, jh = kap // 2, kap % 2
                                sksrc = skE if hlp == 0 else skO
                                nc.tensor.transpose(
                                    ps_pt[:, kap * 64:(kap + 1) * 64],
                                    sksrc[0:64, jh * 128:(jh + 1) * 128],
                                    ident[0:64, 0:64])
                            for kap in range(4):
                                hlp, jh = kap // 2, kap % 2
                                ins = nc.tensor.transpose(
                                    ps_pt[:, 256 + kap * 64:256 + (kap + 1) * 64],
                                    (Vscr if hlp == 0 else VscrX)[
                                        0:64, s * 256 + jh * 128: s * 256 + (jh + 1) * 128],
                                    ident[0:64, 0:64])
                            return ins
                        em.emit("tensor", sktr,
                                waits=[("SC", sc_ptev_k0), ("V", max(v_sqsk, v_qk)),
                                       ("dG", d_k2x)], inc="PE")
                        pe_skt = em.at("PE")
                        pe_pt_prev = pe_skt
                        em.emit("vector", lambda e: nc.vector.tensor_copy(
                            skT_sb[:], ps_pt[:, 0:512]),
                            waits=[("PE", pe_skt)], inc="V")
                        v_skt = em.at("V")
                        v_skt_prev = v_skt
                        if SEG_OPS <= 9:
                            continue

                        def dmm(e, s=s):
                            for ob in range(2):
                                for kap in range(4):
                                    nc.tensor.matmul(
                                        ps_rr[ob * 64:(ob + 1) * 64, 0:64],
                                        skT_sb[:, kap * 64:(kap + 1) * 64],
                                        skT_sb[:, 256 + kap * 64:256 + (kap + 1) * 64],
                                        start=(kap == 0), stop=False,
                                        skip_group_check=True)
                                for kap in range(4):
                                    ins = nc.tensor.matmul(
                                        ps_rr[ob * 64:(ob + 1) * 64, 64:65],
                                        skT_sb[:, kap * 64:(kap + 1) * 64],
                                        ones_b[:, 0:1],
                                        start=False, stop=(kap == 3),
                                        skip_group_check=True)
                            return ins
                        em.emit("tensor", dmm,
                                waits=[("V", max(v_skt, v_rr_prev))], inc="PE")
                        pe_dm = em.at("PE")
                        pe_dm_prev = pe_dm
                        em.emit("vector", lambda e, a=a: nc.vector.tensor_tensor(
                            M_master[:, a * 65:(a + 1) * 65], M_master[:, a * 65:(a + 1) * 65],
                            ps_rr[:, 0:65], ALU.add),
                            waits=[("PE", pe_dm), ("V", v_mbf)], inc="V")
                        v_mup = em.at("V")

                        def numden(e, s=s):
                            for hl, MD in ((0, M_dupA), (1, M_dupB)):
                                nc.tensor.matmul(
                                    ps_num[:, hl * 256:(hl + 1) * 256],
                                    MD[:, 0:128], sq_scr[:, :],
                                    start=True, stop=True, skip_group_check=True)
                            for hl, MD in ((0, M_dupA), (1, M_dupB)):
                                ins = nc.tensor.matmul(
                                    ps_msc[0:1, hl * 256:(hl + 1) * 256],
                                    MD[:, 128:129], sq_scr[:, :],
                                    start=True, stop=True, skip_group_check=True)
                            return ins
                        em.emit("tensor", numden,
                                waits=[("V", max(v_mbf, v_sqsk, v_e1_prev))], inc="PE")
                        pe_num = em.at("PE")
                        pe_numden_prev = pe_num

                        def lammm(e, s=s):
                            for kap in range(4):
                                ins = nc.tensor.matmul(
                                    ps_rr[0:1, :], ones_b[:, 0:1],
                                    PT_sb[:, kap * 512:(kap + 1) * 512],
                                    start=(kap == 0), stop=(kap == 3),
                                    skip_group_check=True)
                            return ins
                        em.emit("tensor", lammm,
                                waits=[("V", max(v_comb, v_e1_prev, v_skt, v_mup)),
                                       ("SC", sc_ptev)],
                                inc="PE")
                        pe_lam = em.at("PE")

                        def dotmm(e, s=s):
                            for ob in range(2):
                                for kap in range(4):
                                    ins = nc.tensor.matmul(
                                        ps_dot[ob * 64:(ob + 1) * 64, :],
                                        skT_sb[:, 256 + kap * 64:256 + (kap + 1) * 64],
                                        PT_sb[:, kap * 512:(kap + 1) * 512],
                                        start=(kap == 0), stop=(kap == 3),
                                        skip_group_check=True)
                            return ins
                        em.emit("tensor", dotmm, inc="PE")
                        pe_dot = em.at("PE")
                        if s + 1 < SEG_RUN:
                            emit_swave(s + 1)
                        if SEG_OPS <= 10:
                            continue

                        def recips_a(e):
                            with nc.allow_low_precision(reason="bf16 matmul operand"):
                                return nc.vector.reciprocal(rdl_a[:], ps_msc[0:1, :])
                        def recips_b(e):
                            with nc.allow_low_precision(reason="bf16 matmul operand"):
                                return nc.vector.reciprocal(rdl_b[:], ps_rr[0:1, :])
                        em.emit("vector", recips_a, waits=[("PE", pe_num)], inc="V")
                        em.emit("vector", recips_b, waits=[("PE", pe_lam)], inc="V")
                        v_rdl = em.at("V")
                        if SEG_OPS <= 11:
                            continue

                        def rrmm(e):
                            nc.tensor.matmul(ps_rr[:], ones_b[0:1, :], rdl_a[:],
                                             start=True, stop=True, skip_group_check=True)
                            return nc.tensor.matmul(ps_msc[:], ones_b[0:1, :], rdl_b[:],
                                                    start=True, stop=True,
                                                    skip_group_check=True)
                        em.emit("tensor", rrmm,
                                waits=[("V", v_rdl)], inc="PE")
                        pe_rep = em.at("PE")
                        pe_rep_prev = pe_rep
                        if SEG_OPS <= 12:
                            continue
                        em.emit("vector", lambda e: nc.vector.tensor_copy(
                            rr_sb[:], ps_rr[:]), waits=[("PE", pe_rep)], inc="V")
                        em.emit("vector", lambda e: nc.vector.tensor_copy(
                            rl_sb[:], ps_msc[:]), inc="V")
                        v_rr = em.at("V")
                        v_rr_prev = v_rr
                        em.emit("vector", lambda e: nc.vector.tensor_tensor(
                            t1_sb[:], ps_num[:], rr_sb[:], ALU.mult), inc="V")
                        v_e1 = em.at("V")
                        v_e1_prev = v_e1
                        em.emit("vector", lambda e: nc.vector.tensor_tensor(
                            t2_sb[:], ps_dot[:], rl_sb[:], ALU.mult), inc="V")
                        em.emit("vector", lambda e, s=s: nc.vector.tensor_tensor(
                            att_bf[:, (s % 8) * 512:(s % 8 + 1) * 512], t1_sb[:], t2_sb[:],
                            ALU.add), inc="V")
                        v_comb = em.at("V")
                        if DBG_SEG0 == 1 and a == 0 and s == 0:
                            dumps = [
                                (outT[0:128, 0:512], att_bf[:, 0:512]),
                                (outT[0:128, 512:1024], t1_sb[:]),
                                (outT[0:128, 1024:1536], t2_sb[:]),
                                (outT[0:128, 1536:2048], skT_sb[:]),
                                (outT[128:256, 0:512], rr_sb[:]),
                                (outT[128:256, 512:1024], rl_sb[:]),
                                (outT[128:256, 1024:1280], sq_scr[:]),
                                (outT[128:256, 1280:1792], P_sb[:, 0:512]),
                                (outT[256:384, :], PT_sb[:]),
                                (outT[384:512, 0:129], M_dupA[:, 0:129]),
                                (outT[384:512, 512:641], M_dupB[:, 0:129]),
                                (outT[384:512, 1024:1089], gamrep[:, 0:65]),
                            ]
                            for i, (dst, src) in enumerate(dumps):
                                em.emit("gpsimd", lambda e, dst=dst, src=src: e.dma_start(
                                    out=dst, in_=src),
                                    waits=([("V", v_comb)] if i == 0 else None),
                                    inc="dO", k=16)
                            em.emit("gpsimd", lambda e: e.dma_start(
                                out=outT[512:640, 0:65], in_=M_master[:, 0:65]),
                                inc="dO", k=16)
                            d_dump = em.at("dO")
                            em.emit("vector", lambda e: e.engine_nop(),
                                    waits=[("dO", d_dump)], inc="V")
                            break
                        if SEG_OPS <= 13:
                            continue

                        if s in (7, 15):
                            base = 0 if s == 7 else 256
                            for h in range(16):
                                hl = h % 2
                                cc = h // 2
                                ccg, ccl = cc // 4, cc % 4
                                moff = hl * 256 + ccg * 128 + ccl * 32

                                def unscr(e, h=h, hl=hl, base=base, moff=moff):
                                    ch = h // 2
                                    dst = A_scr[hl * 64:(hl + 1) * 64,
                                                ch * 512 + base: ch * 512 + base + 256
                                                ].rearrange("p (s j) -> p s j", s=8)
                                    src = att_bf[hl * 64:(hl + 1) * 64, :].rearrange(
                                        "p (s m) -> p s m", s=8)[:, :, moff:moff + 32]
                                    return nc.vector.tensor_copy(dst, src)
                                em.emit("vector", unscr, inc="V")
                            v_comb = em.at("V")
                    v_unscr = em.at("V")

                if DBG_SEG0 == 2 and a == 0:
                    dumps2 = [
                        (outT[0:128, :], att_bf[:, 0:2048]),
                        (outT[128:256, :], att_bf[:, 2048:4096]),
                        (outT[256:384, :], A_scr[:, 0:2048]),
                        (outT[384:512, :], A_scr[:, 2048:4096]),
                    ]
                    for i, (dst, src) in enumerate(dumps2):
                        em.emit("gpsimd", lambda e, dst=dst, src=src: e.dma_start(
                            out=dst, in_=src),
                            waits=([("V", v_unscr)] if i == 0 else None),
                            inc="dO", k=16)
                    d_dump2 = em.at("dO")
                    em.emit("vector", lambda e: e.engine_nop(),
                            waits=[("dO", d_dump2)], inc="V")

                em.scope = "wo"
                if DBG_SEG0:
                    continue
                with nc.psum_tensor(f"ps_wo{a}", [128, 1024], F32) as ps_wo:
                    v_woev = [v_unscr, v_unscr]
                    for oc in range(8):
                        half = oc % 2
                        wh = oc // 4
                        ocl = oc % 4

                        def womm(e, ocl=ocl, half=half):
                            for ch in range(8):
                                ins = nc.tensor.matmul(
                                    ps_wo[:, half * 512:(half + 1) * 512],
                                    wvo_b[:, ch * 512 + ocl * 128: ch * 512 + (ocl + 1) * 128],
                                    A_scr[:, ch * 512:(ch + 1) * 512],
                                    start=(ch == 0), stop=(ch == 7))
                            return ins
                        wl = [("V", max(v_unscr, v_woev[half]))]
                        if ocl == 0:
                            wl.append(("dG", d_wo_h[wh]))
                        em.emit("tensor", womm, waits=wl, inc="PE")
                        pe_wo = em.at("PE")
                        if oc == 3:
                            em.emit("gpsimd", lambda e: wload(e, wvo_b, wo_d, 1),
                                    waits=[("PE", pe_wo)], inc="dG", k=16)
                            d_wo_h[1] = em.at("dG")

                        def woev(e, oc=oc, half=half, a=a):
                            dst = att_out[:].rearrange(
                                "p (o s ar j) -> p o s ar j", o=8, s=16, ar=4)[
                                :, oc:oc + 1, :, a:a + 1, :].squeeze(1).squeeze(2)
                            src = ps_wo[:, half * 512:(half + 1) * 512].rearrange(
                                "p (s j) -> p s j", s=16)
                            return nc.vector.tensor_copy(dst, src)
                        em.emit("vector", woev, waits=[("PE", pe_wo)], inc="V")
                        v_woev[half] = em.at("V")
                pe_prev_a = em.at("PE")
            v_attn = em.at("V")
        p2w.close()

        if STOP_AFTER == 'attn':
            if not DBG_SEG0:
                _finale(nc, em, att_out, outT, v_attn)
            _replay(nc, em)
            return nc

        em.scope = "ffn"
        with contextlib.ExitStack() as ph:
            acc = sbt(ph, "acc", [128, 8 * TL], F32)
            combT = sbt(ph, "combT", [16, TL], BF16)

            with contextlib.ExitStack() as lgs:
                lg_sb = sbt(lgs, "lg_sb", [16, TL], F32)
                lgt = sbt(lgs, "lgt", [128, 256], F32)
                est = sbt(lgs, "est", [128, 256], F32)
                m1s = sbt(lgs, "m1s", [128, 16], F32)
                nm1 = sbt(lgs, "nm1", [128, 16], F32)
                ssum = sbt(lgs, "ssum", [128, 16], F32)
                rs = sbt(lgs, "rs", [128, 16], F32)
                eqm = sbt(lgs, "eqm", [128, 256], F32)
                m2s = sbt(lgs, "m2s", [128, 16], F32)
                keep = sbt(lgs, "keep", [128, 256], F32)
                comb = sbt(lgs, "comb", [128, 256], F32)
                comb_b = sbt(lgs, "comb_b", [128, 256], BF16)

                with nc.psum_tensor("ps_lg", [16, TL], F32) as ps_lg:
                    def lgmm(e):
                        for kc in range(8):
                            for nt in range(4):
                                ins = nc.tensor.matmul(
                                    ps_lg[:, nt * 512:(nt + 1) * 512],
                                    gw_b[:, kc * NE:(kc + 1) * NE],
                                    xn[:, kc * TL + nt * 512: kc * TL + (nt + 1) * 512],
                                    start=(kc == 0), stop=(kc == 7), skip_group_check=True)
                        return ins
                    em.emit("tensor", lgmm, waits=[("V", v_attn)], inc="PE")
                    pe_lg = em.at("PE")
                    em.emit("vector", lambda e: nc.vector.tensor_copy(lg_sb[:], ps_lg[:]),
                            waits=[("PE", pe_lg)], inc="V")
                    v_lg = em.at("V")
                with nc.psum_tensor("ps_lgt", [128, 256], F32) as ps_lgt:
                    def lgtr(e):
                        for tc in range(16):
                            ins = nc.tensor.transpose(
                                ps_lgt[:, tc * 16:(tc + 1) * 16],
                                lg_sb[:, tc * 128:(tc + 1) * 128], identf[:])
                        return ins
                    em.emit("tensor", lgtr, waits=[("V", v_lg)], inc="PE")
                    pe_lgt = em.at("PE")
                    em.emit("vector", lambda e: nc.vector.tensor_copy(lgt[:], ps_lgt[:]),
                            waits=[("PE", pe_lgt)], inc="V")

                    def m1r(e):
                        for tc in range(16):
                            ins = nc.vector.reduce_max(
                                m1s[:, tc:tc + 1], lgt[:, tc * 16:(tc + 1) * 16], AX.X)
                        return ins
                    em.emit("vector", m1r, inc="V")
                    em.emit("vector", lambda e: nc.vector.tensor_scalar(
                        nm1[:], m1s[:], -1.0, None, ALU.mult), inc="V")
                    em.emit("vector", lambda e: e.drain() or e.engine_nop(), inc="V")
                    v_m1 = em.at("V")

                    def sexp(e):
                        for tc in range(16):
                            ins = nc.scalar.activation(
                                est[:, tc * 16:(tc + 1) * 16], lgt[:, tc * 16:(tc + 1) * 16],
                                AF.Exp, bias=nm1[:, tc:tc + 1], accum_out=ssum[:, tc:tc + 1])
                        return ins
                    em.emit("scalar", sexp, waits=[("V", v_m1)], inc="SC")
                    sc_e = em.at("SC")
                    em.emit("vector", lambda e: nc.vector.reciprocal(rs[:], ssum[:]),
                            waits=[("SC", sc_e)], inc="V")

                    def eqms(e):
                        for tc in range(16):
                            ins = nc.vector.tensor_scalar(
                                eqm[:, tc * 16:(tc + 1) * 16], lgt[:, tc * 16:(tc + 1) * 16],
                                m1s[:, tc:tc + 1], None, ALU.is_equal)
                        return ins
                    em.emit("vector", eqms, inc="V")
                    em.emit("vector", lambda e: nc.vector.scalar_tensor_tensor(
                        eqm[:], eqm[:], NEGC, lgt[:], ALU.mult, ALU.add), inc="V")

                    def m2r(e):
                        for tc in range(16):
                            ins = nc.vector.reduce_max(
                                m2s[:, tc:tc + 1], eqm[:, tc * 16:(tc + 1) * 16], AX.X)
                        return ins
                    em.emit("vector", m2r, inc="V")
                    em.emit("vector", lambda e: e.drain() or e.engine_nop(), inc="V")

                    def keeps(e):
                        for tc in range(16):
                            ins = nc.vector.tensor_scalar(
                                keep[:, tc * 16:(tc + 1) * 16], lgt[:, tc * 16:(tc + 1) * 16],
                                m2s[:, tc:tc + 1], None, ALU.is_ge)
                        return ins
                    em.emit("vector", keeps, inc="V")

                    def combs(e):
                        for tc in range(16):
                            ins = nc.vector.scalar_tensor_tensor(
                                comb[:, tc * 16:(tc + 1) * 16], est[:, tc * 16:(tc + 1) * 16],
                                rs[:, tc:tc + 1], keep[:, tc * 16:(tc + 1) * 16],
                                ALU.mult, ALU.mult)
                        return ins
                    em.emit("vector", combs, inc="V")
                    em.emit("vector", lambda e: nc.vector.tensor_copy(comb_b[:], comb[:]), inc="V")
                    v_comb2 = em.at("V")
                with nc.psum_tensor("ps_cbt", [16, TL], BF16) as ps_cbt:
                    def cbtr(e):
                        for tc in range(16):
                            ins = nc.tensor.transpose(
                                ps_cbt[:, tc * 128:(tc + 1) * 128],
                                comb_b[:, tc * 16:(tc + 1) * 16], ident[:])
                        return ins
                    em.emit("tensor", cbtr, waits=[("V", v_comb2)], inc="PE")
                    pe_cbt = em.at("PE")
                    em.emit("vector", lambda e: nc.vector.tensor_copy(combT[:], ps_cbt[:]),
                            waits=[("PE", pe_cbt)], inc="V")
                    v_cbt = em.at("V")

            em.scope = "ffn_sh"
            with contextlib.ExitStack() as shs:
                wsg_b = sbt(shs, "wsg_b", [128, 8 * D], BF16)
                wsu_b = sbt(shs, "wsu_b", [128, 8 * D], BF16)
                wsd_b = sbt(shs, "wsd_b", [128, 8 * D], BF16)
                act_sh = sbt(shs, "act_sh", [128, 4 * 1024], BF16)
                sg_t = sbt(shs, "sg_t", [128, 1024], BF16)
                first = True
                for dst, src in ((wsg_b, wsg_d), (wsu_b, wsu_d), (wsd_b, wsd_d)):
                    em.emit("gpsimd", lambda e, dst=dst, src=src: e.dma_start(
                        out=dst[:].rearrange("p (r c) -> p r c", r=8),
                        in_=src.rearrange("(r p) c -> p r c", p=128)),
                        waits=([("PE", em.at("PE")), ("V", em.at("V"))] if first else None),
                        inc="dG", k=16)
                    first = False
                d_ws = em.at("dG")

                with contextlib.ExitStack() as pss:
                    ps_g = pss.enter_context(nc.psum_tensor("ps_sg", [128, 1024], F32))
                    ps_u = pss.enter_context(nc.psum_tensor("ps_su", [128, 1024], F32))
                    ps_d2 = pss.enter_context(nc.psum_tensor("ps_sd2", [128, 1024], F32))
                    for th in range(2):
                        for ih in range(2):
                            for icl in range(4):
                                ic = ih * 4 + icl

                                def gumm(e, th=th, ic=ic):
                                    for kc in range(8):
                                        for nh in range(2):
                                            nc.tensor.matmul(
                                                ps_g[:, nh * 512:(nh + 1) * 512],
                                                wsg_b[:, kc * D + ic * 128: kc * D + (ic + 1) * 128],
                                                xn[:, kc * TL + th * 1024 + nh * 512:
                                                   kc * TL + th * 1024 + (nh + 1) * 512],
                                                start=(kc == 0), stop=(kc == 7),
                                                skip_group_check=True)
                                    for kc in range(8):
                                        for nh in range(2):
                                            ins = nc.tensor.matmul(
                                                ps_u[:, nh * 512:(nh + 1) * 512],
                                                wsu_b[:, kc * D + ic * 128: kc * D + (ic + 1) * 128],
                                                xn[:, kc * TL + th * 1024 + nh * 512:
                                                   kc * TL + th * 1024 + (nh + 1) * 512],
                                                start=(kc == 0), stop=(kc == 7),
                                                skip_group_check=True)
                                    return ins
                                wl = [("V", em.at("V")), ("SC", em.at("SC"))]
                                if th == 0 and ic == 0:
                                    wl.append(("dG", d_ws))
                                em.emit("tensor", gumm, waits=wl, inc="PE")
                                pe_gu = em.at("PE")
                                em.emit("scalar", lambda e: nc.scalar.activation(
                                    sg_t[:], ps_g[:], AF.Silu),
                                    waits=[("PE", pe_gu)], inc="SC")
                                sc_sg = em.at("SC")
                                em.emit("vector", lambda e, icl=icl: nc.vector.tensor_tensor(
                                    act_sh[:, icl * 1024:(icl + 1) * 1024], sg_t[:], ps_u[:],
                                    ALU.mult),
                                    waits=[("SC", sc_sg)], inc="V")
                            v_actd = em.at("V")
                            v_dev = v_actd
                            for oc in range(8):
                                def sdmm(e, oc=oc, ih=ih):
                                    for icl in range(4):
                                        ic = ih * 4 + icl
                                        for nh in range(2):
                                            ins = nc.tensor.matmul(
                                                ps_d2[:, nh * 512:(nh + 1) * 512],
                                                wsd_b[:, ic * D + oc * 128: ic * D + (oc + 1) * 128],
                                                act_sh[:, icl * 1024 + nh * 512:
                                                       icl * 1024 + (nh + 1) * 512],
                                                start=(icl == 0), stop=(icl == 3),
                                                skip_group_check=True)
                                    return ins
                                em.emit("tensor", sdmm, waits=[("V", v_dev)], inc="PE")
                                pe_sd = em.at("PE")
                                if ih == 0:
                                    em.emit("vector", lambda e, oc=oc, th=th: nc.vector.tensor_copy(
                                        acc[:, oc * TL + th * 1024: oc * TL + (th + 1) * 1024],
                                        ps_d2[:]),
                                        waits=[("PE", pe_sd)], inc="V")
                                else:
                                    em.emit("vector", lambda e, oc=oc, th=th: nc.vector.tensor_tensor(
                                        acc[:, oc * TL + th * 1024: oc * TL + (th + 1) * 1024],
                                        acc[:, oc * TL + th * 1024: oc * TL + (th + 1) * 1024],
                                        ps_d2[:], ALU.add),
                                        waits=[("PE", pe_sd)], inc="V")
                                v_dev = em.at("V")

            em.scope = "ffn_expw"
            with contextlib.ExitStack() as exs:
                wgb = [sbt(exs, f"wgb{i}", [128, 8 * EI], BF16) for i in range(2)]
                wub = [sbt(exs, f"wub{i}", [128, 8 * EI], BF16) for i in range(2)]
                wdb = [sbt(exs, f"wdb{i}", [128, 4 * D], BF16) for i in range(2)]
                act_e = sbt(exs, "act_e", [128, 4 * 1024], BF16)
                sg_e = sbt(exs, "sg_e", [128, 1024], BF16)
                uc_e = sbt(exs, "uc_e", [128, 1024], BF16)
                crep = sbt(exs, "crep", [128, 2 * TL], BF16)

                d_exp = [0] * NE
                d_crep = [0] * NE
                pe_wuse = [0] * NE
                v_uexp = [0] * NE

                def emit_exp_load(ei):
                    sl = ei % 2
                    wl = ([("PE", pe_wuse[ei - 2])] if ei >= 2
                          else [("PE", em.at("PE")), ("V", em.at("V"))])
                    em.emit("gpsimd", lambda e, ei=ei, sl=sl: e.dma_start(
                        out=wgb[sl][:].rearrange("p (r c) -> p r c", r=8),
                        in_=wg_d[ei].rearrange("(r p) c -> p r c", p=128)),
                        waits=wl, inc="dG", k=16)
                    em.emit("gpsimd", lambda e, ei=ei, sl=sl: e.dma_start(
                        out=wub[sl][:].rearrange("p (r c) -> p r c", r=8),
                        in_=wu_d[ei].rearrange("(r p) c -> p r c", p=128)), inc="dG", k=16)
                    em.emit("gpsimd", lambda e, ei=ei, sl=sl: e.dma_start(
                        out=wdb[sl][:].rearrange("p (r c) -> p r c", r=4),
                        in_=wd_d[ei].rearrange("(r p) c -> p r c", p=128)), inc="dG", k=16)
                    d_exp[ei] = em.at("dG")

                def emit_crep_build(ei):
                    sl = ei % 2
                    base = sl * TL
                    wl = [("V", v_cbt)]
                    if ei >= 2:
                        wl.append(("V", v_uexp[ei - 2]))
                    em.emit("sync", lambda e, ei=ei, base=base: e.dma_start(
                        out=crep[0:1, base:base + TL], in_=combT[ei:ei + 1, :]),
                        waits=wl, inc="dS", k=16)
                    for step in (1, 2, 4, 8, 16, 32, 64):
                        dprev = em.at("dS")
                        em.emit("sync", lambda e, step=step, base=base: e.dma_start(
                            out=crep[step:2 * step, base:base + TL],
                            in_=crep[0:step, base:base + TL]),
                            waits=[("dS", dprev)], inc="dS", k=16)
                    d_crep[ei] = em.at("dS")

                emit_exp_load(0)
                if NE_RUN > 1:
                    emit_exp_load(1)

                with contextlib.ExitStack() as pse:
                    ps_g = pse.enter_context(nc.psum_tensor("ps_eg", [128, 1024], F32))
                    ps_u = pse.enter_context(nc.psum_tensor("ps_eu", [128, 1024], F32))
                    ps_d2 = pse.enter_context(nc.psum_tensor("ps_ed", [128, 1024], F32))

                    em.scope = "ffn_exp"
                    half_user = {0: (0, 0), 1: (0, 0)}
                    sg_user = {0: 0, 1: 0}
                    act_user = {0: 0, 1: 0}
                    gp_act = {}
                    v_dev = [em.at("V"), em.at("V")]
                    pending_ed = None

                    def emit_gu(ei, tp):
                        sl = ei % 2
                        for ic in range(4):
                            h = ic % 2

                            def gumm(e, sl=sl, tp=tp, ic=ic, h=h):
                                for kc in range(8):
                                    nc.tensor.matmul(
                                        ps_g[:, h * 512:(h + 1) * 512],
                                        wgb[sl][:, kc * EI + ic * 128: kc * EI + (ic + 1) * 128],
                                        xn[:, kc * TL + tp * 512: kc * TL + (tp + 1) * 512],
                                        start=(kc == 0), stop=(kc == 7),
                                        skip_group_check=True)
                                for kc in range(8):
                                    ins = nc.tensor.matmul(
                                        ps_u[:, h * 512:(h + 1) * 512],
                                        wub[sl][:, kc * EI + ic * 128: kc * EI + (ic + 1) * 128],
                                        xn[:, kc * TL + tp * 512: kc * TL + (tp + 1) * 512],
                                        start=(kc == 0), stop=(kc == 7),
                                        skip_group_check=True)
                                return ins
                            sc_u, v_u = half_user[h]
                            wl = [("SC", sc_u), ("V", v_u)]
                            if tp == 0 and ic == 0:
                                wl.append(("dG", d_exp[ei]))
                            em.emit("tensor", gumm, waits=wl, inc="PE")
                            pe_gu = em.at("PE")
                            em.emit("scalar", lambda e, h=h: nc.scalar.activation(
                                sg_e[:, h * 512:(h + 1) * 512],
                                ps_g[:, h * 512:(h + 1) * 512], AF.Silu),
                                waits=[("PE", pe_gu), ("GP", sg_user[h])], inc="SC")
                            sc_sg = em.at("SC")
                            wl2 = [("PE", pe_gu), ("GP", sg_user[h])]
                            if tp == 0 and ic == 0:
                                wl2.append(("dS", d_crep[ei]))

                            def ucop(e, h=h, tp=tp, sl=sl):
                                return nc.vector.scalar_tensor_tensor(
                                    uc_e[:, h * 512:(h + 1) * 512],
                                    ps_u[:, h * 512:(h + 1) * 512], 0.0,
                                    crep[:, sl * TL + tp * 512: sl * TL + (tp + 1) * 512],
                                    ALU.add, ALU.mult)
                            em.emit("vector", ucop, waits=wl2, inc="V")
                            v_uc = em.at("V")
                            half_user[h] = (sc_sg, v_uc)
                            asl = tp % 2
                            em.emit("gpsimd", lambda e, h=h, ic=ic, asl=asl: e.tensor_tensor(
                                act_e[:, asl * 2048 + ic * 512: asl * 2048 + (ic + 1) * 512],
                                sg_e[:, h * 512:(h + 1) * 512],
                                uc_e[:, h * 512:(h + 1) * 512], ALU.mult),
                                waits=[("SC", sc_sg), ("V", v_uc),
                                       ("PE", act_user[asl])], inc="GP")
                            sg_user[h] = em.at("GP")
                        gp_act[(ei, tp)] = em.at("GP")
                        if tp == 3:
                            v_uexp[ei] = em.at("V")

                    def emit_ed(ei, tp):
                        sl = ei % 2
                        asl = tp % 2
                        for oc in range(8):
                            h = oc % 2

                            def edmm(e, sl=sl, asl=asl, oc=oc, h=h):
                                for ic in range(4):
                                    ins = nc.tensor.matmul(
                                        ps_d2[:, h * 512:(h + 1) * 512],
                                        wdb[sl][:, ic * D + oc * 128: ic * D + (oc + 1) * 128],
                                        act_e[:, asl * 2048 + ic * 512: asl * 2048 + (ic + 1) * 512],
                                        start=(ic == 0), stop=(ic == 3),
                                        skip_group_check=True)
                                return ins
                            wl = [("V", v_dev[h])]
                            if oc == 0:
                                wl.append(("GP", gp_act[(ei, tp)]))
                            em.emit("tensor", edmm, waits=wl, inc="PE")
                            pe_ed = em.at("PE")

                            def dacc(e, oc=oc, tp=tp, h=h):
                                return nc.vector.tensor_tensor(
                                    acc[:, oc * TL + tp * 512: oc * TL + (tp + 1) * 512],
                                    acc[:, oc * TL + tp * 512: oc * TL + (tp + 1) * 512],
                                    ps_d2[:, h * 512:(h + 1) * 512], ALU.add)
                            em.emit("vector", dacc, waits=[("PE", pe_ed)], inc="V")
                            v_dev[h] = em.at("V")
                        act_user[asl] = em.at("PE")

                    emit_crep_build(0)
                    if NE_RUN > 1:
                        emit_crep_build(1)
                    for ei in range(NE_RUN):
                        emit_gu(ei, 0)
                        if pending_ed is not None:
                            emit_ed(*pending_ed)
                            pe_wuse[pending_ed[0]] = em.at("PE")
                            if pending_ed[0] + 2 < NE_RUN:
                                emit_exp_load(pending_ed[0] + 2)
                                emit_crep_build(pending_ed[0] + 2)
                        emit_gu(ei, 1)
                        emit_ed(ei, 0)
                        emit_gu(ei, 2)
                        emit_ed(ei, 1)
                        emit_gu(ei, 3)
                        emit_ed(ei, 2)
                        pending_ed = (ei, 3)
                    emit_ed(*pending_ed)
                v_ffn = em.at("V")

            em.scope = "p4"
            with contextlib.ExitStack() as ph4:
                xf2 = [sbt(ph4, f"xf2_{i}", [128, TL], F32) for i in range(2)]
                osb = [sbt(ph4, f"osb{i}", [128, TL], F32) for i in range(2)]
                v_of = [v_ffn, v_ffn]
                d_of = [0, 0]
                st_sem = ["dO", "dO2"]
                for oc in range(8):
                    half = oc % 2
                    q, qs = (("sync", "dS") if half == 0 else ("gpsimd", "dG"))
                    em.emit(q, lambda e, oc=oc, half=half: e.dma_start(
                        out=xf2[half][:], in_=xT[oc * 128:(oc + 1) * 128, :]),
                        waits=[("V", v_of[half])], inc=qs, k=16)
                    d_x2 = em.at(qs)
                    em.emit("vector", lambda e, oc=oc, half=half: nc.vector.tensor_tensor(
                        osb[half][:], xf2[half][:], att_out[:, oc * TL:(oc + 1) * TL],
                        ALU.add),
                        waits=[(qs, d_x2), (st_sem[half], d_of[half]), ("V", v_ffn)],
                        inc="V")
                    em.emit("vector", lambda e, oc=oc, half=half: nc.vector.tensor_tensor(
                        osb[half][:], osb[half][:], acc[:, oc * TL:(oc + 1) * TL],
                        ALU.add), inc="V")
                    v_of[half] = em.at("V")
                    em.emit(q, lambda e, oc=oc, half=half: e.dma_start(
                        out=outT[oc * 128:(oc + 1) * 128, :], in_=osb[half][:]),
                        waits=[("V", v_of[half])], inc=st_sem[half], k=16)
                    d_of[half] = em.at(st_sem[half])
                d_all = em.at("dO")
                d_all2 = em.at("dO2")
                em.emit("sync", lambda e: e.nop(),
                        waits=[("dO", d_all), ("dO2", d_all2)], inc=None)

        _replay(nc, em)

    return nc


def _finale(nc, em, src_bf, outT, v_wait):
    for oc in range(8):
        em.emit("gpsimd", lambda e, oc=oc: e.dma_start(
            out=outT[oc * 128:(oc + 1) * 128, :],
            in_=src_bf[:, oc * TL:(oc + 1) * TL]),
            waits=([("V", v_wait)] if oc == 0 else None), inc="dO", k=16)
    d_all = em.at("dO")
    em.emit("gpsimd", lambda e: e.nop(), waits=[("dO", d_all)], inc=None)


def _run_ops(nc, e, ops):
    for sc, f in ops:
        if sc:
            with nc.named_scope(sc):
                f(e)
        else:
            f(e)


def _replay(nc, em):
    with nc.Block() as block:
        @block.sync
        def _(e):
            _run_ops(nc, e, em.ops["sync"])

        @block.gpsimd
        def _(e):
            _run_ops(nc, e, em.ops["gpsimd"])

        @block.scalar
        def _(e):
            _run_ops(nc, e, em.ops["scalar"])

        @block.vector
        def _(e):
            _run_ops(nc, e, em.ops["vector"])

        @block.tensor
        def _(e):
            _run_ops(nc, e, em.ops["tensor"])


_NC_CACHE = None


def kernel(L0_attn_x, L0_attn_Wq, L0_attn_Wk, L0_attn_Wv, L0_attn_Wo,
           L0_attn_betas, L0_attn_memory0, L0_attn_z0,
           L0_ffn_gate_w, L0_ffn_w_gate, L0_ffn_w_up, L0_ffn_w_down,
           L0_ffn_ws_gate, L0_ffn_ws_up, L0_ffn_ws_down):
    global _NC_CACHE
    x = np.asarray(L0_attn_x, dtype=np.float32)
    assert x.shape == (2, 8192, 1024)

    cont = np.ascontiguousarray
    shared = {
        "wq": cont(np.asarray(L0_attn_Wq, np.float32)),
        "wk": cont(np.asarray(L0_attn_Wk, np.float32)),
        "wv": cont(np.asarray(L0_attn_Wv, np.float32)),
        "wo": cont(np.asarray(L0_attn_Wo, np.float32)),
        "gwT": cont(np.asarray(L0_ffn_gate_w, np.float32).T),
        "wg": cont(np.asarray(L0_ffn_w_gate, np.float32)),
        "wu": cont(np.asarray(L0_ffn_w_up, np.float32)),
        "wd": cont(np.asarray(L0_ffn_w_down, np.float32)),
        "wsg": cont(np.asarray(L0_ffn_ws_gate, np.float32)),
        "wsu": cont(np.asarray(L0_ffn_ws_up, np.float32)),
        "wsd": cont(np.asarray(L0_ffn_ws_down, np.float32)),
    }
    betas = np.asarray(L0_attn_betas, np.float32)
    mem0 = np.asarray(L0_attn_memory0, np.float32)
    z0 = np.asarray(L0_attn_z0, np.float32)
    maug_full = np.concatenate([mem0[0], np.swapaxes(z0[0], 1, 2)], axis=2)

    in_maps = []
    for c in range(8):
        b, p = c // 4, c % 4
        xs = x[b].reshape(16, 16, 32, 1024)[:, 4 * p:4 * p + 4]
        m = dict(shared)
        m["xT"] = cont(xs.reshape(2048, 1024).T)
        m["betas"] = cont(betas[0, 4 * p:4 * p + 4, 0, :].reshape(1, 256))
        m["maug"] = cont(maug_full[4 * p:4 * p + 4])
        in_maps.append(m)

    if _NC_CACHE is None:
        _NC_CACHE = build_program()
    res = run_bass_kernel_spmd(_NC_CACHE, in_maps, core_ids=list(range(8)))

    out = np.empty((2, 8192, 1024), np.float32)
    ov = out.reshape(2, 16, 16, 32, 1024)
    for c in range(8):
        b, p = c // 4, c % 4
        o = np.asarray(res.results[c]["outT"], np.float32)
        ov[b, :, 4 * p:4 * p + 4] = o.T.reshape(16, 4, 32, 1024)
    return out
